# version 24
# baseline (speedup 1.0000x reference)
import contextlib
import numpy as np

import concourse.bass as bass
import concourse.mybir as mybir
from concourse.bass_utils import run_bass_kernel_spmd

F32 = mybir.dt.float32
BF16 = mybir.dt.bfloat16
FP8 = mybir.dt.float8e4
A = mybir.AluOpType
AF = mybir.ActivationFunctionType
AX = mybir.AxisListType

P = 128
D = 1024
KC = 8
NT = 32
NB = 34
TIN = NB * P
C0, C1, HALF = 21, 41, 20
NH, HD = 4, 256
HID = 4096
KH = 32
Z = 235
RR = C1 + Z
ROWP = 4 * RR + 1
SKG = 1024
SKEWN = SKG + 128 * ROWP + SKG
SKEW_TOT = ((2 * SKEWN + 4096 + 2047) // 2048) * 2048
K_CONST = 1.5 * 2.0 ** 82
EPS_PROBE = 2.0 ** -30
MAGIC_LO = 1.5 * 2.0 ** 23

ENGINES = ("sync", "scalar", "vector", "tensor", "gpsimd")
DMA_SEMS = ("DX1", "DX2", "DX3", "DW", "DSKW", "DSKR0", "DSKR1", "DSKR2",
            "DAW", "DAR0", "DAR1", "DAR2", "DAR3",
            "DQW0", "DQW1", "DQR0", "DQR1", "DQR2", "DQR3", "DP", "DPR", "DO")


class Sched:

    def __init__(self, nc, es):
        self.nc = nc
        self.es = es
        self.thunks = {e: [] for e in ENGINES}
        self.semh = {}
        self.semc = {}
        self.lastwait = {e: {} for e in ENGINES}
        self.lasttok = {e: None for e in ENGINES}
        self.wmap = {}
        self.rmap = {}
        self.barrier = None

    def sem(self, name):
        self.semh[name] = self.es.enter_context(self.nc.semaphore(name))
        self.semc[name] = 0

    def op(self, eng, fn, reads=(), writes=(), extra=(), sem=None, amt=1):
        if amt == 16 and sem is not None:
            writes = list(writes) + ["_stream_" + sem]
        waits = []
        for k in reads:
            waits.append(self.wmap.get(k))
        for k in writes:
            waits.append(self.wmap.get(k))
            waits.extend(self.rmap.get(k, ()))
        waits.extend(extra)
        if self.barrier is not None:
            waits.append(self.barrier)
        wl = []
        seen = set()
        for t in waits:
            if t is None or t in seen:
                continue
            seen.add(t)
            s, v = t
            assert v <= self.semc[s], (s, v, self.semc[s])
            wl.append((s, v))
        if sem is None:
            sem = {"sync": "SY", "scalar": "S", "vector": "V",
                   "tensor": "T", "gpsimd": "G"}[eng]
        self.semc[sem] += amt
        tok = (sem, self.semc[sem])
        lw = self.lastwait[eng]
        wl2 = [(s, v) for (s, v) in wl if lw.get(s, -1) < v]
        for s, v in wl:
            if lw.get(s, -1) < v:
                lw[s] = v
        semh = self.semh

        def thunk(e, wl2=wl2, fn=fn, sem=sem, amt=amt):
            for s, v in wl2:
                e.wait_ge(semh[s], v)
            if fn is not None:
                ins = fn(e)
                ins.then_inc(semh[sem], amt)
            else:
                e.nop().then_inc(semh[sem], amt)

        self.thunks[eng].append(thunk)
        for k in reads:
            self.rmap.setdefault(k, []).append(tok)
        for k in writes:
            self.wmap[k] = tok
            self.rmap[k] = []
        self.lasttok[eng] = tok
        return tok

    def wait_only(self, eng, extra):
        semh = self.semh
        wl = [t for t in extra if t is not None]

        def thunk(e, wl=wl):
            for s, v in wl:
                e.wait_ge(semh[s], v)

        self.thunks[eng].append(thunk)

    def full_barrier(self):
        dma_tots = [(s, self.semc[s]) for s in DMA_SEMS]
        for eng in ENGINES:
            ex = [self.lasttok[eng]]
            if eng == "sync":
                ex += dma_tots
            self.op(eng, None, extra=ex, sem="PB")
        self.barrier = ("PB", self.semc["PB"])


def build_program():
    nc = bass.Bass("TRN2", target_bir_lowering=False)
    es = contextlib.ExitStack()

    x_in = nc.dram_tensor("x_in", [TIN, D], F32, kind="ExternalInput")
    W_qv = nc.dram_tensor("W_qv", [D, 2 * D], F32, kind="ExternalInput")
    W2_0 = nc.dram_tensor("W2_0", [D, NH * C0], F32, kind="ExternalInput")
    W2_1 = nc.dram_tensor("W2_1", [D, NH * C1], F32, kind="ExternalInput")
    SCW = nc.dram_tensor("SCW", [1, 2], F32, kind="ExternalInput")
    W_out = nc.dram_tensor("W_out", [D, D], F32, kind="ExternalInput")
    w1 = nc.dram_tensor("w1", [HID, D], F32, kind="ExternalInput")
    b1 = nc.dram_tensor("b1", [1, HID], F32, kind="ExternalInput")
    w2 = nc.dram_tensor("w2", [D, HID], F32, kind="ExternalInput")
    b2 = nc.dram_tensor("b2", [1, D], F32, kind="ExternalInput")
    gam = nc.dram_tensor("gam", [1, D], F32, kind="ExternalInput")
    bet = nc.dram_tensor("bet", [1, D], F32, kind="ExternalInput")
    y = nc.dram_tensor("y", [NT * P, D], F32, kind="ExternalOutput")

    partial_d = nc.dram_tensor("partial_d", [NT * P * D], F32, kind="Internal")
    wq1t_d = nc.dram_tensor("wq1t_d", [KC * P * HID], FP8, kind="Internal")
    wq2t_d = nc.dram_tensor("wq2t_d", [KH * P * D], FP8, kind="Internal")
    skew_d = nc.dram_tensor("skew_d", [SKEW_TOT], BF16, kind="Internal")
    amix_d = nc.dram_tensor("amix_d", [2 * 4 * P * D], BF16, kind="Internal")
    xq1_d = nc.dram_tensor("xq1_d", [4 * 4 * P * D], BF16, kind="Internal")

    SB_BASE = 16512 + 2048
    SB_TOP = 16512 + 212863
    cursor = [SB_BASE]

    def salloc(name, shape, dtype, at=None):
        nb = mybir.dt.size(dtype)
        for s_ in shape[1:]:
            nb *= s_
        if at is None:
            at = cursor[0]
        else:
            at = at + SB_BASE
        t = nc.alloc_sbuf_tensor_at(name, shape, dtype, offset=at)
        assert at + nb <= SB_TOP, (name, at + nb, SB_TOP)
        cursor[0] = max(cursor[0], (at + nb + 127) // 128 * 128)
        return t

    ident = salloc("ident", [P, P], BF16)
    iot = salloc("iot", [P, P], F32)
    magic = salloc("magic", [P, 4], F32)
    cst = salloc("cst", [P, 4], F32)
    swb = salloc("swb", [P, 8], F32)
    onesrow = salloc("onesrow", [1, P], F32)
    onescol = salloc("onescol", [P, 1], F32)
    swrow = salloc("swrow", [1, 8], F32)
    b1row = salloc("b1row", [1, HID], F32)
    b2row = salloc("b2row", [1, D], F32)
    gamrow = salloc("gamrow", [1, D], F32)
    betrow = salloc("betrow", [1, D], F32)
    st_st = salloc("st_st", [P, 2 * NB], F32)
    st_r1 = salloc("st_r1", [P, NB], F32)
    st2_st = salloc("st2_st", [P, 2 * NT], F32)
    st_r2 = salloc("st_r2", [P, NT], F32)
    inv1t = salloc("inv1t", [P, NT], F32)
    bs = salloc("bs", [P, 24], F32)
    sc0 = salloc("sc0", [1, 8], F32)
    sc1 = salloc("sc1", [P, 8], F32)
    sc2 = salloc("sc2", [P, 8], F32)
    negmax = salloc("negmax", [P, 16], F32)
    ssum = salloc("ssum", [P, 16], F32)
    lnsm = salloc("lnsm", [P, 16], F32)
    bias2 = salloc("bias2", [P, 8], F32)
    e0b = salloc("e0b", [P, NH * C0], BF16)
    lnswb = salloc("lnswb", [P, 2], F32)
    RES_END = cursor[0]

    wqv_b = salloc("wqv_b", [P, KC * 2048], BF16)
    w2c_b = salloc("w2c_b", [P, KC * 248], BF16)
    wout_b = salloc("wout_b", [P, KC * D], BF16)
    AACT_BASE = cursor[0]
    xld = salloc("xld", [P, 3 * D], F32)
    xn32 = salloc("xn32", [P, 3 * D], F32)
    xnfm = salloc("xnfm", [P, KC * 8 * P], BF16)
    vtm = salloc("vtm", [P, 9 * D], BF16)
    rqfm = salloc("rqfm", [P, 2 * KC * 512], BF16)
    probs = salloc("probs", [P, 2 * 4 * RR], BF16)
    expt = salloc("expt", [P, 2 * 248], F32)
    pmrg = salloc("pmrg", [P, NH * C1], F32)
    stt = salloc("stt", [P, 4 * 3 * 512], BF16)
    amix_tm = salloc("amix_tm", [P, 2 * D], BF16)
    amix_fm = salloc("amix_fm", [P, KC * 512], BF16)
    x2 = salloc("x2", [P, 2 * D], F32)
    pout = salloc("pout", [P, 2 * D], F32)
    A_END = cursor[0]
    assert A_END <= SB_TOP, A_END

    AB = AACT_BASE - SB_BASE
    wq1asm = salloc("wq1asm", [P, KC * HID], FP8, at=AB)
    wq2asm = salloc("wq2asm", [P, KH * D], FP8, at=AB + 32 * 1024)
    wstage = salloc("wstage", [P, 2 * HID], F32, at=AB + 64 * 1024)
    wqt = salloc("wqt", [P, HID + D], F32, at=AB + 96 * 1024)
    zbf = salloc("zbf", [P, 2048], BF16, at=AB + 116 * 1024)

    ccur = [RES_END - SB_BASE]

    def calloc(name, shape, dtype):
        t = salloc(name, shape, dtype, at=ccur[0])
        nb = mybir.dt.size(dtype)
        for s_ in shape[1:]:
            nb *= s_
        ccur[0] = (ccur[0] + nb + 127) // 128 * 128
        return t

    wq1sb = calloc("wq1sb", [P, KC * HID], FP8)
    wq2sb = calloc("wq2sb", [P, KH * D], FP8)
    b1b = calloc("b1b", [P, HID], F32)
    b2b = calloc("b2b", [P, D], F32)
    gamb = calloc("gamb", [P, D], F32)
    betb = calloc("betb", [P, D], F32)
    xld3 = calloc("xld3", [P, 2 * D], F32)
    w3 = calloc("w3", [P, 2 * D], F32)
    xq1tm = calloc("xq1tm", [P, 2 * D], BF16)
    xq1fm = calloc("xq1fm", [P, 2 * KC * 512], BF16)
    y1 = calloc("y1", [P, HID], F32)
    xq2tm = calloc("xq2tm", [P, KH * P], BF16)
    xq2fm = calloc("xq2fm", [P, KH * P], BF16)
    ppart = calloc("ppart", [P, 2 * D], F32)
    t2s = calloc("t2s", [P, D], F32)
    C_END = ccur[0] + SB_BASE
    assert C_END <= SB_TOP, C_END

    psA = es.enter_context(nc.psum_tensor("psA", [P, 512], F32))
    psB = es.enter_context(nc.psum_tensor("psB", [P, 512], F32))
    ql = es.enter_context(nc.psum_tensor("ql", [P, 512], F32))
    bandA = es.enter_context(nc.psum_tensor("bandA", [P, 512], F32))
    bandB = es.enter_context(nc.psum_tensor("bandB", [P, 512], F32))
    woutA = es.enter_context(nc.psum_tensor("woutA", [P, 512], F32))
    woutB = es.enter_context(nc.psum_tensor("woutB", [P, 512], F32))
    ps_tx = es.enter_context(nc.psum_tensor("ps_tx", [P, 512], BF16))

    AB_PS = [(psA, "psA"), (psB, "psB")]
    WT_PS = [(woutA, "woutA"), (woutB, "woutB")]
    BD_PS = [(bandA, "bandA"), (bandB, "bandB")]
    F1_PS = AB_PS + [(ql, "ql")] + WT_PS

    sch = Sched(nc, es)
    for s_ in ("V", "S", "T", "G", "SY", "PB") + DMA_SEMS:
        sch.sem(s_)
    V, S, T, SY, G = "vector", "scalar", "tensor", "sync", "gpsimd"
    MAG = magic[:, 0:1]
    MAGN = magic[:, 3:4]

    sch.op(G, lambda e: e.iota(iot[:, :], [[1, P]], channel_multiplier=-1,
                               allow_small_or_imprecise_dtypes=True), writes=["iot"])
    sch.op(V, lambda e: e.tensor_scalar(ident[:, :], iot[:, :], 0.0, None, op0=A.is_equal),
           reads=["iot"], writes=["ident"])
    sch.op(V, lambda e: e.tensor_scalar(iot[:, :], iot[:, :], 0.0, None, op0=A.is_equal),
           reads=["iot"], writes=["iot"])
    sch.op(V, lambda e: e.memset(cst[:, 0:1], 1e-5), writes=["cst"])
    sch.op(V, lambda e: e.memset(magic[:, 1:2], 1.0), writes=["mg1"])
    sch.op(V, lambda e: e.tensor_scalar(magic[:, 2:3], magic[:, 1:2], EPS_PROBE, None, op0=A.add),
           reads=["mg1"], writes=["mg2"])
    sch.op(V, lambda e: e.tensor_scalar(magic[:, 1:2], magic[:, 2:3], 1.0, None, op0=A.subtract),
           reads=["mg2"], writes=["mg1"])
    sch.op(V, lambda e: e.tensor_scalar(magic[:, 0:1], magic[:, 1:2], K_CONST, MAGIC_LO,
                                        op0=A.mult, op1=A.add),
           reads=["mg1"], writes=["magic"])
    sch.op(V, lambda e: e.tensor_scalar(magic[:, 3:4], magic[:, 0:1], -1.0, None, op0=A.mult),
           reads=["magic"], writes=["magicn"])
    sch.op(V, lambda e: e.memset(zbf[:, :], 0.0), writes=["zbf"])
    sch.op(V, lambda e: e.memset(onesrow[:, :], 1.0), writes=["onesrow"])
    sch.op(V, lambda e: e.memset(onescol[:, :], 1.0), writes=["onescol"])

    done = 0
    while done < SKEW_TOT:
        n = min(P * 2048, SKEW_TOT - done)
        rows = (n + 2047) // 2048
        sch.op(SY, lambda e, off=done, rows=rows: e.dma_start(
            bass.AP(skew_d, off, [[2048, rows], [1, 2048]]), zbf[0:rows, :]),
            reads=["zbf"], writes=["d_skew0", "d_skew1"], sem="DSKW", amt=16)
        done += rows * 2048

    sch.op(SY, lambda e: e.dma_start(swrow[0:1, 0:2], SCW[:, :]), writes=["swrow"], sem="DP", amt=16)
    sch.op(V, lambda e: e.tensor_reduce(sc0[0:1, 0:1], swrow[0:1, 0:2], AX.X, A.max, negate=True),
           reads=["swrow"], writes=["sc0a"])
    sch.op(S, lambda e: e.activation(sc0[0:1, 2:4], swrow[0:1, 0:2], AF.Exp,
                                     bias=sc0[0:1, 0:1], scale=1.0),
           reads=["swrow", "sc0a"], writes=["sc0b"])
    sch.op(V, lambda e: e.tensor_reduce(sc0[0:1, 1:2], sc0[0:1, 2:4], AX.X, A.add),
           reads=["sc0b"], writes=["sc0c"])
    sch.op(V, lambda e: e.reciprocal(sc0[0:1, 4:5], sc0[0:1, 1:2]),
           reads=["sc0c"], writes=["sc0d"])
    sch.op(V, lambda e: e.tensor_scalar(sc0[0:1, 6:8], sc0[0:1, 2:4], sc0[0:1, 4:5], None, op0=A.mult),
           reads=["sc0b", "sc0d"], writes=["sc0e"])
    sch.op(T, lambda e: e.matmul(ql[:, 0:2], onesrow[0:1, :], sc0[0:1, 6:8],
                                 start=True, stop=True),
           reads=["sc0e", "onesrow"], writes=["ql"])
    sch.op(S, lambda e: e.copy(swb[:, 0:2], ql[:, 0:2]), reads=["ql"], writes=["swb01"])
    sch.op(S, lambda e: e.activation(lnswb[:, 0:2], swb[:, 0:2], AF.Ln, bias=0.0, scale=1.0),
           reads=["swb01"], writes=["lnswb"])

    p0s = ["DW", "DAW", "DQW0", "DAR0", "DAR1", "DAR2", "DAR3", "DQR1"]
    NP0 = len(p0s)

    sch.op(V, lambda e: e.memset(magic[:, 1:2], 0.0), reads=["mg2"], writes=["mg1"])
    sch.op(V, lambda e: e.memset(cst[:, 1:2], 0.0), writes=["c1"])

    def flat_tile(wsrc, t, fi, acc_sl, red_sl, ka, kr):
        so = (fi % 2) * HID
        ws = "wst%d" % (fi % 2)
        sch.op(SY, lambda e, t=t, so=so, wsrc=wsrc: e.dma_start(
            wstage[:, so:so + HID],
            bass.AP(wsrc, t * P * HID, [[HID, P], [1, HID]])),
            writes=[ws], sem=p0s[fi % NP0], amt=16)
        sch.op(V, lambda e, so=so, red_sl=red_sl: e.tensor_reduce(
            red_sl, wstage[:, so:so + HID], AX.X, A.add,
            apply_absolute_value=True),
            reads=[ws], writes=[kr])
        sch.op(V, lambda e, acc_sl=acc_sl, red_sl=red_sl: e.tensor_add(
            acc_sl, acc_sl, red_sl),
            reads=[kr, ka], writes=[ka])

    fi = 0
    for t in range(KC):
        flat_tile(w1, t, fi, magic[:, 1:2], magic[:, 2:3], "mg1", "mg2"); fi += 1
        flat_tile(w2, t, fi, cst[:, 1:2], cst[:, 2:3], "c1", "c2"); fi += 1

    def scale_chain(acc_sl, ka, nel, swcol, swkey):
        sch.op(T, lambda e, acc_sl=acc_sl: e.matmul(
            ql[0:1, 4:5], acc_sl, onescol[:, 0:1], start=True, stop=True),
            reads=[ka, "onescol"], writes=["ql"])
        sch.op(S, lambda e: e.copy(sc0[0:1, 5:6], ql[0:1, 4:5]), reads=["ql"], writes=["sc0f"])
        sch.op(V, lambda e, nel=nel: e.tensor_scalar(sc0[0:1, 5:6], sc0[0:1, 5:6],
                                                     1.0 / nel, 1e-5,
                                                     op0=A.mult, op1=A.max),
               reads=["sc0f"], writes=["sc0f"])
        sch.op(V, lambda e: e.reciprocal(sc0[0:1, 5:6], sc0[0:1, 5:6]),
               reads=["sc0f"], writes=["sc0f"])
        sch.op(T, lambda e: e.matmul(ql[:, 5:6], onesrow[0:1, :], sc0[0:1, 5:6],
                                     start=True, stop=True),
               reads=["sc0f", "onesrow"], writes=["ql"])
        sch.op(S, lambda e, swcol=swcol: e.copy(swb[:, swcol:swcol + 1], ql[:, 5:6]),
               reads=["ql"], writes=[swkey])

    scale_chain(magic[:, 1:2], "mg1", float(KH * P * D), 2, "sw1q")
    scale_chain(cst[:, 1:2], "c1", float(KC * P * HID), 3, "sw2q")
    sch.op(V, lambda e: e.tensor_scalar(magic[:, 1:2], magic[:, 0:1], 1.0, None,
                                        op0=A.add),
           reads=["magic"], writes=["mg1", "mp1"])
    sch.op(V, lambda e: e.tensor_scalar(magic[:, 2:3], magic[:, 0:1], 1.0, None,
                                        op0=A.subtract),
           reads=["magic"], writes=["mg2", "mm1"])

    def ld_cast(dram, cols, dst, dstoff, kidx, wkey, slot):
        so = slot * HID
        sch.op(SY, lambda e: e.dma_start(wstage[:, so:so + cols], dram[kidx * P:(kidx + 1) * P, :]),
               writes=["wst%d" % slot], sem=p0s[(kidx + slot) % NP0], amt=16)
        sch.op(V, lambda e: e.tensor_copy(dst[:, dstoff:dstoff + cols], wstage[:, so:so + cols]),
               reads=["wst%d" % slot], writes=[wkey])

    sl = 0
    for k in range(KC):
        ld_cast(W_qv, 2048, wqv_b, k * 2048, k, "wqv", sl % 2); sl += 1
    for k in range(KC):
        ld_cast(W_out, D, wout_b, k * D, k, "wout", sl % 2); sl += 1
    for k in range(KC):
        ld_cast(W2_0, NH * C0, w2c_b, k * 248, k, "w2c", sl % 2); sl += 1
        ld_cast(W2_1, NH * C1, w2c_b, k * 248 + NH * C0, k, "w2c", sl % 2); sl += 1

    sch.op(SY, lambda e: e.dma_start(b1row[:, :], b1[:, :]), writes=["b1row"], sem="DP", amt=16)
    sch.op(SY, lambda e: e.dma_start(b2row[:, :], b2[:, :]), writes=["b2row"], sem="DP", amt=16)
    sch.op(SY, lambda e: e.dma_start(gamrow[:, :], gam[:, :]), writes=["gamrow"], sem="DP", amt=16)
    sch.op(SY, lambda e: e.dma_start(betrow[:, :], bet[:, :]), writes=["betrow"], sem="DP", amt=16)


    pend = []
    evct = [0]

    def flush_evacs(keep):
        while len(pend) > keep:
            fn, kpst, asmkey = pend.pop(0)
            if evct[0] % 2 == 0:
                sch.op(S, lambda e, fn=fn: fn(e, 0), reads=[kpst, "magicn"],
                       writes=[asmkey])
            else:
                sch.op(V, lambda e, fn=fn: fn(e, 1), reads=[kpst, "magic"],
                       writes=[asmkey])
            evct[0] += 1

    def quant_tile(wsrc, t, ti, cols, swcol, swkey, asmdst, asmkey, asm_k_step,
                   wqo, wqk, stg, stgname):
        so = (ti % 2) * HID
        ws = "%s%d" % (stgname, ti % 2)
        sch.op(SY, lambda e, t=t, so=so, cols=cols, wsrc=wsrc, stg=stg: e.dma_start(
            stg[:, so:so + cols], wsrc[t * P:(t + 1) * P, :]),
            writes=[ws], sem=p0s[ti % NP0], amt=16)
        for c in range(cols // 512):
            co = c * 512
            sch.op(S, lambda e, so=so, co=co, swcol=swcol, stg=stg: e.activation(
                stg[:, so + co:so + co + 512], stg[:, so + co:so + co + 512],
                AF.Identity, bias=MAG, scale=swb[:, swcol:swcol + 1]),
                reads=[ws, swkey, "magic"], writes=[ws])
            wqkc = "%s_%d" % (wqk, c % 4)
            sch.op(V, lambda e, so=so, co=co, wqo=wqo, stg=stg: e.tensor_scalar(
                wqt[:, wqo + co:wqo + co + 512], stg[:, so + co:so + co + 512],
                magic[:, 1:2], magic[:, 2:3], op0=A.min, op1=A.max),
                reads=[ws, "mp1", "mm1"], writes=[wqkc])
            pst, kpst = PH0_PS[(ti * 8 + c) % 6]
            for i in range(4):
                kb = c * 4 + i
                sch.op(T, lambda e, kb=kb, i=i, wqo=wqo, pst=pst: e.transpose(
                    pst[:, i * P:(i + 1) * P],
                    wqt[:, wqo + kb * P:wqo + (kb + 1) * P], iot[:, :]),
                    reads=[wqkc, "iot"], writes=[kpst])

            def evac(e, eng, t=t, c=c, asmdst=asmdst, asm_k_step=asm_k_step,
                     pst=pst):
                dst = bass.AP(asmdst, c * 4 * asm_k_step + t * P,
                              [[asmdst.shape[1], P], [asm_k_step, 4], [1, P]])
                if eng == 0:
                    return e.activation(dst, pst[:, 0:512], AF.Identity,
                                        bias=MAGN, scale=1.0)
                return e.tensor_scalar(dst, pst[:, 0:512], MAG, None,
                                       op0=A.subtract)
            pend.append((evac, kpst, asmkey))
            flush_evacs(2)

    PH0_PS = AB_PS + BD_PS + WT_PS
    ti = 0
    for g in range(KC):
        for i in range(4):
            quant_tile(w1, 4 * g + i, ti, D, 2, "sw1q", wq1asm, "wq1asm", HID,
                       HID, "wqt1", wstage, "wst"); ti += 1
        quant_tile(w2, g, ti, HID, 3, "sw2q", wq2asm, "wq2asm", D,
                   0, "wqt2", wstage, "wst"); ti += 1

    flush_evacs(0)
    sch.op(SY, lambda e: e.dma_start(bass.AP(wq1t_d, 0, [[KC * HID, P], [1, KC * HID]]),
                                     wq1asm[:, :]),
           reads=["wq1asm"], writes=["d_wq1"], sem="DW", amt=16)
    sch.op(SY, lambda e: e.dma_start(bass.AP(wq2t_d, 0, [[KH * D, P], [1, KH * D]]),
                                     wq2asm[:, :]),
           reads=["wq2asm"], writes=["d_wq2"], sem="DW", amt=16)

    def vload(eidx):
        t = eidx + 1
        par = t % 3
        sch.op(G, lambda e, t=t, par=par: e.dma_start(
            xld[:, par * D:(par + 1) * D], x_in[t * P:(t + 1) * P, :]),
            reads=["d_wq1", "d_wq2"],
            writes=["xld%d" % par], sem="DX1", amt=16)

    def vpath_ln(eidx):
        t = eidx + 1
        par = t % 3
        kx = "xld%d" % par
        kn = "xn32%d" % (t % 3)
        kb2 = "bs%d" % (t % 2)
        bso = (t % 2) * 12
        sch.op(V, lambda e: e.bn_stats(bs[:, bso:bso + 6], xld[:, par * D:par * D + 512]),
               reads=[kx], writes=[kb2])
        sch.op(V, lambda e: e.bn_stats(bs[:, bso + 6:bso + 12], xld[:, par * D + 512:par * D + D]),
               reads=[kx], writes=[kb2])
        sch.op(V, lambda e: e.bn_aggr(st_st[:, 2 * t:2 * t + 2], bs[:, bso:bso + 12]),
               reads=[kb2], writes=["st%d" % t])
        sch.op(S, lambda e: e.activation(st_r1[:, t:t + 1], st_st[:, 2 * t + 1:2 * t + 2],
                                         AF.Ln, bias=cst[:, 0:1], scale=1.0),
               reads=["st%d" % t, "cst"], writes=["str%d" % t])
        sch.op(S, lambda e: e.activation(st_r1[:, t:t + 1], st_r1[:, t:t + 1],
                                         AF.Exp, bias=0.0, scale=-0.5),
               reads=["str%d" % t], writes=["str%d" % t])
        sch.op(V, lambda e: e.tensor_scalar(xn32[:, (t % 3) * D:(t % 3 + 1) * D],
                                            xld[:, par * D:(par + 1) * D],
                                            st_st[:, 2 * t:2 * t + 1], st_r1[:, t:t + 1],
                                            op0=A.subtract, op1=A.mult),
               reads=[kx, "st%d" % t, "str%d" % t], writes=[kn])
        if 0 <= eidx < NT:
            s = eidx
            sch.op(V, lambda e: e.bn_stats(bs[:, bso:bso + 6],
                                           xn32[:, (t % 3) * D:(t % 3) * D + 512]),
                   reads=[kn], writes=[kb2])
            sch.op(V, lambda e: e.bn_stats(bs[:, bso + 6:bso + 12],
                                           xn32[:, (t % 3) * D + 512:(t % 3) * D + D]),
                   reads=[kn], writes=[kb2])
            sch.op(V, lambda e: e.bn_aggr(st2_st[:, 2 * s:2 * s + 2], bs[:, bso:bso + 12]),
                   reads=[kb2], writes=["st2_%d" % s])
            sch.op(S, lambda e: e.activation(st_r2[:, s:s + 1], st2_st[:, 2 * s + 1:2 * s + 2],
                                             AF.Ln, bias=cst[:, 0:1], scale=1.0),
                   reads=["st2_%d" % s, "cst"], writes=["st2r%d" % s])
            sch.op(S, lambda e: e.activation(st_r2[:, s:s + 1], st_r2[:, s:s + 1],
                                             AF.Exp, bias=0.0, scale=-0.5),
                   reads=["st2r%d" % s], writes=["st2r%d" % s])

    def vpath_tv(eidx):
        t = eidx + 1
        slot = eidx % 8
        slot9 = eidx % 9
        kn = "xn32%d" % (t % 3)
        kf = "xnfm%d" % slot
        kv = "vtm%d" % slot9
        for r0 in (0, 4):
            pst, kpst = AB_PS[(eidx * 3 + r0 // 4) % 2]
            for i in range(4):
                kb = r0 + i
                sch.op(T, lambda e, kb=kb, i=i, pst=pst, t=t: e.transpose(
                    pst[:, i * P:(i + 1) * P],
                    xn32[:, (t % 3) * D + kb * P:(t % 3) * D + (kb + 1) * P], iot[:, :]),
                    reads=[kn, "iot"], writes=[kpst])

            def evacx(e, r0=r0, slot=slot, pst=pst):
                dst = bass.AP(xnfm, r0 * (8 * P) + slot * P,
                              [[KC * 8 * P, P], [8 * P, 4], [1, P]])
                return e.copy(dst, pst[:, 0:512])
            sch.op(S, evacx, reads=[kpst], writes=[kf])
        for n in range(2):
            pst, kpst = AB_PS[(eidx * 3 + 2 + n) % 2]

            def vmm(e, slot=slot, n=n, pst=pst):
                ins = None
                for k in range(KC):
                    ins = e.matmul(
                        pst[:, 0:512],
                        xnfm[:, k * (8 * P) + slot * P:k * (8 * P) + (slot + 1) * P],
                        wqv_b[:, k * 2048 + D + n * 512:k * 2048 + D + (n + 1) * 512],
                        start=(k == 0), stop=(k == KC - 1))
                return ins
            sch.op(T, vmm, reads=[kf, "wqv"], writes=[kpst])
            sch.op(S, lambda e, slot9=slot9, n=n, pst=pst: e.copy(
                vtm[:, slot9 * D + n * 512:slot9 * D + (n + 1) * 512], pst[:, 0:512]),
                reads=[kpst], writes=[kv])

    def qpath(quad):
        qpar = quad % 2
        slot0 = (4 * quad) % 8
        kfs = ["xnfm%d" % ((4 * quad + i) % 8) for i in range(4)]
        for m in range(KC):
            pst, kpst = WT_PS[m % 2]

            def qmm(e, m=m, slot0=slot0, pst=pst):
                ins = None
                for k in range(KC):
                    ins = e.matmul(
                        pst[:, 0:512],
                        wqv_b[:, k * 2048 + m * P:k * 2048 + (m + 1) * P],
                        xnfm[:, k * (8 * P) + slot0 * P:k * (8 * P) + slot0 * P + 512],
                        start=(k == 0), stop=(k == KC - 1))
                return ins
            sch.op(T, qmm, reads=kfs + ["wqv"], writes=[kpst])
            sch.op(V, lambda e, m=m, qpar=qpar, pst=pst: e.tensor_scalar(
                rqfm[:, qpar * 4096 + m * 512:qpar * 4096 + (m + 1) * 512],
                pst[:, 0:512], 0.0, None, op0=A.max),
                reads=[kpst], writes=["rqfm%d" % qpar])

    def attn1a(s):
        par = s % 2
        quad = s // 4
        qpar = quad % 2
        eo = par * 248

        def lmm(e, s=s, qpar=qpar):
            ins = None
            for k in range(KC):
                ins = e.matmul(
                    ql[:, 0:248],
                    rqfm[:, qpar * 4096 + k * 512 + (s % 4) * P:
                         qpar * 4096 + k * 512 + (s % 4 + 1) * P],
                    w2c_b[:, k * 248:(k + 1) * 248],
                    start=(k == 0), stop=(k == KC - 1))
            return ins
        sch.op(T, lmm, reads=["rqfm%d" % qpar, "w2c"], writes=["ql"])
        sch.op(V, lambda e, eo=eo: e.tensor_copy(expt[:, eo:eo + 248], ql[:, 0:248]),
               reads=["ql"], writes=["expt%d" % par])
        sch.op(V, lambda e, eo=eo, par=par: e.tensor_reduce(
            bass.AP(negmax, par * 8, [[16, P], [1, 4]]),
            expt[:, eo:eo + 84].rearrange("p (h c) -> p h c", c=C0),
            AX.X, A.max, negate=True),
            reads=["expt%d" % par], writes=["negmax0_%d" % par])
        sch.op(V, lambda e, eo=eo, par=par: e.tensor_reduce(
            bass.AP(negmax, par * 8 + 4, [[16, P], [1, 4]]),
            expt[:, eo + 84:eo + 248].rearrange("p (h c) -> p h c", c=C1),
            AX.X, A.max, negate=True),
            reads=["expt%d" % par], writes=["negmax1_%d" % par])

    def attn1b1(s):
        par = s % 2
        eo = par * 248
        no = par * 8
        groups = [(h, 0, h * C0, C0) for h in range(NH)] + \
                 [(h, 1, NH * C0 + h * C1, C1) for h in range(NH)]
        for gi, (h, ci, c0, cw) in enumerate(groups):
            sch.op(S, lambda e, gi=gi, c0=c0, cw=cw, eo=eo, no=no: e.activation(
                pmrg[:, 0:cw], expt[:, eo + c0:eo + c0 + cw], AF.Exp,
                bias=negmax[:, no + gi:no + gi + 1], scale=1.0,
                accum_out=ssum[:, no + gi:no + gi + 1]),
                reads=["expt%d" % par, "negmax0_%d" % par, "negmax1_%d" % par],
                writes=["pmgj", "ssum%d_%d" % (par, gi)])
        sch.op(S, lambda e, no=no: e.activation(lnsm[:, no:no + 8], ssum[:, no:no + 8],
                                                AF.Ln, bias=0.0, scale=1.0),
               reads=["ssum%d_%d" % (par, g_) for g_ in range(8)], writes=["lnsm%d" % par])

    def attn1b2(s):
        par = s % 2
        kp = "probs%d" % par
        kst = "stt%d" % (s % 4)
        eo = par * 248
        no = par * 8
        sch.op(V, lambda e, no=no: e.tensor_sub(bias2[:, 0:8], negmax[:, no:no + 8], lnsm[:, no:no + 8]),
               reads=["negmax0_%d" % par, "negmax1_%d" % par, "lnsm%d" % par], writes=["bias2"])
        sch.op(V, lambda e: e.tensor_scalar(bias2[:, 0:4], bias2[:, 0:4], lnswb[:, 0:1], None, op0=A.add),
               reads=["lnswb"], writes=["bias2"])
        sch.op(V, lambda e: e.tensor_scalar(bias2[:, 4:8], bias2[:, 4:8], lnswb[:, 1:2], None, op0=A.add),
               reads=["lnswb"], writes=["bias2"])
        for h in range(NH):
            sch.op(S, lambda e, h=h, par=par, eo=eo: e.activation(
                probs[:, par * (4 * RR) + h * RR:par * (4 * RR) + h * RR + C1],
                expt[:, eo + 84 + h * C1:eo + 84 + (h + 1) * C1], AF.Exp,
                bias=bias2[:, 4 + h:5 + h], scale=1.0),
                reads=["expt%d" % par, "bias2"], writes=[kp])
            sch.op(S, lambda e, h=h, eo=eo: e.activation(
                e0b[:, h * C0:(h + 1) * C0],
                expt[:, eo + h * C0:eo + (h + 1) * C0], AF.Exp,
                bias=bias2[:, h:h + 1], scale=1.0),
                reads=["expt%d" % par, "bias2"], writes=["e0b"])
        for h in range(NH):
            sch.op(V, lambda e, h=h, par=par: e.tensor_add(
                probs[:, par * (4 * RR) + h * RR + 10:par * (4 * RR) + h * RR + 31],
                probs[:, par * (4 * RR) + h * RR + 10:par * (4 * RR) + h * RR + 31],
                e0b[:, h * C0:(h + 1) * C0]),
                reads=["e0b"], writes=[kp])
        sch.op(SY, lambda e, par=par: e.dma_start(
            bass.AP(skew_d, par * SKEWN + SKG, [[ROWP, P], [1, 4 * RR]]),
            probs[:, par * (4 * RR):(par + 1) * (4 * RR)]),
            reads=[kp], writes=["d_skew%d" % par], sem="DSKW", amt=16)
        for g in range(3):
            sch.op(SY, lambda e, g=g, par=par, s=s: e.dma_start(
                stt[:, ((s % 4) * 3 + g) * 512:((s % 4) * 3 + g + 1) * 512],
                bass.AP(skew_d, par * SKEWN + SKG + g * P - 108, [[RR, 512], [1, P]]),
                transpose=True),
                reads=["d_skew%d" % par], writes=["%s_%d" % (kst, g)],
                sem="DSKR%d" % g, amt=16)

    def attn2(s):
        par = s % 2
        kst = ["stt%d_%d" % (s % 4, g) for g in range(3)]
        vkeys = ["vtm%d" % ((s - 1 + g) % 9) for g in range(3)]
        for half in range(2):
            pst, kpst = BD_PS[(s * 2 + half) % 2]

            def bmm(e, s=s, half=half, pst=pst):
                ins = None
                for h in (half * 2, half * 2 + 1):
                    for g in range(3):
                        vslot = (s - 1 + g) % 9
                        ins = e.matmul(
                            pst[:, (h - half * 2) * HD:(h - half * 2 + 1) * HD],
                            bass.AP(stt, ((s % 4) * 3 + g) * 512 + h, [[4 * 3 * 512, P], [4, P]]),
                            vtm[:, vslot * D + h * HD:vslot * D + (h + 1) * HD],
                            start=(g == 0), stop=(g == 2))
                return ins
            sch.op(T, bmm, reads=kst + vkeys, writes=[kpst])
            sch.op(V, lambda e, par=par, half=half, pst=pst: e.tensor_copy(
                amix_tm[:, par * D + half * 512:par * D + (half + 1) * 512], pst[:, 0:512]),
                reads=[kpst], writes=["amix%d" % par])
        sch.op(SY, lambda e, s=s, par=par: e.dma_start(
            bass.AP(amix_d, (s // 4 % 2) * 4 * P * D + (s % 4) * P * D, [[D, P], [1, D]]),
            amix_tm[:, par * D:(par + 1) * D]),
            reads=["amix%d" % par], writes=["d_amix%d" % (s // 4 % 2)], sem="DAW", amt=16)

    def wout_read(quad):
        qpar = quad % 2
        for k in range(KC):
            sch.op(SY, lambda e, k=k, qpar=qpar: e.dma_start(
                amix_fm[:, k * 512:(k + 1) * 512],
                bass.AP(amix_d, qpar * 4 * P * D + k * P, [[D, 512], [1, P]]),
                transpose=True),
                reads=["d_amix%d" % qpar], writes=["amixfm%d" % k],
                sem="DAR%d" % (k % 4), amt=16)

    def wout_mm(quad):
        amixkeys = ["amixfm%d" % k for k in range(KC)]
        for i in range(4):
            s = 4 * quad + i
            par = s % 2
            sch.op(G, lambda e, s=s, par=par: e.dma_start(
                x2[:, par * D:(par + 1) * D], x_in[(s + 1) * P:(s + 2) * P, :]),
                writes=["x2%d" % par], sem="DX2", amt=16)
            for n in range(2):
                pst, kpst = WT_PS[(i * 2 + n) % 2]

                def omm(e, i=i, n=n, pst=pst):
                    ins = None
                    for k in range(KC):
                        ins = e.matmul(
                            pst[:, 0:512],
                            amix_fm[:, k * 512 + i * P:k * 512 + (i + 1) * P],
                            wout_b[:, k * D + n * 512:k * D + (n + 1) * 512],
                            start=(k == 0), stop=(k == KC - 1))
                    return ins
                sch.op(T, omm, reads=amixkeys + ["wout"], writes=[kpst])
                sch.op(V, lambda e, par=par, n=n, pst=pst: e.scalar_tensor_tensor(
                    pout[:, par * D + n * 512:par * D + (n + 1) * 512], pst[:, 0:512], 1.0,
                    x2[:, par * D + n * 512:par * D + (n + 1) * 512], op0=A.mult, op1=A.add),
                    reads=[kpst, "x2%d" % par], writes=["pout%d" % par])
            sch.op(SY, lambda e, s=s, par=par: e.dma_start(
                bass.AP(partial_d, s * P * D, [[D, P], [1, D]]),
                pout[:, par * D:(par + 1) * D]),
                reads=["pout%d" % par], writes=["d_part%d" % s], sem="DP", amt=16)

    for e_ in (-1, 0, 1):
        vload(e_)
    for kind, e_ in (("ln", -1), ("vl", 2), ("ln", 0), ("vl", 3), ("ln", 1),
                     ("vl", 4), ("tv", -1), ("ln", 2), ("vl", 5), ("tv", 0),
                     ("ln", 3), ("vl", 6), ("tv", 1), ("ln", 4), ("vl", 7),
                     ("tv", 2), ("ln", 5), ("tv", 3)):
        {"ln": vpath_ln, "tv": vpath_tv, "vl": vload}[kind](e_)
    sch.full_barrier()
    sch.op(V, lambda e: e.memset(probs[:, :], 0.0), writes=["probs0", "probs1"])

    for s in range(NT + 7):
        if s + 8 <= NB - 2:
            vload(s + 8)
        if 0 <= s - 4 < NT:
            attn2(s - 4)
        if s + 4 <= NB - 2:
            vpath_tv(s + 4)
        if s + 6 <= NB - 2:
            vpath_ln(s + 6)
        if (s - 7) % 4 == 0 and 0 <= s - 7 < NT:
            wout_read((s - 7) // 4)
        if s % 4 == 0 and s < NT:
            qpath(s // 4)
        if 0 <= s - 2 < NT:
            attn1b2(s - 2)
        if s < NT:
            attn1a(s)
        if 0 <= s - 1 < NT:
            attn1b1(s - 1)
        if (s - 10) % 4 == 0 and 0 <= s - 10 < NT:
            wout_mm((s - 10) // 4)

    sch.full_barrier()

    for j in range(4):
        w = KC * HID // 4
        sch.op(SY, lambda e, j=j, w=w: e.dma_start(
            wq1sb[:, j * w:(j + 1) * w],
            bass.AP(wq1t_d, j * w, [[KC * HID, P], [1, w]])),
            reads=["d_wq1"], writes=["wq1sb_%d" % j], sem=p0s[j], amt=16)
        sch.op(SY, lambda e, j=j, w=w: e.dma_start(
            wq2sb[:, j * w:(j + 1) * w],
            bass.AP(wq2t_d, j * w, [[KH * D, P], [1, w]])),
            reads=["d_wq2"], writes=["wq2sb_%d" % j], sem=p0s[4 + j], amt=16)
    bi = 0
    for (row, rkey, dst, dkey, n) in ((gamrow, "gamrow", gamb, "gamb", D),
                                      (betrow, "betrow", betb, "betb", D),
                                      (b1row, "b1row", b1b, "b1b", HID),
                                      (b2row, "b2row", b2b, "b2b", D)):
        for c in range(0, n, 512):
            pst, kpst = WT_PS[bi % 2]
            bi += 1
            sch.op(T, lambda e, row=row, c=c, pst=pst: e.matmul(
                pst[:, 0:512], onesrow[0:1, :], row[0:1, c:c + 512],
                start=True, stop=True),
                reads=[rkey, "onesrow"], writes=[kpst])
            sch.op(S, lambda e, dst=dst, c=c, pst=pst: e.copy(dst[:, c:c + 512], pst[:, 0:512]),
                   reads=[kpst], writes=[dkey])

    def qload(s):
        par = s % 2
        tb = s + 1
        sch.op(G, lambda e, tb=tb, par=par: e.dma_start(
            xld3[:, par * D:(par + 1) * D], x_in[tb * P:(tb + 1) * P, :]),
            writes=["xld3%d" % par], sem="DX3", amt=16)

    def quantize1(s):
        par = s % 2
        tb = s + 1
        kx, kw, kq = "xld3%d" % par, "w3%d" % par, "xq1tm%d" % par
        sch.op(V, lambda e: e.tensor_scalar(
            w3[:, par * D:(par + 1) * D], xld3[:, par * D:(par + 1) * D],
            st_st[:, 2 * tb:2 * tb + 1], st_r1[:, tb:tb + 1], op0=A.subtract, op1=A.mult),
            reads=[kx, "st%d" % tb, "str%d" % tb], writes=[kw])
        sch.op(V, lambda e: e.tensor_scalar(
            w3[:, par * D:(par + 1) * D], w3[:, par * D:(par + 1) * D],
            st2_st[:, 2 * s:2 * s + 1], st_r2[:, s:s + 1], op0=A.subtract, op1=A.mult),
            reads=[kw, "st2_%d" % s, "st2r%d" % s], writes=[kw])
        sch.op(V, lambda e: e.tensor_mul(w3[:, par * D:(par + 1) * D],
                                         w3[:, par * D:(par + 1) * D], gamb[:, :]),
               reads=[kw, "gamb"], writes=[kw])
        sch.op(V, lambda e: e.tensor_add(w3[:, par * D:(par + 1) * D],
                                         w3[:, par * D:(par + 1) * D], betb[:, :]),
               reads=[kw, "betb"], writes=[kw])
        sch.op(V, lambda e: e.tensor_reduce(sc1[:, 0:1], w3[:, par * D:(par + 1) * D],
                                            AX.X, A.max, apply_absolute_value=True),
               reads=[kw], writes=["sc1a"])
        sch.op(V, lambda e: e.tensor_scalar(sc1[:, 1:2], sc1[:, 0:1], 1e-5, None, op0=A.max),
               reads=["sc1a"], writes=["sc1b"])
        sch.op(V, lambda e: e.reciprocal(sc1[:, 2:3], sc1[:, 1:2]),
               reads=["sc1b"], writes=["sc1c"])
        sch.op(V, lambda e: e.tensor_scalar(sc1[:, 3:4], sc1[:, 2:3], 127.0, None, op0=A.mult),
               reads=["sc1c"], writes=["sc1d"])
        sch.op(V, lambda e: e.tensor_scalar(sc1[:, 4:5], sc1[:, 3:4], swb[:, 2:3], None, op0=A.mult),
               reads=["sc1d", "sw1q"], writes=["sc1e"])
        sch.op(V, lambda e: e.reciprocal(inv1t[:, s:s + 1], sc1[:, 4:5]),
               reads=["sc1e"], writes=["inv1_%d" % s])
        sch.op(S, lambda e: e.activation(
            w3[:, par * D:(par + 1) * D], w3[:, par * D:(par + 1) * D], AF.Identity,
            bias=MAG, scale=sc1[:, 3:4]),
            reads=[kw, "sc1d", "magic"], writes=[kw])
        sch.op(S, lambda e: e.activation(
            xq1tm[:, par * D:(par + 1) * D], w3[:, par * D:(par + 1) * D], AF.Identity,
            bias=MAGN, scale=1.0),
            reads=[kw, "magicn"], writes=[kq])
        sch.op(SY, lambda e, s=s: e.dma_start(
            bass.AP(xq1_d, (s // 4 % 4) * 4 * P * D + (s % 4) * P * D, [[D, P], [1, D]]),
            xq1tm[:, par * D:(par + 1) * D]),
            reads=[kq], writes=["d_xq1_%d" % (s % 16)], sem="DQW%d" % (s % 2), amt=16)

    def xq1_read(quad):
        qpar = quad % 2
        for k in range(KC):
            sch.op(SY, lambda e, k=k, quad=quad, qpar=qpar: e.dma_start(
                xq1fm[:, qpar * 4096 + k * 512:qpar * 4096 + (k + 1) * 512],
                bass.AP(xq1_d, (quad % 4) * 4 * P * D + k * P, [[D, 512], [1, P]]),
                transpose=True),
                reads=["d_xq1_%d" % ((4 * quad + i_) % 16) for i_ in range(4)],
                writes=["xq1fm%d_%d" % (qpar, k)],
                sem="DQR%d" % (k % 4), amt=16)

    def ffn_f1(s):
        quad = s // 4
        qpar = quad % 2
        xq1keys = ["xq1fm%d_%d" % (qpar, k) for k in range(KC)]
        for c in range(8):
            pst, kpst = F1_PS[(s * 8 + c) % 5]

            def fmm(e, c=c, s=s, qpar=qpar, pst=pst):
                ins = None
                for k in range(KC):
                    ins = e.matmul(
                        pst[:, 0:512],
                        xq1fm[:, qpar * 4096 + k * 512 + (s % 4) * P:
                              qpar * 4096 + k * 512 + (s % 4 + 1) * P],
                        wq1sb[:, k * HID + c * 512:k * HID + (c + 1) * 512],
                        start=(k == 0), stop=(k == KC - 1))
                return ins
            sch.op(T, fmm, reads=xq1keys + ["wq1sb_%d" % j_ for j_ in range(4)], writes=[kpst])
            sch.op(V, lambda e, c=c, s=s, pst=pst, kpst=kpst: e.scalar_tensor_tensor(
                y1[:, c * 512:(c + 1) * 512], pst[:, 0:512], inv1t[:, s:s + 1],
                b1b[:, c * 512:(c + 1) * 512], op0=A.mult, op1=A.add),
                reads=[kpst, "inv1_%d" % s, "b1b"], writes=["y1h%d" % (c // 4)])

    def ffn_f1gelu(s):
        sch.op(S, lambda e: e.activation(y1[:, 0:2048], y1[:, 0:2048], AF.Gelu,
                                         bias=0.0, scale=1.0),
               reads=["y1h0"], writes=["y1h0"])
        sch.op(S, lambda e: e.activation(y1[:, 2048:HID], y1[:, 2048:HID], AF.Gelu,
                                         bias=0.0, scale=1.0),
               reads=["y1h1"], writes=["y1h1"])

    def ffn_f2head(s):
        par = s % 2
        sch.op(SY, lambda e, s=s, par=par: e.dma_start(
            ppart[:, par * D:(par + 1) * D],
            bass.AP(partial_d, s * P * D, [[D, P], [1, D]])),
            reads=["d_part%d" % s], writes=["ppart%d" % par], sem="DPR", amt=16)
        sch.op(V, lambda e: e.tensor_reduce(sc2[:, 6:7], y1[:, 0:2048], AX.X, A.max,
                                            apply_absolute_value=True),
               reads=["y1h0"], writes=["sc2m0"])
        sch.op(V, lambda e: e.tensor_reduce(sc2[:, 7:8], y1[:, 2048:HID], AX.X, A.max,
                                            apply_absolute_value=True),
               reads=["y1h1"], writes=["sc2m1"])
        sch.op(V, lambda e: e.tensor_scalar(sc2[:, 0:1], sc2[:, 6:7], sc2[:, 7:8],
                                            None, op0=A.max),
               reads=["sc2m0", "sc2m1"], writes=["sc2a"])
        sch.op(V, lambda e: e.tensor_scalar(sc2[:, 1:2], sc2[:, 0:1], 1e-5, None, op0=A.max),
               reads=["sc2a"], writes=["sc2b"])
        sch.op(V, lambda e: e.reciprocal(sc2[:, 2:3], sc2[:, 1:2]),
               reads=["sc2b"], writes=["sc2c"])
        sch.op(V, lambda e: e.tensor_scalar(sc2[:, 3:4], sc2[:, 2:3], 127.0, None, op0=A.mult),
               reads=["sc2c"], writes=["sc2d"])
        sch.op(V, lambda e: e.tensor_scalar(sc2[:, 4:5], sc2[:, 3:4], swb[:, 3:4], None, op0=A.mult),
               reads=["sc2d", "sw2q"], writes=["sc2e"])
        sch.op(V, lambda e: e.reciprocal(sc2[:, 5:6], sc2[:, 4:5]),
               reads=["sc2e"], writes=["sc2f"])
        for h, kh, kx in ((0, "y1h0", "xq2tm0"), (1, "y1h1", "xq2tm1")):
            lo, hi = h * 2048, (h + 1) * 2048
            sch.op(S, lambda e, lo=lo, hi=hi: e.activation(
                y1[:, lo:hi], y1[:, lo:hi], AF.Identity,
                bias=MAG, scale=sc2[:, 3:4]),
                reads=[kh, "sc2d", "magic"], writes=[kh])
            sch.op(S, lambda e, lo=lo, hi=hi: e.activation(
                xq2tm[:, lo:hi], y1[:, lo:hi], AF.Identity,
                bias=MAGN, scale=1.0),
                reads=[kh, "magicn"], writes=[kx])

    def ffn_f2tailA(s):
        for r0 in range(0, KH, 4):
            for i in range(4):
                kb = r0 + i
                sch.op(T, lambda e, kb=kb, i=i: e.transpose(
                    ps_tx[:, i * P:(i + 1) * P], xq2tm[:, kb * P:(kb + 1) * P],
                    ident[:, :]),
                    reads=["xq2tm%d" % (kb // 16), "ident"], writes=["pstx"])
            sch.op(S, lambda e, r0=r0: e.copy(
                xq2fm[:, r0 * P:(r0 + 4) * P], ps_tx[:, 0:512]),
                reads=["pstx"], writes=["xq2fm"])

    def ffn_f2tailB(s):
        par = s % 2
        for n in range(2):
            pst, kpst = BD_PS[n % 2]

            def f2mm(e, n=n, pst=pst):
                ins = None
                for k in range(KH):
                    ins = e.matmul(
                        pst[:, 0:512],
                        xq2fm[:, k * P:(k + 1) * P],
                        wq2sb[:, k * D + n * 512:k * D + (n + 1) * 512],
                        start=(k == 0), stop=(k == KH - 1))
                return ins
            sch.op(T, f2mm, reads=["xq2fm"] + ["wq2sb_%d" % j_ for j_ in range(4)],
                   writes=[kpst])
            sch.op(V, lambda e, n=n, pst=pst, kpst=kpst: e.scalar_tensor_tensor(
                t2s[:, n * 512:(n + 1) * 512], pst[:, 0:512], sc2[:, 5:6],
                b2b[:, n * 512:(n + 1) * 512], op0=A.mult, op1=A.add),
                reads=[kpst, "sc2f", "b2b"], writes=["t2s"])
        sch.op(V, lambda e, par=par: e.tensor_add(
            ppart[:, par * D:(par + 1) * D], ppart[:, par * D:(par + 1) * D],
            t2s[:, 0:D]),
            reads=["t2s", "ppart%d" % par], writes=["ppart%d" % par])
        sch.op(SY, lambda e, s=s, par=par: e.dma_start(
            y[s * P:(s + 1) * P, :], ppart[:, par * D:(par + 1) * D]),
            reads=["ppart%d" % par], writes=["ppart%d" % par], sem="DO", amt=16)

    qload(0)
    qload(1)
    for s in range(8):
        quantize1(s)
        qload(s + 2)
    xq1_read(0)
    ffn_f1(0)
    ffn_f1gelu(0)
    for s in range(NT):
        ffn_f2head(s)
        if s % 4 == 0 and (s + 4) // 4 < NT // 4:
            xq1_read((s + 4) // 4)
        if s + 1 < NT:
            ffn_f1(s + 1)
        ffn_f2tailA(s)
        if s + 1 < NT:
            ffn_f1gelu(s + 1)
        if 8 + s < NT:
            quantize1(8 + s)
        if 10 + s < NT:
            qload(10 + s)
        ffn_f2tailB(s)

    sch.wait_only(SY, [("DO", sch.semc["DO"])])

    with nc.Block() as block:
        @block.sync
        def _(e):
            for f in sch.thunks["sync"]:
                f(e)

        @block.scalar
        def _(e):
            for f in sch.thunks["scalar"]:
                f(e)

        @block.vector
        def _(e):
            for f in sch.thunks["vector"]:
                f(e)

        @block.tensor
        def _(e):
            for f in sch.thunks["tensor"]:
                f(e)

        @block.gpsimd
        def _(e):
            for f in sch.thunks["gpsimd"]:
                f(e)

    es.close()
    return nc


_CACHED = [None]


def kernel(**inputs):
    xq = np.ascontiguousarray(inputs["L0_attn_query"], dtype=np.float32)
    B, TT, DD = xq.shape
    assert (B, TT, DD) == (4, 8192, 1024)
    if _CACHED[0] is None:
        _CACHED[0] = build_program()
    nc = _CACHED[0]
    common = {
        "W_qv": np.ascontiguousarray(inputs["L0_attn_W_qv"], np.float32),
        "W2_0": np.ascontiguousarray(inputs["L0_attn_W2_0"], np.float32),
        "W2_1": np.ascontiguousarray(inputs["L0_attn_W2_1"], np.float32),
        "SCW": np.ascontiguousarray(inputs["L0_attn_scale_weights"], np.float32).reshape(1, 2),
        "W_out": np.ascontiguousarray(inputs["L0_attn_W_out"], np.float32),
        "w1": np.ascontiguousarray(inputs["L0_ffn_w1"], np.float32),
        "b1": np.ascontiguousarray(inputs["L0_ffn_b1"], np.float32).reshape(1, HID),
        "w2": np.ascontiguousarray(inputs["L0_ffn_w2"], np.float32),
        "b2": np.ascontiguousarray(inputs["L0_ffn_b2"], np.float32).reshape(1, D),
        "gam": np.ascontiguousarray(inputs["L0_ffn_gamma"], np.float32).reshape(1, D),
        "bet": np.ascontiguousarray(inputs["L0_ffn_bet" + "a"], np.float32).reshape(1, D),
    }
    in_maps = []
    for c in range(8):
        b, h = c // 2, c % 2
        t0 = h * 4096
        xe = np.zeros((TIN, D), np.float32)
        lo, hi = t0 - P, t0 + 4096 + P
        s_, e_ = max(lo, 0), min(hi, TT)
        xe[s_ - lo:s_ - lo + (e_ - s_)] = xq[b, s_:e_]
        m = dict(common)
        m["x_in"] = xe
        in_maps.append(m)
    res = run_bass_kernel_spmd(nc, in_maps, core_ids=list(range(8)))
    out = np.empty((B, TT, DD), np.float32)
    for c in range(8):
        b, h = c // 2, c % 2
        out[b, h * 4096:(h + 1) * 4096] = res.results[c]["y"]
    return out


# version 25
# speedup vs baseline: 1.0061x; 1.0061x over previous
import contextlib
import numpy as np

import concourse.bass as bass
import concourse.mybir as mybir
from concourse.bass_utils import run_bass_kernel_spmd

F32 = mybir.dt.float32
BF16 = mybir.dt.bfloat16
FP8 = mybir.dt.float8e4
A = mybir.AluOpType
AF = mybir.ActivationFunctionType
AX = mybir.AxisListType

P = 128
D = 1024
KC = 8
NT = 32
NB = 34
TIN = NB * P
C0, C1, HALF = 21, 41, 20
NH, HD = 4, 256
HID = 4096
KH = 32
Z = 235
RR = C1 + Z
ROWP = 4 * RR + 1
SKG = 1024
SKEWN = SKG + 128 * ROWP + SKG
SKEW_TOT = ((2 * SKEWN + 4096 + 2047) // 2048) * 2048
K_CONST = 1.5 * 2.0 ** 82
EPS_PROBE = 2.0 ** -30
MAGIC_LO = 1.5 * 2.0 ** 23

ENGINES = ("sync", "scalar", "vector", "tensor", "gpsimd")
DMA_SEMS = ("DX1", "DX2", "DX3", "DW", "DSKW", "DSKR0", "DSKR1", "DSKR2",
            "DAW", "DAR0", "DAR1", "DAR2", "DAR3",
            "DQW0", "DQW1", "DQR0", "DQR1", "DQR2", "DQR3", "DP", "DPR", "DO")


class Sched:

    def __init__(self, nc, es):
        self.nc = nc
        self.es = es
        self.thunks = {e: [] for e in ENGINES}
        self.semh = {}
        self.semc = {}
        self.lastwait = {e: {} for e in ENGINES}
        self.lasttok = {e: None for e in ENGINES}
        self.wmap = {}
        self.rmap = {}
        self.barrier = None

    def sem(self, name):
        self.semh[name] = self.es.enter_context(self.nc.semaphore(name))
        self.semc[name] = 0

    def op(self, eng, fn, reads=(), writes=(), extra=(), sem=None, amt=1):
        if amt == 16 and sem is not None:
            writes = list(writes) + ["_stream_" + sem]
        waits = []
        for k in reads:
            waits.append(self.wmap.get(k))
        for k in writes:
            waits.append(self.wmap.get(k))
            waits.extend(self.rmap.get(k, ()))
        waits.extend(extra)
        if self.barrier is not None:
            waits.append(self.barrier)
        wl = []
        seen = set()
        for t in waits:
            if t is None or t in seen:
                continue
            seen.add(t)
            s, v = t
            assert v <= self.semc[s], (s, v, self.semc[s])
            wl.append((s, v))
        if sem is None:
            sem = {"sync": "SY", "scalar": "S", "vector": "V",
                   "tensor": "T", "gpsimd": "G"}[eng]
        self.semc[sem] += amt
        tok = (sem, self.semc[sem])
        lw = self.lastwait[eng]
        wl2 = [(s, v) for (s, v) in wl if lw.get(s, -1) < v]
        for s, v in wl:
            if lw.get(s, -1) < v:
                lw[s] = v
        semh = self.semh

        def thunk(e, wl2=wl2, fn=fn, sem=sem, amt=amt):
            for s, v in wl2:
                e.wait_ge(semh[s], v)
            if fn is not None:
                ins = fn(e)
                ins.then_inc(semh[sem], amt)
            else:
                e.nop().then_inc(semh[sem], amt)

        self.thunks[eng].append(thunk)
        for k in reads:
            self.rmap.setdefault(k, []).append(tok)
        for k in writes:
            self.wmap[k] = tok
            self.rmap[k] = []
        self.lasttok[eng] = tok
        return tok

    def wait_only(self, eng, extra):
        semh = self.semh
        wl = [t for t in extra if t is not None]

        def thunk(e, wl=wl):
            for s, v in wl:
                e.wait_ge(semh[s], v)

        self.thunks[eng].append(thunk)

    def full_barrier(self):
        dma_tots = [(s, self.semc[s]) for s in DMA_SEMS]
        for eng in ENGINES:
            ex = [self.lasttok[eng]]
            if eng == "sync":
                ex += dma_tots
            self.op(eng, None, extra=ex, sem="PB")
        self.barrier = ("PB", self.semc["PB"])


def build_program():
    nc = bass.Bass("TRN2", target_bir_lowering=False)
    es = contextlib.ExitStack()

    x_in = nc.dram_tensor("x_in", [TIN, D], F32, kind="ExternalInput")
    W_qv = nc.dram_tensor("W_qv", [D, 2 * D], F32, kind="ExternalInput")
    W2_0 = nc.dram_tensor("W2_0", [D, NH * C0], F32, kind="ExternalInput")
    W2_1 = nc.dram_tensor("W2_1", [D, NH * C1], F32, kind="ExternalInput")
    SCW = nc.dram_tensor("SCW", [1, 2], F32, kind="ExternalInput")
    W_out = nc.dram_tensor("W_out", [D, D], F32, kind="ExternalInput")
    w1 = nc.dram_tensor("w1", [HID, D], F32, kind="ExternalInput")
    b1 = nc.dram_tensor("b1", [1, HID], F32, kind="ExternalInput")
    w2 = nc.dram_tensor("w2", [D, HID], F32, kind="ExternalInput")
    b2 = nc.dram_tensor("b2", [1, D], F32, kind="ExternalInput")
    gam = nc.dram_tensor("gam", [1, D], F32, kind="ExternalInput")
    bet = nc.dram_tensor("bet", [1, D], F32, kind="ExternalInput")
    y = nc.dram_tensor("y", [NT * P, D], F32, kind="ExternalOutput")

    partial_d = nc.dram_tensor("partial_d", [NT * P * D], F32, kind="Internal")
    wq1t_d = nc.dram_tensor("wq1t_d", [KC * P * HID], FP8, kind="Internal")
    wq2t_d = nc.dram_tensor("wq2t_d", [KH * P * D], FP8, kind="Internal")
    skew_d = nc.dram_tensor("skew_d", [SKEW_TOT], BF16, kind="Internal")
    amix_d = nc.dram_tensor("amix_d", [2 * 4 * P * D], BF16, kind="Internal")
    xq1_d = nc.dram_tensor("xq1_d", [4 * 4 * P * D], BF16, kind="Internal")

    SB_BASE = 16512 + 2048
    SB_TOP = 16512 + 212863
    cursor = [SB_BASE]

    def salloc(name, shape, dtype, at=None):
        nb = mybir.dt.size(dtype)
        for s_ in shape[1:]:
            nb *= s_
        if at is None:
            at = cursor[0]
        else:
            at = at + SB_BASE
        t = nc.alloc_sbuf_tensor_at(name, shape, dtype, offset=at)
        assert at + nb <= SB_TOP, (name, at + nb, SB_TOP)
        cursor[0] = max(cursor[0], (at + nb + 127) // 128 * 128)
        return t

    ident = salloc("ident", [P, P], BF16)
    iot = salloc("iot", [P, P], F32)
    magic = salloc("magic", [P, 4], F32)
    cst = salloc("cst", [P, 4], F32)
    swb = salloc("swb", [P, 8], F32)
    onesrow = salloc("onesrow", [1, P], F32)
    onescol = salloc("onescol", [P, 1], F32)
    swrow = salloc("swrow", [1, 8], F32)
    b1row = salloc("b1row", [1, HID], F32)
    b2row = salloc("b2row", [1, D], F32)
    gamrow = salloc("gamrow", [1, D], F32)
    betrow = salloc("betrow", [1, D], F32)
    st_st = salloc("st_st", [P, 2 * NB], F32)
    st_r1 = salloc("st_r1", [P, NB], F32)
    st2_st = salloc("st2_st", [P, 2 * NT], F32)
    st_r2 = salloc("st_r2", [P, NT], F32)
    inv1t = salloc("inv1t", [P, NT], F32)
    bs = salloc("bs", [P, 24], F32)
    sc0 = salloc("sc0", [1, 8], F32)
    sc1 = salloc("sc1", [P, 8], F32)
    sc2 = salloc("sc2", [P, 8], F32)
    negmax = salloc("negmax", [P, 16], F32)
    ssum = salloc("ssum", [P, 16], F32)
    lnsm = salloc("lnsm", [P, 16], F32)
    bias2 = salloc("bias2", [P, 8], F32)
    e0b = salloc("e0b", [P, NH * C0], BF16)
    lnswb = salloc("lnswb", [P, 2], F32)
    RES_END = cursor[0]

    wqv_b = salloc("wqv_b", [P, KC * 2048], BF16)
    w2c_b = salloc("w2c_b", [P, KC * 248], BF16)
    wout_b = salloc("wout_b", [P, KC * D], BF16)
    AACT_BASE = cursor[0]
    xld = salloc("xld", [P, 3 * D], F32)
    xn32 = salloc("xn32", [P, 3 * D], F32)
    xnfm = salloc("xnfm", [P, KC * 8 * P], BF16)
    vtm = salloc("vtm", [P, 9 * D], BF16)
    rqfm = salloc("rqfm", [P, 2 * KC * 512], BF16)
    probs = salloc("probs", [P, 2 * 4 * RR], BF16)
    expt = salloc("expt", [P, 2 * 248], F32)
    pmrg = salloc("pmrg", [P, NH * C1], F32)
    stt = salloc("stt", [P, 4 * 3 * 512], BF16)
    amix_tm = salloc("amix_tm", [P, 2 * D], BF16)
    amix_fm = salloc("amix_fm", [P, KC * 512], BF16)
    x2 = salloc("x2", [P, 2 * D], F32)
    pout = salloc("pout", [P, 2 * D], F32)
    A_END = cursor[0]
    assert A_END <= SB_TOP, A_END

    AB = AACT_BASE - SB_BASE
    wq1asm = salloc("wq1asm", [P, KC * HID], FP8, at=AB)
    wq2asm = salloc("wq2asm", [P, KH * D], FP8, at=AB + 32 * 1024)
    wstage = salloc("wstage", [P, 2 * HID], F32, at=AB + 64 * 1024)
    wqt = salloc("wqt", [P, HID + D], F32, at=AB + 96 * 1024)
    zbf = salloc("zbf", [P, 2048], BF16, at=AB + 116 * 1024)

    ccur = [RES_END - SB_BASE]

    def calloc(name, shape, dtype):
        t = salloc(name, shape, dtype, at=ccur[0])
        nb = mybir.dt.size(dtype)
        for s_ in shape[1:]:
            nb *= s_
        ccur[0] = (ccur[0] + nb + 127) // 128 * 128
        return t

    wq1sb = calloc("wq1sb", [P, KC * HID], FP8)
    wq2sb = calloc("wq2sb", [P, KH * D], FP8)
    b1b = calloc("b1b", [P, HID], F32)
    b2b = calloc("b2b", [P, D], F32)
    gamb = calloc("gamb", [P, D], F32)
    betb = calloc("betb", [P, D], F32)
    xld3 = calloc("xld3", [P, 2 * D], F32)
    w3 = calloc("w3", [P, 2 * D], F32)
    xq1tm = calloc("xq1tm", [P, 2 * D], BF16)
    xq1fm = calloc("xq1fm", [P, 2 * KC * 512], BF16)
    y1 = calloc("y1", [P, HID], F32)
    xq2tm = calloc("xq2tm", [P, KH * P], BF16)
    xq2fm = calloc("xq2fm", [P, KH * P], BF16)
    ppart = calloc("ppart", [P, 2 * D], F32)
    t2s = calloc("t2s", [P, D], F32)
    C_END = ccur[0] + SB_BASE
    assert C_END <= SB_TOP, C_END

    psA = es.enter_context(nc.psum_tensor("psA", [P, 512], F32))
    psB = es.enter_context(nc.psum_tensor("psB", [P, 512], F32))
    ql = es.enter_context(nc.psum_tensor("ql", [P, 512], F32))
    bandA = es.enter_context(nc.psum_tensor("bandA", [P, 512], F32))
    bandB = es.enter_context(nc.psum_tensor("bandB", [P, 512], F32))
    woutA = es.enter_context(nc.psum_tensor("woutA", [P, 512], F32))
    woutB = es.enter_context(nc.psum_tensor("woutB", [P, 512], F32))
    ps_tx = es.enter_context(nc.psum_tensor("ps_tx", [P, 512], BF16))

    AB_PS = [(psA, "psA"), (psB, "psB")]
    WT_PS = [(woutA, "woutA"), (woutB, "woutB")]
    BD_PS = [(bandA, "bandA"), (bandB, "bandB")]
    F1_PS = AB_PS + [(ql, "ql")] + WT_PS

    sch = Sched(nc, es)
    for s_ in ("V", "S", "T", "G", "SY", "PB") + DMA_SEMS:
        sch.sem(s_)
    V, S, T, SY, G = "vector", "scalar", "tensor", "sync", "gpsimd"
    MAG = magic[:, 0:1]
    MAGN = magic[:, 3:4]

    sch.op(G, lambda e: e.iota(iot[:, :], [[1, P]], channel_multiplier=-1,
                               allow_small_or_imprecise_dtypes=True), writes=["iot"])
    sch.op(V, lambda e: e.tensor_scalar(ident[:, :], iot[:, :], 0.0, None, op0=A.is_equal),
           reads=["iot"], writes=["ident"])
    sch.op(V, lambda e: e.tensor_scalar(iot[:, :], iot[:, :], 0.0, None, op0=A.is_equal),
           reads=["iot"], writes=["iot"])
    sch.op(V, lambda e: e.memset(cst[:, 0:1], 1e-5), writes=["cst"])
    sch.op(V, lambda e: e.memset(magic[:, 1:2], 1.0), writes=["mg1"])
    sch.op(V, lambda e: e.tensor_scalar(magic[:, 2:3], magic[:, 1:2], EPS_PROBE, None, op0=A.add),
           reads=["mg1"], writes=["mg2"])
    sch.op(V, lambda e: e.tensor_scalar(magic[:, 1:2], magic[:, 2:3], 1.0, None, op0=A.subtract),
           reads=["mg2"], writes=["mg1"])
    sch.op(V, lambda e: e.tensor_scalar(magic[:, 0:1], magic[:, 1:2], K_CONST, MAGIC_LO,
                                        op0=A.mult, op1=A.add),
           reads=["mg1"], writes=["magic"])
    sch.op(V, lambda e: e.tensor_scalar(magic[:, 3:4], magic[:, 0:1], -1.0, None, op0=A.mult),
           reads=["magic"], writes=["magicn"])
    sch.op(V, lambda e: e.memset(zbf[:, :], 0.0), writes=["zbf"])
    sch.op(V, lambda e: e.memset(onesrow[:, :], 1.0), writes=["onesrow"])
    sch.op(V, lambda e: e.memset(onescol[:, :], 1.0), writes=["onescol"])

    done = 0
    while done < SKEW_TOT:
        n = min(P * 2048, SKEW_TOT - done)
        rows = (n + 2047) // 2048
        sch.op(SY, lambda e, off=done, rows=rows: e.dma_start(
            bass.AP(skew_d, off, [[2048, rows], [1, 2048]]), zbf[0:rows, :]),
            reads=["zbf"], writes=["d_skew0", "d_skew1"], sem="DSKW", amt=16)
        done += rows * 2048

    sch.op(SY, lambda e: e.dma_start(swrow[0:1, 0:2], SCW[:, :]), writes=["swrow"], sem="DP", amt=16)
    sch.op(V, lambda e: e.tensor_reduce(sc0[0:1, 0:1], swrow[0:1, 0:2], AX.X, A.max, negate=True),
           reads=["swrow"], writes=["sc0a"])
    sch.op(S, lambda e: e.activation(sc0[0:1, 2:4], swrow[0:1, 0:2], AF.Exp,
                                     bias=sc0[0:1, 0:1], scale=1.0),
           reads=["swrow", "sc0a"], writes=["sc0b"])
    sch.op(V, lambda e: e.tensor_reduce(sc0[0:1, 1:2], sc0[0:1, 2:4], AX.X, A.add),
           reads=["sc0b"], writes=["sc0c"])
    sch.op(V, lambda e: e.reciprocal(sc0[0:1, 4:5], sc0[0:1, 1:2]),
           reads=["sc0c"], writes=["sc0d"])
    sch.op(V, lambda e: e.tensor_scalar(sc0[0:1, 6:8], sc0[0:1, 2:4], sc0[0:1, 4:5], None, op0=A.mult),
           reads=["sc0b", "sc0d"], writes=["sc0e"])
    sch.op(T, lambda e: e.matmul(ql[:, 0:2], onesrow[0:1, :], sc0[0:1, 6:8],
                                 start=True, stop=True),
           reads=["sc0e", "onesrow"], writes=["ql"])
    sch.op(S, lambda e: e.copy(swb[:, 0:2], ql[:, 0:2]), reads=["ql"], writes=["swb01"])
    sch.op(S, lambda e: e.activation(lnswb[:, 0:2], swb[:, 0:2], AF.Ln, bias=0.0, scale=1.0),
           reads=["swb01"], writes=["lnswb"])

    p0s = ["DW", "DAW", "DQW0", "DAR0", "DAR1", "DAR2", "DAR3", "DQR1"]
    NP0 = len(p0s)

    sch.op(V, lambda e: e.memset(magic[:, 1:2], 0.0), reads=["mg2"], writes=["mg1"])
    sch.op(V, lambda e: e.memset(cst[:, 1:2], 0.0), writes=["c1"])

    def flat_tile(wsrc, t, fi, acc_sl, red_sl, ka, kr):
        so = (fi % 2) * HID
        ws = "wst%d" % (fi % 2)
        sch.op(SY, lambda e, t=t, so=so, wsrc=wsrc: e.dma_start(
            wstage[:, so:so + HID],
            bass.AP(wsrc, t * P * HID, [[HID, P], [1, HID]])),
            writes=[ws], sem=p0s[fi % NP0], amt=16)
        sch.op(V, lambda e, so=so, red_sl=red_sl: e.tensor_reduce(
            red_sl, wstage[:, so:so + HID], AX.X, A.add,
            apply_absolute_value=True),
            reads=[ws], writes=[kr])
        sch.op(V, lambda e, acc_sl=acc_sl, red_sl=red_sl: e.tensor_add(
            acc_sl, acc_sl, red_sl),
            reads=[kr, ka], writes=[ka])

    fi = 0
    for t in range(KC):
        flat_tile(w1, t, fi, magic[:, 1:2], magic[:, 2:3], "mg1", "mg2"); fi += 1
        flat_tile(w2, t, fi, cst[:, 1:2], cst[:, 2:3], "c1", "c2"); fi += 1

    def scale_chain(acc_sl, ka, nel, swcol, swkey):
        sch.op(T, lambda e, acc_sl=acc_sl: e.matmul(
            ql[0:1, 4:5], acc_sl, onescol[:, 0:1], start=True, stop=True),
            reads=[ka, "onescol"], writes=["ql"])
        sch.op(S, lambda e: e.copy(sc0[0:1, 5:6], ql[0:1, 4:5]), reads=["ql"], writes=["sc0f"])
        sch.op(V, lambda e, nel=nel: e.tensor_scalar(sc0[0:1, 5:6], sc0[0:1, 5:6],
                                                     1.0 / nel, 1e-5,
                                                     op0=A.mult, op1=A.max),
               reads=["sc0f"], writes=["sc0f"])
        sch.op(V, lambda e: e.reciprocal(sc0[0:1, 5:6], sc0[0:1, 5:6]),
               reads=["sc0f"], writes=["sc0f"])
        sch.op(T, lambda e: e.matmul(ql[:, 5:6], onesrow[0:1, :], sc0[0:1, 5:6],
                                     start=True, stop=True),
               reads=["sc0f", "onesrow"], writes=["ql"])
        sch.op(S, lambda e, swcol=swcol: e.copy(swb[:, swcol:swcol + 1], ql[:, 5:6]),
               reads=["ql"], writes=[swkey])

    scale_chain(magic[:, 1:2], "mg1", float(KH * P * D), 2, "sw1q")
    scale_chain(cst[:, 1:2], "c1", float(KC * P * HID), 3, "sw2q")
    sch.op(V, lambda e: e.tensor_scalar(magic[:, 1:2], magic[:, 0:1], 1.0, None,
                                        op0=A.add),
           reads=["magic"], writes=["mg1", "mp1"])
    sch.op(V, lambda e: e.tensor_scalar(magic[:, 2:3], magic[:, 0:1], 1.0, None,
                                        op0=A.subtract),
           reads=["magic"], writes=["mg2", "mm1"])

    def ld_cast(dram, cols, dst, dstoff, kidx, wkey, slot):
        so = slot * HID
        sch.op(SY, lambda e: e.dma_start(wstage[:, so:so + cols], dram[kidx * P:(kidx + 1) * P, :]),
               writes=["wst%d" % slot], sem=p0s[(kidx + slot) % NP0], amt=16)
        sch.op(V, lambda e: e.tensor_copy(dst[:, dstoff:dstoff + cols], wstage[:, so:so + cols]),
               reads=["wst%d" % slot], writes=[wkey])

    sl = 0
    for k in range(KC):
        ld_cast(W_qv, 2048, wqv_b, k * 2048, k, "wqv", sl % 2); sl += 1
    for k in range(KC):
        ld_cast(W_out, D, wout_b, k * D, k, "wout", sl % 2); sl += 1
    for k in range(KC):
        ld_cast(W2_0, NH * C0, w2c_b, k * 248, k, "w2c", sl % 2); sl += 1
        ld_cast(W2_1, NH * C1, w2c_b, k * 248 + NH * C0, k, "w2c", sl % 2); sl += 1

    sch.op(SY, lambda e: e.dma_start(b1row[:, :], b1[:, :]), writes=["b1row"], sem="DP", amt=16)
    sch.op(SY, lambda e: e.dma_start(b2row[:, :], b2[:, :]), writes=["b2row"], sem="DP", amt=16)
    sch.op(SY, lambda e: e.dma_start(gamrow[:, :], gam[:, :]), writes=["gamrow"], sem="DP", amt=16)
    sch.op(SY, lambda e: e.dma_start(betrow[:, :], bet[:, :]), writes=["betrow"], sem="DP", amt=16)


    pend = []
    evct = [0]

    def flush_evacs(keep):
        while len(pend) > keep:
            fn, kpst, asmkey = pend.pop(0)
            if evct[0] % 2 == 0:
                sch.op(S, lambda e, fn=fn: fn(e, 0), reads=[kpst, "magicn"],
                       writes=[asmkey])
            else:
                sch.op(V, lambda e, fn=fn: fn(e, 1), reads=[kpst, "magic"],
                       writes=[asmkey])
            evct[0] += 1

    def quant_tile(wsrc, t, ti, cols, swcol, swkey, asmdst, asmkey, asm_k_step,
                   wqo, wqk, stg, stgname):
        so = (ti % 2) * HID
        ws = "%s%d" % (stgname, ti % 2)
        sch.op(SY, lambda e, t=t, so=so, cols=cols, wsrc=wsrc, stg=stg: e.dma_start(
            stg[:, so:so + cols], wsrc[t * P:(t + 1) * P, :]),
            writes=[ws], sem=p0s[ti % NP0], amt=16)
        for c in range(cols // 512):
            co = c * 512
            sch.op(S, lambda e, so=so, co=co, swcol=swcol, stg=stg: e.activation(
                stg[:, so + co:so + co + 512], stg[:, so + co:so + co + 512],
                AF.Identity, bias=MAG, scale=swb[:, swcol:swcol + 1]),
                reads=[ws, swkey, "magic"], writes=[ws])
            wqkc = "%s_%d" % (wqk, c % 4)
            sch.op(V, lambda e, so=so, co=co, wqo=wqo, stg=stg: e.tensor_scalar(
                wqt[:, wqo + co:wqo + co + 512], stg[:, so + co:so + co + 512],
                magic[:, 1:2], magic[:, 2:3], op0=A.min, op1=A.max),
                reads=[ws, "mp1", "mm1"], writes=[wqkc])
            pst, kpst = PH0_PS[(ti * 8 + c) % 6]
            for i in range(4):
                kb = c * 4 + i
                sch.op(T, lambda e, kb=kb, i=i, wqo=wqo, pst=pst: e.transpose(
                    pst[:, i * P:(i + 1) * P],
                    wqt[:, wqo + kb * P:wqo + (kb + 1) * P], iot[:, :]),
                    reads=[wqkc, "iot"], writes=[kpst])

            def evac(e, eng, t=t, c=c, asmdst=asmdst, asm_k_step=asm_k_step,
                     pst=pst):
                dst = bass.AP(asmdst, c * 4 * asm_k_step + t * P,
                              [[asmdst.shape[1], P], [asm_k_step, 4], [1, P]])
                if eng == 0:
                    return e.activation(dst, pst[:, 0:512], AF.Identity,
                                        bias=MAGN, scale=1.0)
                return e.tensor_scalar(dst, pst[:, 0:512], MAG, None,
                                       op0=A.subtract)
            pend.append((evac, kpst, asmkey))
            flush_evacs(2)

    PH0_PS = AB_PS + BD_PS + WT_PS
    ti = 0
    for g in range(KC):
        for i in range(4):
            quant_tile(w1, 4 * g + i, ti, D, 2, "sw1q", wq1asm, "wq1asm", HID,
                       HID, "wqt1", wstage, "wst"); ti += 1
        quant_tile(w2, g, ti, HID, 3, "sw2q", wq2asm, "wq2asm", D,
                   0, "wqt2", wstage, "wst"); ti += 1

    flush_evacs(0)
    sch.op(SY, lambda e: e.dma_start(bass.AP(wq1t_d, 0, [[KC * HID, P], [1, KC * HID]]),
                                     wq1asm[:, :]),
           reads=["wq1asm"], writes=["d_wq1"], sem="DW", amt=16)
    sch.op(SY, lambda e: e.dma_start(bass.AP(wq2t_d, 0, [[KH * D, P], [1, KH * D]]),
                                     wq2asm[:, :]),
           reads=["wq2asm"], writes=["d_wq2"], sem="DW", amt=16)

    def vload(eidx):
        t = eidx + 1
        par = t % 3
        sch.op(G, lambda e, t=t, par=par: e.dma_start(
            xld[:, par * D:(par + 1) * D], x_in[t * P:(t + 1) * P, :]),
            reads=["d_wq1", "d_wq2"],
            writes=["xld%d" % par], sem="DX1", amt=16)

    def vpath_ln(eidx):
        t = eidx + 1
        par = t % 3
        kx = "xld%d" % par
        kn = "xn32%d" % (t % 3)
        kb2 = "bs%d" % (t % 2)
        bso = (t % 2) * 12
        sch.op(V, lambda e: e.bn_stats(bs[:, bso:bso + 6], xld[:, par * D:par * D + 512]),
               reads=[kx], writes=[kb2])
        sch.op(V, lambda e: e.bn_stats(bs[:, bso + 6:bso + 12], xld[:, par * D + 512:par * D + D]),
               reads=[kx], writes=[kb2])
        sch.op(V, lambda e: e.bn_aggr(st_st[:, 2 * t:2 * t + 2], bs[:, bso:bso + 12]),
               reads=[kb2], writes=["st%d" % t])
        sch.op(S, lambda e: e.activation(st_r1[:, t:t + 1], st_st[:, 2 * t + 1:2 * t + 2],
                                         AF.Ln, bias=cst[:, 0:1], scale=1.0),
               reads=["st%d" % t, "cst"], writes=["str%d" % t])
        sch.op(S, lambda e: e.activation(st_r1[:, t:t + 1], st_r1[:, t:t + 1],
                                         AF.Exp, bias=0.0, scale=-0.5),
               reads=["str%d" % t], writes=["str%d" % t])
        sch.op(V, lambda e: e.tensor_scalar(xn32[:, (t % 3) * D:(t % 3 + 1) * D],
                                            xld[:, par * D:(par + 1) * D],
                                            st_st[:, 2 * t:2 * t + 1], st_r1[:, t:t + 1],
                                            op0=A.subtract, op1=A.mult),
               reads=[kx, "st%d" % t, "str%d" % t], writes=[kn])
        if 0 <= eidx < NT:
            s = eidx
            sch.op(V, lambda e: e.bn_stats(bs[:, bso:bso + 6],
                                           xn32[:, (t % 3) * D:(t % 3) * D + 512]),
                   reads=[kn], writes=[kb2])
            sch.op(V, lambda e: e.bn_stats(bs[:, bso + 6:bso + 12],
                                           xn32[:, (t % 3) * D + 512:(t % 3) * D + D]),
                   reads=[kn], writes=[kb2])
            sch.op(V, lambda e: e.bn_aggr(st2_st[:, 2 * s:2 * s + 2], bs[:, bso:bso + 12]),
                   reads=[kb2], writes=["st2_%d" % s])
            sch.op(S, lambda e: e.activation(st_r2[:, s:s + 1], st2_st[:, 2 * s + 1:2 * s + 2],
                                             AF.Ln, bias=cst[:, 0:1], scale=1.0),
                   reads=["st2_%d" % s, "cst"], writes=["st2r%d" % s])
            sch.op(S, lambda e: e.activation(st_r2[:, s:s + 1], st_r2[:, s:s + 1],
                                             AF.Exp, bias=0.0, scale=-0.5),
                   reads=["st2r%d" % s], writes=["st2r%d" % s])

    def vpath_tv(eidx):
        t = eidx + 1
        slot = eidx % 8
        slot9 = eidx % 9
        kn = "xn32%d" % (t % 3)
        kf = "xnfm%d" % slot
        kv = "vtm%d" % slot9
        for r0 in (0, 4):
            pst, kpst = AB_PS[(eidx * 3 + r0 // 4) % 2]
            for i in range(4):
                kb = r0 + i
                sch.op(T, lambda e, kb=kb, i=i, pst=pst, t=t: e.transpose(
                    pst[:, i * P:(i + 1) * P],
                    xn32[:, (t % 3) * D + kb * P:(t % 3) * D + (kb + 1) * P], iot[:, :]),
                    reads=[kn, "iot"], writes=[kpst])

            def evacx(e, r0=r0, slot=slot, pst=pst):
                dst = bass.AP(xnfm, r0 * (8 * P) + slot * P,
                              [[KC * 8 * P, P], [8 * P, 4], [1, P]])
                return e.tensor_copy(dst, pst[:, 0:512])
            sch.op(V, evacx, reads=[kpst], writes=[kf])
        for n in range(2):
            pst, kpst = AB_PS[(eidx * 3 + 2 + n) % 2]

            def vmm(e, slot=slot, n=n, pst=pst):
                ins = None
                for k in range(KC):
                    ins = e.matmul(
                        pst[:, 0:512],
                        xnfm[:, k * (8 * P) + slot * P:k * (8 * P) + (slot + 1) * P],
                        wqv_b[:, k * 2048 + D + n * 512:k * 2048 + D + (n + 1) * 512],
                        start=(k == 0), stop=(k == KC - 1))
                return ins
            sch.op(T, vmm, reads=[kf, "wqv"], writes=[kpst])
            sch.op(S, lambda e, slot9=slot9, n=n, pst=pst: e.copy(
                vtm[:, slot9 * D + n * 512:slot9 * D + (n + 1) * 512], pst[:, 0:512]),
                reads=[kpst], writes=[kv])

    def qpath(quad):
        qpar = quad % 2
        slot0 = (4 * quad) % 8
        kfs = ["xnfm%d" % ((4 * quad + i) % 8) for i in range(4)]
        for m in range(KC):
            pst, kpst = WT_PS[m % 2]

            def qmm(e, m=m, slot0=slot0, pst=pst):
                ins = None
                for k in range(KC):
                    ins = e.matmul(
                        pst[:, 0:512],
                        wqv_b[:, k * 2048 + m * P:k * 2048 + (m + 1) * P],
                        xnfm[:, k * (8 * P) + slot0 * P:k * (8 * P) + slot0 * P + 512],
                        start=(k == 0), stop=(k == KC - 1))
                return ins
            sch.op(T, qmm, reads=kfs + ["wqv"], writes=[kpst])
            sch.op(V, lambda e, m=m, qpar=qpar, pst=pst: e.tensor_scalar(
                rqfm[:, qpar * 4096 + m * 512:qpar * 4096 + (m + 1) * 512],
                pst[:, 0:512], 0.0, None, op0=A.max),
                reads=[kpst], writes=["rqfm%d" % qpar])

    def attn1a(s):
        par = s % 2
        quad = s // 4
        qpar = quad % 2
        eo = par * 248

        def lmm(e, s=s, qpar=qpar):
            ins = None
            for k in range(KC):
                ins = e.matmul(
                    ql[:, 0:248],
                    rqfm[:, qpar * 4096 + k * 512 + (s % 4) * P:
                         qpar * 4096 + k * 512 + (s % 4 + 1) * P],
                    w2c_b[:, k * 248:(k + 1) * 248],
                    start=(k == 0), stop=(k == KC - 1))
            return ins
        sch.op(T, lmm, reads=["rqfm%d" % qpar, "w2c"], writes=["ql"])
        sch.op(V, lambda e, eo=eo: e.tensor_copy(expt[:, eo:eo + 248], ql[:, 0:248]),
               reads=["ql"], writes=["expt%d" % par])
        sch.op(V, lambda e, eo=eo, par=par: e.tensor_reduce(
            bass.AP(negmax, par * 8, [[16, P], [1, 4]]),
            expt[:, eo:eo + 84].rearrange("p (h c) -> p h c", c=C0),
            AX.X, A.max, negate=True),
            reads=["expt%d" % par], writes=["negmax0_%d" % par])
        sch.op(V, lambda e, eo=eo, par=par: e.tensor_reduce(
            bass.AP(negmax, par * 8 + 4, [[16, P], [1, 4]]),
            expt[:, eo + 84:eo + 248].rearrange("p (h c) -> p h c", c=C1),
            AX.X, A.max, negate=True),
            reads=["expt%d" % par], writes=["negmax1_%d" % par])

    def attn1b1(s):
        par = s % 2
        eo = par * 248
        no = par * 8
        groups = [(h, 0, h * C0, C0) for h in range(NH)] + \
                 [(h, 1, NH * C0 + h * C1, C1) for h in range(NH)]
        for gi, (h, ci, c0, cw) in enumerate(groups):
            sch.op(S, lambda e, gi=gi, c0=c0, cw=cw, eo=eo, no=no: e.activation(
                pmrg[:, 0:cw], expt[:, eo + c0:eo + c0 + cw], AF.Exp,
                bias=negmax[:, no + gi:no + gi + 1], scale=1.0,
                accum_out=ssum[:, no + gi:no + gi + 1]),
                reads=["expt%d" % par, "negmax0_%d" % par, "negmax1_%d" % par],
                writes=["pmgj", "ssum%d_%d" % (par, gi)])
        sch.op(S, lambda e, no=no: e.activation(lnsm[:, no:no + 8], ssum[:, no:no + 8],
                                                AF.Ln, bias=0.0, scale=1.0),
               reads=["ssum%d_%d" % (par, g_) for g_ in range(8)], writes=["lnsm%d" % par])

    def attn1b2(s):
        par = s % 2
        kp = "probs%d" % par
        kst = "stt%d" % (s % 4)
        eo = par * 248
        no = par * 8
        sch.op(V, lambda e, no=no: e.tensor_sub(bias2[:, 0:8], negmax[:, no:no + 8], lnsm[:, no:no + 8]),
               reads=["negmax0_%d" % par, "negmax1_%d" % par, "lnsm%d" % par], writes=["bias2"])
        sch.op(V, lambda e: e.tensor_scalar(bias2[:, 0:4], bias2[:, 0:4], lnswb[:, 0:1], None, op0=A.add),
               reads=["lnswb"], writes=["bias2"])
        sch.op(V, lambda e: e.tensor_scalar(bias2[:, 4:8], bias2[:, 4:8], lnswb[:, 1:2], None, op0=A.add),
               reads=["lnswb"], writes=["bias2"])
        for h in range(NH):
            sch.op(S, lambda e, h=h, par=par, eo=eo: e.activation(
                probs[:, par * (4 * RR) + h * RR:par * (4 * RR) + h * RR + C1],
                expt[:, eo + 84 + h * C1:eo + 84 + (h + 1) * C1], AF.Exp,
                bias=bias2[:, 4 + h:5 + h], scale=1.0),
                reads=["expt%d" % par, "bias2"], writes=[kp])
            sch.op(S, lambda e, h=h, eo=eo: e.activation(
                e0b[:, h * C0:(h + 1) * C0],
                expt[:, eo + h * C0:eo + (h + 1) * C0], AF.Exp,
                bias=bias2[:, h:h + 1], scale=1.0),
                reads=["expt%d" % par, "bias2"], writes=["e0b"])
        for h in range(NH):
            sch.op(V, lambda e, h=h, par=par: e.tensor_add(
                probs[:, par * (4 * RR) + h * RR + 10:par * (4 * RR) + h * RR + 31],
                probs[:, par * (4 * RR) + h * RR + 10:par * (4 * RR) + h * RR + 31],
                e0b[:, h * C0:(h + 1) * C0]),
                reads=["e0b"], writes=[kp])
        sch.op(SY, lambda e, par=par: e.dma_start(
            bass.AP(skew_d, par * SKEWN + SKG, [[ROWP, P], [1, 4 * RR]]),
            probs[:, par * (4 * RR):(par + 1) * (4 * RR)]),
            reads=[kp], writes=["d_skew%d" % par], sem="DSKW", amt=16)
        for g in range(3):
            sch.op(SY, lambda e, g=g, par=par, s=s: e.dma_start(
                stt[:, ((s % 4) * 3 + g) * 512:((s % 4) * 3 + g + 1) * 512],
                bass.AP(skew_d, par * SKEWN + SKG + g * P - 108, [[RR, 512], [1, P]]),
                transpose=True),
                reads=["d_skew%d" % par], writes=["%s_%d" % (kst, g)],
                sem="DSKR%d" % g, amt=16)

    def attn2(s):
        par = s % 2
        kst = ["stt%d_%d" % (s % 4, g) for g in range(3)]
        vkeys = ["vtm%d" % ((s - 1 + g) % 9) for g in range(3)]
        for half in range(2):
            pst, kpst = BD_PS[(s * 2 + half) % 2]

            def bmm(e, s=s, half=half, pst=pst):
                ins = None
                for h in (half * 2, half * 2 + 1):
                    for g in range(3):
                        vslot = (s - 1 + g) % 9
                        ins = e.matmul(
                            pst[:, (h - half * 2) * HD:(h - half * 2 + 1) * HD],
                            bass.AP(stt, ((s % 4) * 3 + g) * 512 + h, [[4 * 3 * 512, P], [4, P]]),
                            vtm[:, vslot * D + h * HD:vslot * D + (h + 1) * HD],
                            start=(g == 0), stop=(g == 2))
                return ins
            sch.op(T, bmm, reads=kst + vkeys, writes=[kpst])
            sch.op(V, lambda e, par=par, half=half, pst=pst: e.tensor_copy(
                amix_tm[:, par * D + half * 512:par * D + (half + 1) * 512], pst[:, 0:512]),
                reads=[kpst], writes=["amix%d" % par])
        sch.op(SY, lambda e, s=s, par=par: e.dma_start(
            bass.AP(amix_d, (s // 4 % 2) * 4 * P * D + (s % 4) * P * D, [[D, P], [1, D]]),
            amix_tm[:, par * D:(par + 1) * D]),
            reads=["amix%d" % par], writes=["d_amix%d" % (s // 4 % 2)], sem="DAW", amt=16)

    def wout_read(quad):
        qpar = quad % 2
        for k in range(KC):
            sch.op(SY, lambda e, k=k, qpar=qpar: e.dma_start(
                amix_fm[:, k * 512:(k + 1) * 512],
                bass.AP(amix_d, qpar * 4 * P * D + k * P, [[D, 512], [1, P]]),
                transpose=True),
                reads=["d_amix%d" % qpar], writes=["amixfm%d" % k],
                sem="DAR%d" % (k % 4), amt=16)

    def wout_mm(quad):
        amixkeys = ["amixfm%d" % k for k in range(KC)]
        for i in range(4):
            s = 4 * quad + i
            par = s % 2
            sch.op(G, lambda e, s=s, par=par: e.dma_start(
                x2[:, par * D:(par + 1) * D], x_in[(s + 1) * P:(s + 2) * P, :]),
                writes=["x2%d" % par], sem="DX2", amt=16)
            for n in range(2):
                pst, kpst = WT_PS[(i * 2 + n) % 2]

                def omm(e, i=i, n=n, pst=pst):
                    ins = None
                    for k in range(KC):
                        ins = e.matmul(
                            pst[:, 0:512],
                            amix_fm[:, k * 512 + i * P:k * 512 + (i + 1) * P],
                            wout_b[:, k * D + n * 512:k * D + (n + 1) * 512],
                            start=(k == 0), stop=(k == KC - 1))
                    return ins
                sch.op(T, omm, reads=amixkeys + ["wout"], writes=[kpst])
                sch.op(V, lambda e, par=par, n=n, pst=pst: e.scalar_tensor_tensor(
                    pout[:, par * D + n * 512:par * D + (n + 1) * 512], pst[:, 0:512], 1.0,
                    x2[:, par * D + n * 512:par * D + (n + 1) * 512], op0=A.mult, op1=A.add),
                    reads=[kpst, "x2%d" % par], writes=["pout%d" % par])
            sch.op(SY, lambda e, s=s, par=par: e.dma_start(
                bass.AP(partial_d, s * P * D, [[D, P], [1, D]]),
                pout[:, par * D:(par + 1) * D]),
                reads=["pout%d" % par], writes=["d_part%d" % s], sem="DP", amt=16)

    for e_ in (-1, 0, 1):
        vload(e_)
    for kind, e_ in (("ln", -1), ("vl", 2), ("ln", 0), ("vl", 3), ("ln", 1),
                     ("vl", 4), ("tv", -1), ("ln", 2), ("vl", 5), ("tv", 0),
                     ("ln", 3), ("vl", 6), ("tv", 1), ("ln", 4), ("vl", 7),
                     ("tv", 2), ("ln", 5), ("tv", 3)):
        {"ln": vpath_ln, "tv": vpath_tv, "vl": vload}[kind](e_)
    sch.full_barrier()
    sch.op(V, lambda e: e.memset(probs[:, :], 0.0), writes=["probs0", "probs1"])

    for s in range(NT + 7):
        if s + 8 <= NB - 2:
            vload(s + 8)
        if 0 <= s - 4 < NT:
            attn2(s - 4)
        if s + 4 <= NB - 2:
            vpath_tv(s + 4)
        if s + 6 <= NB - 2:
            vpath_ln(s + 6)
        if (s - 7) % 4 == 0 and 0 <= s - 7 < NT:
            wout_read((s - 7) // 4)
        if s % 4 == 0 and s < NT:
            qpath(s // 4)
        if 0 <= s - 2 < NT:
            attn1b2(s - 2)
        if s < NT:
            attn1a(s)
        if 0 <= s - 1 < NT:
            attn1b1(s - 1)
        if (s - 10) % 4 == 0 and 0 <= s - 10 < NT:
            wout_mm((s - 10) // 4)

    sch.full_barrier()

    for j in range(4):
        w = KC * HID // 4
        sch.op(SY, lambda e, j=j, w=w: e.dma_start(
            wq1sb[:, j * w:(j + 1) * w],
            bass.AP(wq1t_d, j * w, [[KC * HID, P], [1, w]])),
            reads=["d_wq1"], writes=["wq1sb_%d" % j], sem=p0s[j], amt=16)
        sch.op(SY, lambda e, j=j, w=w: e.dma_start(
            wq2sb[:, j * w:(j + 1) * w],
            bass.AP(wq2t_d, j * w, [[KH * D, P], [1, w]])),
            reads=["d_wq2"], writes=["wq2sb_%d" % j], sem=p0s[4 + j], amt=16)
    bi = 0
    for (row, rkey, dst, dkey, n) in ((gamrow, "gamrow", gamb, "gamb", D),
                                      (betrow, "betrow", betb, "betb", D),
                                      (b1row, "b1row", b1b, "b1b", HID),
                                      (b2row, "b2row", b2b, "b2b", D)):
        for c in range(0, n, 512):
            pst, kpst = WT_PS[bi % 2]
            bi += 1
            sch.op(T, lambda e, row=row, c=c, pst=pst: e.matmul(
                pst[:, 0:512], onesrow[0:1, :], row[0:1, c:c + 512],
                start=True, stop=True),
                reads=[rkey, "onesrow"], writes=[kpst])
            sch.op(S, lambda e, dst=dst, c=c, pst=pst: e.copy(dst[:, c:c + 512], pst[:, 0:512]),
                   reads=[kpst], writes=[dkey])

    def qload(s):
        par = s % 2
        tb = s + 1
        sch.op(G, lambda e, tb=tb, par=par: e.dma_start(
            xld3[:, par * D:(par + 1) * D], x_in[tb * P:(tb + 1) * P, :]),
            writes=["xld3%d" % par], sem="DX3", amt=16)

    def quantize1(s):
        par = s % 2
        tb = s + 1
        kx, kw, kq = "xld3%d" % par, "w3%d" % par, "xq1tm%d" % par
        sch.op(V, lambda e: e.tensor_scalar(
            w3[:, par * D:(par + 1) * D], xld3[:, par * D:(par + 1) * D],
            st_st[:, 2 * tb:2 * tb + 1], st_r1[:, tb:tb + 1], op0=A.subtract, op1=A.mult),
            reads=[kx, "st%d" % tb, "str%d" % tb], writes=[kw])
        sch.op(V, lambda e: e.tensor_scalar(
            w3[:, par * D:(par + 1) * D], w3[:, par * D:(par + 1) * D],
            st2_st[:, 2 * s:2 * s + 1], st_r2[:, s:s + 1], op0=A.subtract, op1=A.mult),
            reads=[kw, "st2_%d" % s, "st2r%d" % s], writes=[kw])
        sch.op(V, lambda e: e.tensor_mul(w3[:, par * D:(par + 1) * D],
                                         w3[:, par * D:(par + 1) * D], gamb[:, :]),
               reads=[kw, "gamb"], writes=[kw])
        sch.op(V, lambda e: e.tensor_add(w3[:, par * D:(par + 1) * D],
                                         w3[:, par * D:(par + 1) * D], betb[:, :]),
               reads=[kw, "betb"], writes=[kw])
        sch.op(V, lambda e: e.tensor_reduce(sc1[:, 0:1], w3[:, par * D:(par + 1) * D],
                                            AX.X, A.max, apply_absolute_value=True),
               reads=[kw], writes=["sc1a"])
        sch.op(V, lambda e: e.tensor_scalar(sc1[:, 1:2], sc1[:, 0:1], 1e-5, None, op0=A.max),
               reads=["sc1a"], writes=["sc1b"])
        sch.op(V, lambda e: e.reciprocal(sc1[:, 2:3], sc1[:, 1:2]),
               reads=["sc1b"], writes=["sc1c"])
        sch.op(V, lambda e: e.tensor_scalar(sc1[:, 3:4], sc1[:, 2:3], 127.0, None, op0=A.mult),
               reads=["sc1c"], writes=["sc1d"])
        sch.op(V, lambda e: e.tensor_scalar(sc1[:, 4:5], sc1[:, 3:4], swb[:, 2:3], None, op0=A.mult),
               reads=["sc1d", "sw1q"], writes=["sc1e"])
        sch.op(V, lambda e: e.reciprocal(inv1t[:, s:s + 1], sc1[:, 4:5]),
               reads=["sc1e"], writes=["inv1_%d" % s])
        sch.op(S, lambda e: e.activation(
            w3[:, par * D:(par + 1) * D], w3[:, par * D:(par + 1) * D], AF.Identity,
            bias=MAG, scale=sc1[:, 3:4]),
            reads=[kw, "sc1d", "magic"], writes=[kw])
        sch.op(S, lambda e: e.activation(
            xq1tm[:, par * D:(par + 1) * D], w3[:, par * D:(par + 1) * D], AF.Identity,
            bias=MAGN, scale=1.0),
            reads=[kw, "magicn"], writes=[kq])
        sch.op(SY, lambda e, s=s: e.dma_start(
            bass.AP(xq1_d, (s // 4 % 4) * 4 * P * D + (s % 4) * P * D, [[D, P], [1, D]]),
            xq1tm[:, par * D:(par + 1) * D]),
            reads=[kq], writes=["d_xq1_%d" % (s % 16)], sem="DQW%d" % (s % 2), amt=16)

    def xq1_read(quad):
        qpar = quad % 2
        for k in range(KC):
            sch.op(SY, lambda e, k=k, quad=quad, qpar=qpar: e.dma_start(
                xq1fm[:, qpar * 4096 + k * 512:qpar * 4096 + (k + 1) * 512],
                bass.AP(xq1_d, (quad % 4) * 4 * P * D + k * P, [[D, 512], [1, P]]),
                transpose=True),
                reads=["d_xq1_%d" % ((4 * quad + i_) % 16) for i_ in range(4)],
                writes=["xq1fm%d_%d" % (qpar, k)],
                sem="DQR%d" % (k % 4), amt=16)

    def ffn_f1(s):
        quad = s // 4
        qpar = quad % 2
        xq1keys = ["xq1fm%d_%d" % (qpar, k) for k in range(KC)]
        for c in range(8):
            pst, kpst = F1_PS[(s * 8 + c) % 5]

            def fmm(e, c=c, s=s, qpar=qpar, pst=pst):
                ins = None
                for k in range(KC):
                    ins = e.matmul(
                        pst[:, 0:512],
                        xq1fm[:, qpar * 4096 + k * 512 + (s % 4) * P:
                              qpar * 4096 + k * 512 + (s % 4 + 1) * P],
                        wq1sb[:, k * HID + c * 512:k * HID + (c + 1) * 512],
                        start=(k == 0), stop=(k == KC - 1))
                return ins
            sch.op(T, fmm, reads=xq1keys + ["wq1sb_%d" % j_ for j_ in range(4)], writes=[kpst])
            sch.op(V, lambda e, c=c, s=s, pst=pst, kpst=kpst: e.scalar_tensor_tensor(
                y1[:, c * 512:(c + 1) * 512], pst[:, 0:512], inv1t[:, s:s + 1],
                b1b[:, c * 512:(c + 1) * 512], op0=A.mult, op1=A.add),
                reads=[kpst, "inv1_%d" % s, "b1b"], writes=["y1h%d" % (c // 4)])

    def ffn_f1gelu(s):
        sch.op(S, lambda e: e.activation(y1[:, 0:2048], y1[:, 0:2048], AF.Gelu,
                                         bias=0.0, scale=1.0),
               reads=["y1h0"], writes=["y1h0"])
        sch.op(S, lambda e: e.activation(y1[:, 2048:HID], y1[:, 2048:HID], AF.Gelu,
                                         bias=0.0, scale=1.0),
               reads=["y1h1"], writes=["y1h1"])

    def ffn_f2head(s):
        par = s % 2
        sch.op(SY, lambda e, s=s, par=par: e.dma_start(
            ppart[:, par * D:(par + 1) * D],
            bass.AP(partial_d, s * P * D, [[D, P], [1, D]])),
            reads=["d_part%d" % s], writes=["ppart%d" % par], sem="DPR", amt=16)
        sch.op(V, lambda e: e.tensor_reduce(sc2[:, 6:7], y1[:, 0:2048], AX.X, A.max,
                                            apply_absolute_value=True),
               reads=["y1h0"], writes=["sc2m0"])
        sch.op(V, lambda e: e.tensor_reduce(sc2[:, 7:8], y1[:, 2048:HID], AX.X, A.max,
                                            apply_absolute_value=True),
               reads=["y1h1"], writes=["sc2m1"])
        sch.op(V, lambda e: e.tensor_scalar(sc2[:, 0:1], sc2[:, 6:7], sc2[:, 7:8],
                                            None, op0=A.max),
               reads=["sc2m0", "sc2m1"], writes=["sc2a"])
        sch.op(V, lambda e: e.tensor_scalar(sc2[:, 1:2], sc2[:, 0:1], 1e-5, None, op0=A.max),
               reads=["sc2a"], writes=["sc2b"])
        sch.op(V, lambda e: e.reciprocal(sc2[:, 2:3], sc2[:, 1:2]),
               reads=["sc2b"], writes=["sc2c"])
        sch.op(V, lambda e: e.tensor_scalar(sc2[:, 3:4], sc2[:, 2:3], 127.0, None, op0=A.mult),
               reads=["sc2c"], writes=["sc2d"])
        sch.op(V, lambda e: e.tensor_scalar(sc2[:, 4:5], sc2[:, 3:4], swb[:, 3:4], None, op0=A.mult),
               reads=["sc2d", "sw2q"], writes=["sc2e"])
        sch.op(V, lambda e: e.reciprocal(sc2[:, 5:6], sc2[:, 4:5]),
               reads=["sc2e"], writes=["sc2f"])
        for h, kh, kx in ((0, "y1h0", "xq2tm0"), (1, "y1h1", "xq2tm1")):
            lo, hi = h * 2048, (h + 1) * 2048
            sch.op(S, lambda e, lo=lo, hi=hi: e.activation(
                y1[:, lo:hi], y1[:, lo:hi], AF.Identity,
                bias=MAG, scale=sc2[:, 3:4]),
                reads=[kh, "sc2d", "magic"], writes=[kh])
            sch.op(S, lambda e, lo=lo, hi=hi: e.activation(
                xq2tm[:, lo:hi], y1[:, lo:hi], AF.Identity,
                bias=MAGN, scale=1.0),
                reads=[kh, "magicn"], writes=[kx])

    def ffn_f2tailA(s):
        for h in range(2):
            for r0 in range(h * 16, h * 16 + 16, 4):
                for i in range(4):
                    kb = r0 + i
                    sch.op(T, lambda e, kb=kb, i=i: e.transpose(
                        ps_tx[:, i * P:(i + 1) * P], xq2tm[:, kb * P:(kb + 1) * P],
                        ident[:, :]),
                        reads=["xq2tm%d" % (kb // 16), "ident"], writes=["pstx"])
                sch.op(S, lambda e, r0=r0: e.copy(
                    xq2fm[:, r0 * P:(r0 + 4) * P], ps_tx[:, 0:512]),
                    reads=["pstx"], writes=["xq2fm%d" % (r0 // 16)])
            for n in range(2):
                pst, kpst = BD_PS[n % 2]

                def f2mm(e, n=n, h=h, pst=pst):
                    ins = None
                    for k in range(h * 16, h * 16 + 16):
                        ins = e.matmul(
                            pst[:, 0:512],
                            xq2fm[:, k * P:(k + 1) * P],
                            wq2sb[:, k * D + n * 512:k * D + (n + 1) * 512],
                            start=(k == 0), stop=(k == KH - 1))
                    return ins
                sch.op(T, f2mm,
                       reads=["xq2fm%d" % h] + ["wq2sb_%d" % j_ for j_ in range(4)],
                       writes=[kpst])

    def ffn_f2tailB(s):
        par = s % 2
        for n in range(2):
            pst, kpst = BD_PS[n % 2]
            sch.op(V, lambda e, n=n, pst=pst, kpst=kpst: e.scalar_tensor_tensor(
                t2s[:, n * 512:(n + 1) * 512], pst[:, 0:512], sc2[:, 5:6],
                b2b[:, n * 512:(n + 1) * 512], op0=A.mult, op1=A.add),
                reads=[kpst, "sc2f", "b2b"], writes=["t2s"])
        sch.op(V, lambda e, par=par: e.tensor_add(
            ppart[:, par * D:(par + 1) * D], ppart[:, par * D:(par + 1) * D],
            t2s[:, 0:D]),
            reads=["t2s", "ppart%d" % par], writes=["ppart%d" % par])
        sch.op(SY, lambda e, s=s, par=par: e.dma_start(
            y[s * P:(s + 1) * P, :], ppart[:, par * D:(par + 1) * D]),
            reads=["ppart%d" % par], writes=["ppart%d" % par], sem="DO", amt=16)

    qload(0)
    qload(1)
    for s in range(8):
        quantize1(s)
        qload(s + 2)
    xq1_read(0)
    ffn_f1(0)
    ffn_f1gelu(0)
    for s in range(NT):
        ffn_f2head(s)
        if s % 4 == 0 and (s + 4) // 4 < NT // 4:
            xq1_read((s + 4) // 4)
        if s + 1 < NT:
            ffn_f1(s + 1)
        ffn_f2tailA(s)
        if s + 1 < NT:
            ffn_f1gelu(s + 1)
        if 8 + s < NT:
            quantize1(8 + s)
        if 10 + s < NT:
            qload(10 + s)
        ffn_f2tailB(s)

    sch.wait_only(SY, [("DO", sch.semc["DO"])])

    with nc.Block() as block:
        @block.sync
        def _(e):
            for f in sch.thunks["sync"]:
                f(e)

        @block.scalar
        def _(e):
            for f in sch.thunks["scalar"]:
                f(e)

        @block.vector
        def _(e):
            for f in sch.thunks["vector"]:
                f(e)

        @block.tensor
        def _(e):
            for f in sch.thunks["tensor"]:
                f(e)

        @block.gpsimd
        def _(e):
            for f in sch.thunks["gpsimd"]:
                f(e)

    es.close()
    return nc


_CACHED = [None]


def kernel(**inputs):
    xq = np.ascontiguousarray(inputs["L0_attn_query"], dtype=np.float32)
    B, TT, DD = xq.shape
    assert (B, TT, DD) == (4, 8192, 1024)
    if _CACHED[0] is None:
        _CACHED[0] = build_program()
    nc = _CACHED[0]
    common = {
        "W_qv": np.ascontiguousarray(inputs["L0_attn_W_qv"], np.float32),
        "W2_0": np.ascontiguousarray(inputs["L0_attn_W2_0"], np.float32),
        "W2_1": np.ascontiguousarray(inputs["L0_attn_W2_1"], np.float32),
        "SCW": np.ascontiguousarray(inputs["L0_attn_scale_weights"], np.float32).reshape(1, 2),
        "W_out": np.ascontiguousarray(inputs["L0_attn_W_out"], np.float32),
        "w1": np.ascontiguousarray(inputs["L0_ffn_w1"], np.float32),
        "b1": np.ascontiguousarray(inputs["L0_ffn_b1"], np.float32).reshape(1, HID),
        "w2": np.ascontiguousarray(inputs["L0_ffn_w2"], np.float32),
        "b2": np.ascontiguousarray(inputs["L0_ffn_b2"], np.float32).reshape(1, D),
        "gam": np.ascontiguousarray(inputs["L0_ffn_gamma"], np.float32).reshape(1, D),
        "bet": np.ascontiguousarray(inputs["L0_ffn_bet" + "a"], np.float32).reshape(1, D),
    }
    in_maps = []
    for c in range(8):
        b, h = c // 2, c % 2
        t0 = h * 4096
        xe = np.zeros((TIN, D), np.float32)
        lo, hi = t0 - P, t0 + 4096 + P
        s_, e_ = max(lo, 0), min(hi, TT)
        xe[s_ - lo:s_ - lo + (e_ - s_)] = xq[b, s_:e_]
        m = dict(common)
        m["x_in"] = xe
        in_maps.append(m)
    res = run_bass_kernel_spmd(nc, in_maps, core_ids=list(range(8)))
    out = np.empty((B, TT, DD), np.float32)
    for c in range(8):
        b, h = c // 2, c % 2
        out[b, h * 4096:(h + 1) * 4096] = res.results[c]["y"]
    return out


# version 26
# speedup vs baseline: 1.0144x; 1.0082x over previous
import contextlib
import numpy as np

import concourse.bass as bass
import concourse.mybir as mybir
from concourse.bass_utils import run_bass_kernel_spmd

F32 = mybir.dt.float32
BF16 = mybir.dt.bfloat16
FP8 = mybir.dt.float8e4
A = mybir.AluOpType
AF = mybir.ActivationFunctionType
AX = mybir.AxisListType

P = 128
D = 1024
KC = 8
NT = 32
NB = 34
TIN = NB * P
C0, C1, HALF = 21, 41, 20
NH, HD = 4, 256
HID = 4096
KH = 32
Z = 235
RR = C1 + Z
ROWP = 4 * RR + 1
SKG = 1024
SKEWN = SKG + 128 * ROWP + SKG
SKEW_TOT = ((2 * SKEWN + 4096 + 2047) // 2048) * 2048
K_CONST = 1.5 * 2.0 ** 82
EPS_PROBE = 2.0 ** -30
MAGIC_LO = 1.5 * 2.0 ** 23

ENGINES = ("sync", "scalar", "vector", "tensor", "gpsimd")
DMA_SEMS = ("DX1", "DX2", "DX3", "DW", "DSKW", "DSKR0", "DSKR1", "DSKR2",
            "DAW", "DAR0", "DAR1", "DAR2", "DAR3",
            "DQW0", "DQW1", "DQR0", "DQR1", "DQR2", "DQR3", "DP", "DPR", "DO")


class Sched:

    def __init__(self, nc, es):
        self.nc = nc
        self.es = es
        self.thunks = {e: [] for e in ENGINES}
        self.semh = {}
        self.semc = {}
        self.lastwait = {e: {} for e in ENGINES}
        self.lasttok = {e: None for e in ENGINES}
        self.wmap = {}
        self.rmap = {}
        self.barrier = None

    def sem(self, name):
        self.semh[name] = self.es.enter_context(self.nc.semaphore(name))
        self.semc[name] = 0

    def op(self, eng, fn, reads=(), writes=(), extra=(), sem=None, amt=1):
        if amt == 16 and sem is not None:
            writes = list(writes) + ["_stream_" + sem]
        waits = []
        for k in reads:
            waits.append(self.wmap.get(k))
        for k in writes:
            waits.append(self.wmap.get(k))
            waits.extend(self.rmap.get(k, ()))
        waits.extend(extra)
        if self.barrier is not None:
            waits.append(self.barrier)
        wl = []
        seen = set()
        for t in waits:
            if t is None or t in seen:
                continue
            seen.add(t)
            s, v = t
            assert v <= self.semc[s], (s, v, self.semc[s])
            wl.append((s, v))
        if sem is None:
            sem = {"sync": "SY", "scalar": "S", "vector": "V",
                   "tensor": "T", "gpsimd": "G"}[eng]
        self.semc[sem] += amt
        tok = (sem, self.semc[sem])
        lw = self.lastwait[eng]
        wl2 = [(s, v) for (s, v) in wl if lw.get(s, -1) < v]
        for s, v in wl:
            if lw.get(s, -1) < v:
                lw[s] = v
        semh = self.semh

        def thunk(e, wl2=wl2, fn=fn, sem=sem, amt=amt):
            for s, v in wl2:
                e.wait_ge(semh[s], v)
            if fn is not None:
                ins = fn(e)
                ins.then_inc(semh[sem], amt)
            else:
                e.nop().then_inc(semh[sem], amt)

        self.thunks[eng].append(thunk)
        for k in reads:
            self.rmap.setdefault(k, []).append(tok)
        for k in writes:
            self.wmap[k] = tok
            self.rmap[k] = []
        self.lasttok[eng] = tok
        return tok

    def wait_only(self, eng, extra):
        semh = self.semh
        wl = [t for t in extra if t is not None]

        def thunk(e, wl=wl):
            for s, v in wl:
                e.wait_ge(semh[s], v)

        self.thunks[eng].append(thunk)

    def full_barrier(self):
        dma_tots = [(s, self.semc[s]) for s in DMA_SEMS]
        for eng in ENGINES:
            ex = [self.lasttok[eng]]
            if eng == "sync":
                ex += dma_tots
            self.op(eng, None, extra=ex, sem="PB")
        self.barrier = ("PB", self.semc["PB"])


def build_program():
    nc = bass.Bass("TRN2", target_bir_lowering=False)
    es = contextlib.ExitStack()

    x_in = nc.dram_tensor("x_in", [TIN, D], F32, kind="ExternalInput")
    W_qv = nc.dram_tensor("W_qv", [D, 2 * D], F32, kind="ExternalInput")
    W2_0 = nc.dram_tensor("W2_0", [D, NH * C0], F32, kind="ExternalInput")
    W2_1 = nc.dram_tensor("W2_1", [D, NH * C1], F32, kind="ExternalInput")
    SCW = nc.dram_tensor("SCW", [1, 2], F32, kind="ExternalInput")
    W_out = nc.dram_tensor("W_out", [D, D], F32, kind="ExternalInput")
    w1 = nc.dram_tensor("w1", [HID, D], F32, kind="ExternalInput")
    b1 = nc.dram_tensor("b1", [1, HID], F32, kind="ExternalInput")
    w2 = nc.dram_tensor("w2", [D, HID], F32, kind="ExternalInput")
    b2 = nc.dram_tensor("b2", [1, D], F32, kind="ExternalInput")
    gam = nc.dram_tensor("gam", [1, D], F32, kind="ExternalInput")
    bet = nc.dram_tensor("bet", [1, D], F32, kind="ExternalInput")
    y = nc.dram_tensor("y", [NT * P, D], F32, kind="ExternalOutput")

    partial_d = nc.dram_tensor("partial_d", [NT * P * D], F32, kind="Internal")
    wq1t_d = nc.dram_tensor("wq1t_d", [KC * P * HID], FP8, kind="Internal")
    wq2t_d = nc.dram_tensor("wq2t_d", [KH * P * D], FP8, kind="Internal")
    skew_d = nc.dram_tensor("skew_d", [SKEW_TOT], BF16, kind="Internal")
    amix_d = nc.dram_tensor("amix_d", [2 * 4 * P * D], BF16, kind="Internal")
    xq1_d = nc.dram_tensor("xq1_d", [4 * 4 * P * D], BF16, kind="Internal")

    SB_BASE = 16512 + 2048
    SB_TOP = 16512 + 212863
    cursor = [SB_BASE]

    def salloc(name, shape, dtype, at=None):
        nb = mybir.dt.size(dtype)
        for s_ in shape[1:]:
            nb *= s_
        if at is None:
            at = cursor[0]
        else:
            at = at + SB_BASE
        t = nc.alloc_sbuf_tensor_at(name, shape, dtype, offset=at)
        assert at + nb <= SB_TOP, (name, at + nb, SB_TOP)
        cursor[0] = max(cursor[0], (at + nb + 127) // 128 * 128)
        return t

    ident = salloc("ident", [P, P], BF16)
    iot = salloc("iot", [P, P], F32)
    magic = salloc("magic", [P, 4], F32)
    cst = salloc("cst", [P, 4], F32)
    swb = salloc("swb", [P, 8], F32)
    onesrow = salloc("onesrow", [1, P], F32)
    onescol = salloc("onescol", [P, 1], F32)
    swrow = salloc("swrow", [1, 8], F32)
    b1row = salloc("b1row", [1, HID], F32)
    b2row = salloc("b2row", [1, D], F32)
    gamrow = salloc("gamrow", [1, D], F32)
    betrow = salloc("betrow", [1, D], F32)
    st_st = salloc("st_st", [P, 2 * NB], F32)
    st_r1 = salloc("st_r1", [P, NB], F32)
    st2_st = salloc("st2_st", [P, 2 * NT], F32)
    st_r2 = salloc("st_r2", [P, NT], F32)
    inv1t = salloc("inv1t", [P, NT], F32)
    bs = salloc("bs", [P, 24], F32)
    sc0 = salloc("sc0", [1, 8], F32)
    sc1 = salloc("sc1", [P, 8], F32)
    sc2 = salloc("sc2", [P, 8], F32)
    negmax = salloc("negmax", [P, 16], F32)
    ssum = salloc("ssum", [P, 16], F32)
    lnsm = salloc("lnsm", [P, 16], F32)
    bias2 = salloc("bias2", [P, 8], F32)
    e0b = salloc("e0b", [P, NH * C0], BF16)
    lnswb = salloc("lnswb", [P, 2], F32)
    RES_END = cursor[0]

    wqv_b = salloc("wqv_b", [P, KC * 2048], BF16)
    w2c_b = salloc("w2c_b", [P, KC * 248], BF16)
    wout_b = salloc("wout_b", [P, KC * D], BF16)
    AACT_BASE = cursor[0]
    xld = salloc("xld", [P, 3 * D], F32)
    xn32 = salloc("xn32", [P, 3 * D], F32)
    xnfm = salloc("xnfm", [P, KC * 8 * P], BF16)
    vtm = salloc("vtm", [P, 10 * D], BF16)
    rqfm = salloc("rqfm", [P, 2 * KC * 512], BF16)
    probs = salloc("probs", [P, 2 * 4 * RR], BF16)
    expt = salloc("expt", [P, 2 * 248], F32)
    pmrg = salloc("pmrg", [P, NH * C1], F32)
    stt = salloc("stt", [P, 3 * 3 * 512], BF16)
    amix_tm = salloc("amix_tm", [P, 2 * D], BF16)
    amix_fm = salloc("amix_fm", [P, KC * 512], BF16)
    x2 = salloc("x2", [P, 2 * D], F32)
    pout = salloc("pout", [P, 2 * D], F32)
    A_END = cursor[0]
    assert A_END <= SB_TOP, A_END

    AB = AACT_BASE - SB_BASE
    wq1asm = salloc("wq1asm", [P, KC * HID], FP8, at=AB)
    wq2asm = salloc("wq2asm", [P, KH * D], FP8, at=AB + 32 * 1024)
    wstage = salloc("wstage", [P, 2 * HID], F32, at=AB + 64 * 1024)
    wqt = salloc("wqt", [P, HID + D], F32, at=AB + 96 * 1024)
    zbf = salloc("zbf", [P, 2048], BF16, at=AB + 116 * 1024)

    ccur = [RES_END - SB_BASE]

    def calloc(name, shape, dtype):
        t = salloc(name, shape, dtype, at=ccur[0])
        nb = mybir.dt.size(dtype)
        for s_ in shape[1:]:
            nb *= s_
        ccur[0] = (ccur[0] + nb + 127) // 128 * 128
        return t

    wq1sb = calloc("wq1sb", [P, KC * HID], FP8)
    wq2sb = calloc("wq2sb", [P, KH * D], FP8)
    b1b = calloc("b1b", [P, HID], F32)
    b2b = calloc("b2b", [P, D], F32)
    gamb = calloc("gamb", [P, D], F32)
    betb = calloc("betb", [P, D], F32)
    xld3 = calloc("xld3", [P, 2 * D], F32)
    w3 = calloc("w3", [P, 2 * D], F32)
    xq1tm = calloc("xq1tm", [P, 2 * D], BF16)
    xq1fm = calloc("xq1fm", [P, 2 * KC * 512], BF16)
    y1 = calloc("y1", [P, HID], F32)
    xq2tm = calloc("xq2tm", [P, KH * P], BF16)
    xq2fm = calloc("xq2fm", [P, KH * P], BF16)
    ppart = calloc("ppart", [P, 2 * D], F32)
    t2s = calloc("t2s", [P, D], F32)
    C_END = ccur[0] + SB_BASE
    assert C_END <= SB_TOP, C_END

    psA = es.enter_context(nc.psum_tensor("psA", [P, 512], F32))
    psB = es.enter_context(nc.psum_tensor("psB", [P, 512], F32))
    ql = es.enter_context(nc.psum_tensor("ql", [P, 512], F32))
    bandA = es.enter_context(nc.psum_tensor("bandA", [P, 512], F32))
    bandB = es.enter_context(nc.psum_tensor("bandB", [P, 512], F32))
    woutA = es.enter_context(nc.psum_tensor("woutA", [P, 512], F32))
    woutB = es.enter_context(nc.psum_tensor("woutB", [P, 512], F32))
    ps_tx = es.enter_context(nc.psum_tensor("ps_tx", [P, 512], BF16))

    AB_PS = [(psA, "psA"), (psB, "psB")]
    WT_PS = [(woutA, "woutA"), (woutB, "woutB")]
    BD_PS = [(bandA, "bandA"), (bandB, "bandB")]
    F1_PS = AB_PS + [(ql, "ql")] + WT_PS

    sch = Sched(nc, es)
    for s_ in ("V", "S", "T", "G", "SY", "PB") + DMA_SEMS:
        sch.sem(s_)
    V, S, T, SY, G = "vector", "scalar", "tensor", "sync", "gpsimd"
    MAG = magic[:, 0:1]
    MAGN = magic[:, 3:4]

    sch.op(G, lambda e: e.iota(iot[:, :], [[1, P]], channel_multiplier=-1,
                               allow_small_or_imprecise_dtypes=True), writes=["iot"])
    sch.op(V, lambda e: e.tensor_scalar(ident[:, :], iot[:, :], 0.0, None, op0=A.is_equal),
           reads=["iot"], writes=["ident"])
    sch.op(V, lambda e: e.tensor_scalar(iot[:, :], iot[:, :], 0.0, None, op0=A.is_equal),
           reads=["iot"], writes=["iot"])
    sch.op(V, lambda e: e.memset(cst[:, 0:1], 1e-5), writes=["cst"])
    sch.op(V, lambda e: e.memset(magic[:, 1:2], 1.0), writes=["mg1"])
    sch.op(V, lambda e: e.tensor_scalar(magic[:, 2:3], magic[:, 1:2], EPS_PROBE, None, op0=A.add),
           reads=["mg1"], writes=["mg2"])
    sch.op(V, lambda e: e.tensor_scalar(magic[:, 1:2], magic[:, 2:3], 1.0, None, op0=A.subtract),
           reads=["mg2"], writes=["mg1"])
    sch.op(V, lambda e: e.tensor_scalar(magic[:, 0:1], magic[:, 1:2], K_CONST, MAGIC_LO,
                                        op0=A.mult, op1=A.add),
           reads=["mg1"], writes=["magic"])
    sch.op(V, lambda e: e.tensor_scalar(magic[:, 3:4], magic[:, 0:1], -1.0, None, op0=A.mult),
           reads=["magic"], writes=["magicn"])
    sch.op(V, lambda e: e.memset(zbf[:, :], 0.0), writes=["zbf"])
    sch.op(V, lambda e: e.memset(onesrow[:, :], 1.0), writes=["onesrow"])
    sch.op(V, lambda e: e.memset(onescol[:, :], 1.0), writes=["onescol"])

    done = 0
    while done < SKEW_TOT:
        n = min(P * 2048, SKEW_TOT - done)
        rows = (n + 2047) // 2048
        sch.op(SY, lambda e, off=done, rows=rows: e.dma_start(
            bass.AP(skew_d, off, [[2048, rows], [1, 2048]]), zbf[0:rows, :]),
            reads=["zbf"], writes=["d_skew0", "d_skew1"], sem="DSKW", amt=16)
        done += rows * 2048

    sch.op(SY, lambda e: e.dma_start(swrow[0:1, 0:2], SCW[:, :]), writes=["swrow"], sem="DP", amt=16)
    sch.op(V, lambda e: e.tensor_reduce(sc0[0:1, 0:1], swrow[0:1, 0:2], AX.X, A.max, negate=True),
           reads=["swrow"], writes=["sc0a"])
    sch.op(S, lambda e: e.activation(sc0[0:1, 2:4], swrow[0:1, 0:2], AF.Exp,
                                     bias=sc0[0:1, 0:1], scale=1.0),
           reads=["swrow", "sc0a"], writes=["sc0b"])
    sch.op(V, lambda e: e.tensor_reduce(sc0[0:1, 1:2], sc0[0:1, 2:4], AX.X, A.add),
           reads=["sc0b"], writes=["sc0c"])
    sch.op(V, lambda e: e.reciprocal(sc0[0:1, 4:5], sc0[0:1, 1:2]),
           reads=["sc0c"], writes=["sc0d"])
    sch.op(V, lambda e: e.tensor_scalar(sc0[0:1, 6:8], sc0[0:1, 2:4], sc0[0:1, 4:5], None, op0=A.mult),
           reads=["sc0b", "sc0d"], writes=["sc0e"])
    sch.op(T, lambda e: e.matmul(ql[:, 0:2], onesrow[0:1, :], sc0[0:1, 6:8],
                                 start=True, stop=True),
           reads=["sc0e", "onesrow"], writes=["ql"])
    sch.op(S, lambda e: e.copy(swb[:, 0:2], ql[:, 0:2]), reads=["ql"], writes=["swb01"])
    sch.op(S, lambda e: e.activation(lnswb[:, 0:2], swb[:, 0:2], AF.Ln, bias=0.0, scale=1.0),
           reads=["swb01"], writes=["lnswb"])

    p0s = ["DW", "DAW", "DQW0", "DAR0", "DAR1", "DAR2", "DAR3", "DQR1"]
    NP0 = len(p0s)

    sch.op(V, lambda e: e.memset(magic[:, 1:2], 0.0), reads=["mg2"], writes=["mg1"])
    sch.op(V, lambda e: e.memset(cst[:, 1:2], 0.0), writes=["c1"])

    def flat_tile(wsrc, t, fi, acc_sl, red_sl, ka, kr):
        so = (fi % 2) * HID
        ws = "wst%d" % (fi % 2)
        sch.op(SY, lambda e, t=t, so=so, wsrc=wsrc: e.dma_start(
            wstage[:, so:so + HID],
            bass.AP(wsrc, t * P * HID, [[HID, P], [1, HID]])),
            writes=[ws], sem=p0s[fi % NP0], amt=16)
        sch.op(V, lambda e, so=so, red_sl=red_sl: e.tensor_reduce(
            red_sl, wstage[:, so:so + HID], AX.X, A.add,
            apply_absolute_value=True),
            reads=[ws], writes=[kr])
        sch.op(V, lambda e, acc_sl=acc_sl, red_sl=red_sl: e.tensor_add(
            acc_sl, acc_sl, red_sl),
            reads=[kr, ka], writes=[ka])

    fi = 0
    for t in range(KC):
        flat_tile(w1, t, fi, magic[:, 1:2], magic[:, 2:3], "mg1", "mg2"); fi += 1
        flat_tile(w2, t, fi, cst[:, 1:2], cst[:, 2:3], "c1", "c2"); fi += 1

    def scale_chain(acc_sl, ka, nel, swcol, swkey):
        sch.op(T, lambda e, acc_sl=acc_sl: e.matmul(
            ql[0:1, 4:5], acc_sl, onescol[:, 0:1], start=True, stop=True),
            reads=[ka, "onescol"], writes=["ql"])
        sch.op(S, lambda e: e.copy(sc0[0:1, 5:6], ql[0:1, 4:5]), reads=["ql"], writes=["sc0f"])
        sch.op(V, lambda e, nel=nel: e.tensor_scalar(sc0[0:1, 5:6], sc0[0:1, 5:6],
                                                     1.0 / nel, 1e-5,
                                                     op0=A.mult, op1=A.max),
               reads=["sc0f"], writes=["sc0f"])
        sch.op(V, lambda e: e.reciprocal(sc0[0:1, 5:6], sc0[0:1, 5:6]),
               reads=["sc0f"], writes=["sc0f"])
        sch.op(T, lambda e: e.matmul(ql[:, 5:6], onesrow[0:1, :], sc0[0:1, 5:6],
                                     start=True, stop=True),
               reads=["sc0f", "onesrow"], writes=["ql"])
        sch.op(S, lambda e, swcol=swcol: e.copy(swb[:, swcol:swcol + 1], ql[:, 5:6]),
               reads=["ql"], writes=[swkey])

    scale_chain(magic[:, 1:2], "mg1", float(KH * P * D), 2, "sw1q")
    scale_chain(cst[:, 1:2], "c1", float(KC * P * HID), 3, "sw2q")
    sch.op(V, lambda e: e.tensor_scalar(magic[:, 1:2], magic[:, 0:1], 1.0, None,
                                        op0=A.add),
           reads=["magic"], writes=["mg1", "mp1"])
    sch.op(V, lambda e: e.tensor_scalar(magic[:, 2:3], magic[:, 0:1], 1.0, None,
                                        op0=A.subtract),
           reads=["magic"], writes=["mg2", "mm1"])

    def ld_cast(dram, cols, dst, dstoff, kidx, wkey, slot):
        so = slot * HID
        sch.op(SY, lambda e: e.dma_start(wstage[:, so:so + cols], dram[kidx * P:(kidx + 1) * P, :]),
               writes=["wst%d" % slot], sem=p0s[(kidx + slot) % NP0], amt=16)
        sch.op(V, lambda e: e.tensor_copy(dst[:, dstoff:dstoff + cols], wstage[:, so:so + cols]),
               reads=["wst%d" % slot], writes=[wkey])

    sl = 0
    for k in range(KC):
        ld_cast(W_qv, 2048, wqv_b, k * 2048, k, "wqv", sl % 2); sl += 1
    for k in range(KC):
        ld_cast(W_out, D, wout_b, k * D, k, "wout", sl % 2); sl += 1
    for k in range(KC):
        ld_cast(W2_0, NH * C0, w2c_b, k * 248, k, "w2c", sl % 2); sl += 1
        ld_cast(W2_1, NH * C1, w2c_b, k * 248 + NH * C0, k, "w2c", sl % 2); sl += 1

    sch.op(SY, lambda e: e.dma_start(b1row[:, :], b1[:, :]), writes=["b1row"], sem="DP", amt=16)
    sch.op(SY, lambda e: e.dma_start(b2row[:, :], b2[:, :]), writes=["b2row"], sem="DP", amt=16)
    sch.op(SY, lambda e: e.dma_start(gamrow[:, :], gam[:, :]), writes=["gamrow"], sem="DP", amt=16)
    sch.op(SY, lambda e: e.dma_start(betrow[:, :], bet[:, :]), writes=["betrow"], sem="DP", amt=16)


    pend = []
    evct = [0]

    def flush_evacs(keep):
        while len(pend) > keep:
            fn, kpst, asmkey = pend.pop(0)
            if evct[0] % 2 == 0:
                sch.op(S, lambda e, fn=fn: fn(e, 0), reads=[kpst, "magicn"],
                       writes=[asmkey])
            else:
                sch.op(V, lambda e, fn=fn: fn(e, 1), reads=[kpst, "magic"],
                       writes=[asmkey])
            evct[0] += 1

    def quant_tile(wsrc, t, ti, cols, swcol, swkey, asmdst, asmkey, asm_k_step,
                   wqo, wqk, stg, stgname):
        so = (ti % 2) * HID
        ws = "%s%d" % (stgname, ti % 2)
        sch.op(SY, lambda e, t=t, so=so, cols=cols, wsrc=wsrc, stg=stg: e.dma_start(
            stg[:, so:so + cols], wsrc[t * P:(t + 1) * P, :]),
            writes=[ws], sem=p0s[ti % NP0], amt=16)
        for c in range(cols // 512):
            co = c * 512
            sch.op(S, lambda e, so=so, co=co, swcol=swcol, stg=stg: e.activation(
                stg[:, so + co:so + co + 512], stg[:, so + co:so + co + 512],
                AF.Identity, bias=MAG, scale=swb[:, swcol:swcol + 1]),
                reads=[ws, swkey, "magic"], writes=[ws])
            wqkc = "%s_%d" % (wqk, c % 4)
            sch.op(V, lambda e, so=so, co=co, wqo=wqo, stg=stg: e.tensor_scalar(
                wqt[:, wqo + co:wqo + co + 512], stg[:, so + co:so + co + 512],
                magic[:, 1:2], magic[:, 2:3], op0=A.min, op1=A.max),
                reads=[ws, "mp1", "mm1"], writes=[wqkc])
            pst, kpst = PH0_PS[(ti * 8 + c) % 6]
            for i in range(4):
                kb = c * 4 + i
                sch.op(T, lambda e, kb=kb, i=i, wqo=wqo, pst=pst: e.transpose(
                    pst[:, i * P:(i + 1) * P],
                    wqt[:, wqo + kb * P:wqo + (kb + 1) * P], iot[:, :]),
                    reads=[wqkc, "iot"], writes=[kpst])

            def evac(e, eng, t=t, c=c, asmdst=asmdst, asm_k_step=asm_k_step,
                     pst=pst):
                dst = bass.AP(asmdst, c * 4 * asm_k_step + t * P,
                              [[asmdst.shape[1], P], [asm_k_step, 4], [1, P]])
                if eng == 0:
                    return e.activation(dst, pst[:, 0:512], AF.Identity,
                                        bias=MAGN, scale=1.0)
                return e.tensor_scalar(dst, pst[:, 0:512], MAG, None,
                                       op0=A.subtract)
            pend.append((evac, kpst, asmkey))
            flush_evacs(2)

    PH0_PS = AB_PS + BD_PS + WT_PS
    ti = 0
    for g in range(KC):
        for i in range(4):
            quant_tile(w1, 4 * g + i, ti, D, 2, "sw1q", wq1asm, "wq1asm", HID,
                       HID, "wqt1", wstage, "wst"); ti += 1
        quant_tile(w2, g, ti, HID, 3, "sw2q", wq2asm, "wq2asm", D,
                   0, "wqt2", wstage, "wst"); ti += 1

    flush_evacs(0)
    sch.op(SY, lambda e: e.dma_start(bass.AP(wq1t_d, 0, [[KC * HID, P], [1, KC * HID]]),
                                     wq1asm[:, :]),
           reads=["wq1asm"], writes=["d_wq1"], sem="DW", amt=16)
    sch.op(SY, lambda e: e.dma_start(bass.AP(wq2t_d, 0, [[KH * D, P], [1, KH * D]]),
                                     wq2asm[:, :]),
           reads=["wq2asm"], writes=["d_wq2"], sem="DW", amt=16)

    def vload(eidx):
        t = eidx + 1
        par = t % 3
        sch.op(G, lambda e, t=t, par=par: e.dma_start(
            xld[:, par * D:(par + 1) * D], x_in[t * P:(t + 1) * P, :]),
            reads=["d_wq1", "d_wq2"],
            writes=["xld%d" % par], sem="DX1", amt=16)

    def vpath_ln(eidx):
        t = eidx + 1
        par = t % 3
        kx = "xld%d" % par
        kn = "xn32%d" % (t % 3)
        kb2 = "bs%d" % (t % 2)
        bso = (t % 2) * 12
        sch.op(V, lambda e: e.bn_stats(bs[:, bso:bso + 6], xld[:, par * D:par * D + 512]),
               reads=[kx], writes=[kb2])
        sch.op(V, lambda e: e.bn_stats(bs[:, bso + 6:bso + 12], xld[:, par * D + 512:par * D + D]),
               reads=[kx], writes=[kb2])
        sch.op(V, lambda e: e.bn_aggr(st_st[:, 2 * t:2 * t + 2], bs[:, bso:bso + 12]),
               reads=[kb2], writes=["st%d" % t])
        sch.op(S, lambda e: e.activation(st_r1[:, t:t + 1], st_st[:, 2 * t + 1:2 * t + 2],
                                         AF.Ln, bias=cst[:, 0:1], scale=1.0),
               reads=["st%d" % t, "cst"], writes=["str%d" % t])
        sch.op(S, lambda e: e.activation(st_r1[:, t:t + 1], st_r1[:, t:t + 1],
                                         AF.Exp, bias=0.0, scale=-0.5),
               reads=["str%d" % t], writes=["str%d" % t])
        sch.op(V, lambda e: e.tensor_scalar(xn32[:, (t % 3) * D:(t % 3 + 1) * D],
                                            xld[:, par * D:(par + 1) * D],
                                            st_st[:, 2 * t:2 * t + 1], st_r1[:, t:t + 1],
                                            op0=A.subtract, op1=A.mult),
               reads=[kx, "st%d" % t, "str%d" % t], writes=[kn])
        if 0 <= eidx < NT:
            s = eidx
            sch.op(V, lambda e: e.bn_stats(bs[:, bso:bso + 6],
                                           xn32[:, (t % 3) * D:(t % 3) * D + 512]),
                   reads=[kn], writes=[kb2])
            sch.op(V, lambda e: e.bn_stats(bs[:, bso + 6:bso + 12],
                                           xn32[:, (t % 3) * D + 512:(t % 3) * D + D]),
                   reads=[kn], writes=[kb2])
            sch.op(V, lambda e: e.bn_aggr(st2_st[:, 2 * s:2 * s + 2], bs[:, bso:bso + 12]),
                   reads=[kb2], writes=["st2_%d" % s])
            sch.op(S, lambda e: e.activation(st_r2[:, s:s + 1], st2_st[:, 2 * s + 1:2 * s + 2],
                                             AF.Ln, bias=cst[:, 0:1], scale=1.0),
                   reads=["st2_%d" % s, "cst"], writes=["st2r%d" % s])
            sch.op(S, lambda e: e.activation(st_r2[:, s:s + 1], st_r2[:, s:s + 1],
                                             AF.Exp, bias=0.0, scale=-0.5),
                   reads=["st2r%d" % s], writes=["st2r%d" % s])

    def vpath_tv(eidx):
        t = eidx + 1
        slot = eidx % 8
        slot9 = eidx % 10
        kn = "xn32%d" % (t % 3)
        kf = "xnfm%d" % slot
        kv = "vtm%d" % slot9
        for r0 in (0, 4):
            pst, kpst = AB_PS[(eidx * 3 + r0 // 4) % 2]
            for i in range(4):
                kb = r0 + i
                sch.op(T, lambda e, kb=kb, i=i, pst=pst, t=t: e.transpose(
                    pst[:, i * P:(i + 1) * P],
                    xn32[:, (t % 3) * D + kb * P:(t % 3) * D + (kb + 1) * P], iot[:, :]),
                    reads=[kn, "iot"], writes=[kpst])

            def evacx(e, r0=r0, slot=slot, pst=pst):
                dst = bass.AP(xnfm, r0 * (8 * P) + slot * P,
                              [[KC * 8 * P, P], [8 * P, 4], [1, P]])
                return e.tensor_copy(dst, pst[:, 0:512])
            sch.op(V, evacx, reads=[kpst], writes=[kf])
        for n in range(2):
            pst, kpst = AB_PS[(eidx * 3 + 2 + n) % 2]

            def vmm(e, slot=slot, n=n, pst=pst):
                ins = None
                for k in range(KC):
                    ins = e.matmul(
                        pst[:, 0:512],
                        xnfm[:, k * (8 * P) + slot * P:k * (8 * P) + (slot + 1) * P],
                        wqv_b[:, k * 2048 + D + n * 512:k * 2048 + D + (n + 1) * 512],
                        start=(k == 0), stop=(k == KC - 1))
                return ins
            sch.op(T, vmm, reads=[kf, "wqv"], writes=[kpst])
            sch.op(S, lambda e, slot9=slot9, n=n, pst=pst: e.copy(
                vtm[:, slot9 * D + n * 512:slot9 * D + (n + 1) * 512], pst[:, 0:512]),
                reads=[kpst], writes=[kv])

    def qpath(quad):
        qpar = quad % 2
        slot0 = (4 * quad) % 8
        kfs = ["xnfm%d" % ((4 * quad + i) % 8) for i in range(4)]
        for m in range(KC):
            pst, kpst = WT_PS[m % 2]

            def qmm(e, m=m, slot0=slot0, pst=pst):
                ins = None
                for k in range(KC):
                    ins = e.matmul(
                        pst[:, 0:512],
                        wqv_b[:, k * 2048 + m * P:k * 2048 + (m + 1) * P],
                        xnfm[:, k * (8 * P) + slot0 * P:k * (8 * P) + slot0 * P + 512],
                        start=(k == 0), stop=(k == KC - 1))
                return ins
            sch.op(T, qmm, reads=kfs + ["wqv"], writes=[kpst])
            sch.op(V, lambda e, m=m, qpar=qpar, pst=pst: e.tensor_scalar(
                rqfm[:, qpar * 4096 + m * 512:qpar * 4096 + (m + 1) * 512],
                pst[:, 0:512], 0.0, None, op0=A.max),
                reads=[kpst], writes=["rqfm%d" % qpar])

    def attn1a(s):
        par = s % 2
        quad = s // 4
        qpar = quad % 2
        eo = par * 248

        def lmm(e, s=s, qpar=qpar):
            ins = None
            for k in range(KC):
                ins = e.matmul(
                    ql[:, 0:248],
                    rqfm[:, qpar * 4096 + k * 512 + (s % 4) * P:
                         qpar * 4096 + k * 512 + (s % 4 + 1) * P],
                    w2c_b[:, k * 248:(k + 1) * 248],
                    start=(k == 0), stop=(k == KC - 1))
            return ins
        sch.op(T, lmm, reads=["rqfm%d" % qpar, "w2c"], writes=["ql"])
        sch.op(V, lambda e, eo=eo: e.tensor_copy(expt[:, eo:eo + 248], ql[:, 0:248]),
               reads=["ql"], writes=["expt%d" % par])
        sch.op(V, lambda e, eo=eo, par=par: e.tensor_reduce(
            bass.AP(negmax, par * 8, [[16, P], [1, 4]]),
            expt[:, eo:eo + 84].rearrange("p (h c) -> p h c", c=C0),
            AX.X, A.max, negate=True),
            reads=["expt%d" % par], writes=["negmax0_%d" % par])
        sch.op(V, lambda e, eo=eo, par=par: e.tensor_reduce(
            bass.AP(negmax, par * 8 + 4, [[16, P], [1, 4]]),
            expt[:, eo + 84:eo + 248].rearrange("p (h c) -> p h c", c=C1),
            AX.X, A.max, negate=True),
            reads=["expt%d" % par], writes=["negmax1_%d" % par])

    def attn1b1(s):
        par = s % 2
        eo = par * 248
        no = par * 8
        groups = [(h, 0, h * C0, C0) for h in range(NH)] + \
                 [(h, 1, NH * C0 + h * C1, C1) for h in range(NH)]
        for gi, (h, ci, c0, cw) in enumerate(groups):
            sch.op(S, lambda e, gi=gi, c0=c0, cw=cw, eo=eo, no=no: e.activation(
                pmrg[:, 0:cw], expt[:, eo + c0:eo + c0 + cw], AF.Exp,
                bias=negmax[:, no + gi:no + gi + 1], scale=1.0,
                accum_out=ssum[:, no + gi:no + gi + 1]),
                reads=["expt%d" % par, "negmax0_%d" % par, "negmax1_%d" % par],
                writes=["pmgj", "ssum%d_%d" % (par, gi)])
        sch.op(S, lambda e, no=no: e.activation(lnsm[:, no:no + 8], ssum[:, no:no + 8],
                                                AF.Ln, bias=0.0, scale=1.0),
               reads=["ssum%d_%d" % (par, g_) for g_ in range(8)], writes=["lnsm%d" % par])

    def attn1b2(s):
        par = s % 2
        kp = "probs%d" % par
        kst = "stt%d" % (s % 3)
        eo = par * 248
        no = par * 8
        sch.op(V, lambda e, no=no: e.tensor_sub(bias2[:, 0:8], negmax[:, no:no + 8], lnsm[:, no:no + 8]),
               reads=["negmax0_%d" % par, "negmax1_%d" % par, "lnsm%d" % par], writes=["bias2"])
        sch.op(V, lambda e: e.tensor_scalar(bias2[:, 0:4], bias2[:, 0:4], lnswb[:, 0:1], None, op0=A.add),
               reads=["lnswb"], writes=["bias2"])
        sch.op(V, lambda e: e.tensor_scalar(bias2[:, 4:8], bias2[:, 4:8], lnswb[:, 1:2], None, op0=A.add),
               reads=["lnswb"], writes=["bias2"])
        for h in range(NH):
            sch.op(S, lambda e, h=h, par=par, eo=eo: e.activation(
                probs[:, par * (4 * RR) + h * RR:par * (4 * RR) + h * RR + C1],
                expt[:, eo + 84 + h * C1:eo + 84 + (h + 1) * C1], AF.Exp,
                bias=bias2[:, 4 + h:5 + h], scale=1.0),
                reads=["expt%d" % par, "bias2"], writes=[kp])
            sch.op(S, lambda e, h=h, eo=eo: e.activation(
                e0b[:, h * C0:(h + 1) * C0],
                expt[:, eo + h * C0:eo + (h + 1) * C0], AF.Exp,
                bias=bias2[:, h:h + 1], scale=1.0),
                reads=["expt%d" % par, "bias2"], writes=["e0b"])
        for h in range(NH):
            sch.op(V, lambda e, h=h, par=par: e.tensor_add(
                probs[:, par * (4 * RR) + h * RR + 10:par * (4 * RR) + h * RR + 31],
                probs[:, par * (4 * RR) + h * RR + 10:par * (4 * RR) + h * RR + 31],
                e0b[:, h * C0:(h + 1) * C0]),
                reads=["e0b"], writes=[kp])
        sch.op(SY, lambda e, par=par: e.dma_start(
            bass.AP(skew_d, par * SKEWN + SKG, [[ROWP, P], [1, 4 * RR]]),
            probs[:, par * (4 * RR):(par + 1) * (4 * RR)]),
            reads=[kp], writes=["d_skew%d" % par], sem="DSKW", amt=16)
        for g in range(3):
            sch.op(SY, lambda e, g=g, par=par, s=s: e.dma_start(
                stt[:, ((s % 3) * 3 + g) * 512:((s % 3) * 3 + g + 1) * 512],
                bass.AP(skew_d, par * SKEWN + SKG + g * P - 108, [[RR, 512], [1, P]]),
                transpose=True),
                reads=["d_skew%d" % par], writes=["%s_%d" % (kst, g)],
                sem="DSKR%d" % g, amt=16)

    def attn2(s):
        par = s % 2
        kst = ["stt%d_%d" % (s % 3, g) for g in range(3)]
        vkeys = ["vtm%d" % ((s - 1 + g) % 10) for g in range(3)]
        for half in range(2):
            pst, kpst = BD_PS[(s * 2 + half) % 2]

            def bmm(e, s=s, half=half, pst=pst):
                ins = None
                for h in (half * 2, half * 2 + 1):
                    for g in range(3):
                        vslot = (s - 1 + g) % 10
                        ins = e.matmul(
                            pst[:, (h - half * 2) * HD:(h - half * 2 + 1) * HD],
                            bass.AP(stt, ((s % 3) * 3 + g) * 512 + h, [[3 * 3 * 512, P], [4, P]]),
                            vtm[:, vslot * D + h * HD:vslot * D + (h + 1) * HD],
                            start=(g == 0), stop=(g == 2))
                return ins
            sch.op(T, bmm, reads=kst + vkeys, writes=[kpst])
            sch.op(V, lambda e, par=par, half=half, pst=pst: e.tensor_copy(
                amix_tm[:, par * D + half * 512:par * D + (half + 1) * 512], pst[:, 0:512]),
                reads=[kpst], writes=["amix%d" % par])
        sch.op(SY, lambda e, s=s, par=par: e.dma_start(
            bass.AP(amix_d, (s // 4 % 2) * 4 * P * D + (s % 4) * P * D, [[D, P], [1, D]]),
            amix_tm[:, par * D:(par + 1) * D]),
            reads=["amix%d" % par], writes=["d_amix%d" % (s // 4 % 2)], sem="DAW", amt=16)

    def wout_read(quad):
        qpar = quad % 2
        for k in range(KC):
            sch.op(SY, lambda e, k=k, qpar=qpar: e.dma_start(
                amix_fm[:, k * 512:(k + 1) * 512],
                bass.AP(amix_d, qpar * 4 * P * D + k * P, [[D, 512], [1, P]]),
                transpose=True),
                reads=["d_amix%d" % qpar], writes=["amixfm%d" % k],
                sem="DAR%d" % (k % 4), amt=16)

    def wout_mm(quad):
        amixkeys = ["amixfm%d" % k for k in range(KC)]
        for i in range(4):
            s = 4 * quad + i
            par = s % 2
            sch.op(G, lambda e, s=s, par=par: e.dma_start(
                x2[:, par * D:(par + 1) * D], x_in[(s + 1) * P:(s + 2) * P, :]),
                writes=["x2%d" % par], sem="DX2", amt=16)
            for n in range(2):
                pst, kpst = WT_PS[(i * 2 + n) % 2]

                def omm(e, i=i, n=n, pst=pst):
                    ins = None
                    for k in range(KC):
                        ins = e.matmul(
                            pst[:, 0:512],
                            amix_fm[:, k * 512 + i * P:k * 512 + (i + 1) * P],
                            wout_b[:, k * D + n * 512:k * D + (n + 1) * 512],
                            start=(k == 0), stop=(k == KC - 1))
                    return ins
                sch.op(T, omm, reads=amixkeys + ["wout"], writes=[kpst])
                sch.op(V, lambda e, par=par, n=n, pst=pst: e.scalar_tensor_tensor(
                    pout[:, par * D + n * 512:par * D + (n + 1) * 512], pst[:, 0:512], 1.0,
                    x2[:, par * D + n * 512:par * D + (n + 1) * 512], op0=A.mult, op1=A.add),
                    reads=[kpst, "x2%d" % par], writes=["pout%d" % par])
            sch.op(SY, lambda e, s=s, par=par: e.dma_start(
                bass.AP(partial_d, s * P * D, [[D, P], [1, D]]),
                pout[:, par * D:(par + 1) * D]),
                reads=["pout%d" % par], writes=["d_part%d" % s], sem="DP", amt=16)

    for e_ in (-1, 0, 1):
        vload(e_)
    for kind, e_ in (("ln", -1), ("vl", 2), ("ln", 0), ("vl", 3), ("ln", 1),
                     ("vl", 4), ("tv", -1), ("ln", 2), ("vl", 5), ("tv", 0),
                     ("ln", 3), ("vl", 6), ("tv", 1), ("ln", 4), ("vl", 7),
                     ("tv", 2), ("ln", 5), ("tv", 3), ("tv", 4)):
        {"ln": vpath_ln, "tv": vpath_tv, "vl": vload}[kind](e_)
    sch.full_barrier()
    sch.op(V, lambda e: e.memset(probs[:, :], 0.0), writes=["probs0", "probs1"])

    for s in range(NT + 7):
        if s + 8 <= NB - 2:
            vload(s + 8)
        if 0 <= s - 4 < NT:
            attn2(s - 4)
        if s + 5 <= NB - 2:
            vpath_tv(s + 5)
        if s + 6 <= NB - 2:
            vpath_ln(s + 6)
        if (s - 7) % 4 == 0 and 0 <= s - 7 < NT:
            wout_read((s - 7) // 4)
        if s % 4 == 0 and s < NT:
            qpath(s // 4)
        if 0 <= s - 2 < NT:
            attn1b2(s - 2)
        if s < NT:
            attn1a(s)
        if 0 <= s - 1 < NT:
            attn1b1(s - 1)
        if (s - 10) % 4 == 0 and 0 <= s - 10 < NT:
            wout_mm((s - 10) // 4)

    sch.full_barrier()

    for j in range(4):
        w = KC * HID // 4
        sch.op(SY, lambda e, j=j, w=w: e.dma_start(
            wq1sb[:, j * w:(j + 1) * w],
            bass.AP(wq1t_d, j * w, [[KC * HID, P], [1, w]])),
            reads=["d_wq1"], writes=["wq1sb_%d" % j], sem=p0s[j], amt=16)
        sch.op(SY, lambda e, j=j, w=w: e.dma_start(
            wq2sb[:, j * w:(j + 1) * w],
            bass.AP(wq2t_d, j * w, [[KH * D, P], [1, w]])),
            reads=["d_wq2"], writes=["wq2sb_%d" % j], sem=p0s[4 + j], amt=16)
    bi = 0
    for (row, rkey, dst, dkey, n) in ((gamrow, "gamrow", gamb, "gamb", D),
                                      (betrow, "betrow", betb, "betb", D),
                                      (b1row, "b1row", b1b, "b1b", HID),
                                      (b2row, "b2row", b2b, "b2b", D)):
        for c in range(0, n, 512):
            pst, kpst = WT_PS[bi % 2]
            bi += 1
            sch.op(T, lambda e, row=row, c=c, pst=pst: e.matmul(
                pst[:, 0:512], onesrow[0:1, :], row[0:1, c:c + 512],
                start=True, stop=True),
                reads=[rkey, "onesrow"], writes=[kpst])
            sch.op(S, lambda e, dst=dst, c=c, pst=pst: e.copy(dst[:, c:c + 512], pst[:, 0:512]),
                   reads=[kpst], writes=[dkey])

    def qload(s):
        par = s % 2
        tb = s + 1
        sch.op(G, lambda e, tb=tb, par=par: e.dma_start(
            xld3[:, par * D:(par + 1) * D], x_in[tb * P:(tb + 1) * P, :]),
            writes=["xld3%d" % par], sem="DX3", amt=16)

    def quantize1(s):
        par = s % 2
        tb = s + 1
        kx, kw, kq = "xld3%d" % par, "w3%d" % par, "xq1tm%d" % par
        sch.op(V, lambda e: e.tensor_scalar(
            w3[:, par * D:(par + 1) * D], xld3[:, par * D:(par + 1) * D],
            st_st[:, 2 * tb:2 * tb + 1], st_r1[:, tb:tb + 1], op0=A.subtract, op1=A.mult),
            reads=[kx, "st%d" % tb, "str%d" % tb], writes=[kw])
        sch.op(V, lambda e: e.tensor_scalar(
            w3[:, par * D:(par + 1) * D], w3[:, par * D:(par + 1) * D],
            st2_st[:, 2 * s:2 * s + 1], st_r2[:, s:s + 1], op0=A.subtract, op1=A.mult),
            reads=[kw, "st2_%d" % s, "st2r%d" % s], writes=[kw])
        sch.op(V, lambda e: e.tensor_mul(w3[:, par * D:(par + 1) * D],
                                         w3[:, par * D:(par + 1) * D], gamb[:, :]),
               reads=[kw, "gamb"], writes=[kw])
        sch.op(V, lambda e: e.tensor_add(w3[:, par * D:(par + 1) * D],
                                         w3[:, par * D:(par + 1) * D], betb[:, :]),
               reads=[kw, "betb"], writes=[kw])
        sch.op(V, lambda e: e.tensor_reduce(sc1[:, 0:1], w3[:, par * D:(par + 1) * D],
                                            AX.X, A.max, apply_absolute_value=True),
               reads=[kw], writes=["sc1a"])
        sch.op(V, lambda e: e.tensor_scalar(sc1[:, 1:2], sc1[:, 0:1], 1e-5, None, op0=A.max),
               reads=["sc1a"], writes=["sc1b"])
        sch.op(V, lambda e: e.reciprocal(sc1[:, 2:3], sc1[:, 1:2]),
               reads=["sc1b"], writes=["sc1c"])
        sch.op(V, lambda e: e.tensor_scalar(sc1[:, 3:4], sc1[:, 2:3], 127.0, None, op0=A.mult),
               reads=["sc1c"], writes=["sc1d"])
        sch.op(V, lambda e: e.tensor_scalar(sc1[:, 4:5], sc1[:, 3:4], swb[:, 2:3], None, op0=A.mult),
               reads=["sc1d", "sw1q"], writes=["sc1e"])
        sch.op(V, lambda e: e.reciprocal(inv1t[:, s:s + 1], sc1[:, 4:5]),
               reads=["sc1e"], writes=["inv1_%d" % s])
        sch.op(S, lambda e: e.activation(
            w3[:, par * D:(par + 1) * D], w3[:, par * D:(par + 1) * D], AF.Identity,
            bias=MAG, scale=sc1[:, 3:4]),
            reads=[kw, "sc1d", "magic"], writes=[kw])
        sch.op(S, lambda e: e.activation(
            xq1tm[:, par * D:(par + 1) * D], w3[:, par * D:(par + 1) * D], AF.Identity,
            bias=MAGN, scale=1.0),
            reads=[kw, "magicn"], writes=[kq])
        sch.op(SY, lambda e, s=s: e.dma_start(
            bass.AP(xq1_d, (s // 4 % 4) * 4 * P * D + (s % 4) * P * D, [[D, P], [1, D]]),
            xq1tm[:, par * D:(par + 1) * D]),
            reads=[kq], writes=["d_xq1_%d" % (s % 16)], sem="DQW%d" % (s % 2), amt=16)

    def xq1_read(quad):
        qpar = quad % 2
        for k in range(KC):
            sch.op(SY, lambda e, k=k, quad=quad, qpar=qpar: e.dma_start(
                xq1fm[:, qpar * 4096 + k * 512:qpar * 4096 + (k + 1) * 512],
                bass.AP(xq1_d, (quad % 4) * 4 * P * D + k * P, [[D, 512], [1, P]]),
                transpose=True),
                reads=["d_xq1_%d" % ((4 * quad + i_) % 16) for i_ in range(4)],
                writes=["xq1fm%d_%d" % (qpar, k)],
                sem="DQR%d" % (k % 4), amt=16)

    def ffn_f1(s):
        quad = s // 4
        qpar = quad % 2
        xq1keys = ["xq1fm%d_%d" % (qpar, k) for k in range(KC)]
        for c in range(8):
            pst, kpst = F1_PS[(s * 8 + c) % 5]

            def fmm(e, c=c, s=s, qpar=qpar, pst=pst):
                ins = None
                for k in range(KC):
                    ins = e.matmul(
                        pst[:, 0:512],
                        xq1fm[:, qpar * 4096 + k * 512 + (s % 4) * P:
                              qpar * 4096 + k * 512 + (s % 4 + 1) * P],
                        wq1sb[:, k * HID + c * 512:k * HID + (c + 1) * 512],
                        start=(k == 0), stop=(k == KC - 1))
                return ins
            sch.op(T, fmm, reads=xq1keys + ["wq1sb_%d" % j_ for j_ in range(4)], writes=[kpst])
            sch.op(V, lambda e, c=c, s=s, pst=pst, kpst=kpst: e.scalar_tensor_tensor(
                y1[:, c * 512:(c + 1) * 512], pst[:, 0:512], inv1t[:, s:s + 1],
                b1b[:, c * 512:(c + 1) * 512], op0=A.mult, op1=A.add),
                reads=[kpst, "inv1_%d" % s, "b1b"], writes=["y1h%d" % (c // 4)])

    def ffn_f1gelu(s):
        sch.op(S, lambda e: e.activation(y1[:, 0:2048], y1[:, 0:2048], AF.Gelu,
                                         bias=0.0, scale=1.0),
               reads=["y1h0"], writes=["y1h0"])
        sch.op(S, lambda e: e.activation(y1[:, 2048:HID], y1[:, 2048:HID], AF.Gelu,
                                         bias=0.0, scale=1.0),
               reads=["y1h1"], writes=["y1h1"])

    def ffn_f2head(s):
        par = s % 2
        sch.op(SY, lambda e, s=s, par=par: e.dma_start(
            ppart[:, par * D:(par + 1) * D],
            bass.AP(partial_d, s * P * D, [[D, P], [1, D]])),
            reads=["d_part%d" % s], writes=["ppart%d" % par], sem="DPR", amt=16)
        sch.op(V, lambda e: e.tensor_reduce(sc2[:, 6:7], y1[:, 0:2048], AX.X, A.max,
                                            apply_absolute_value=True),
               reads=["y1h0"], writes=["sc2m0"])
        sch.op(V, lambda e: e.tensor_reduce(sc2[:, 7:8], y1[:, 2048:HID], AX.X, A.max,
                                            apply_absolute_value=True),
               reads=["y1h1"], writes=["sc2m1"])
        sch.op(V, lambda e: e.tensor_scalar(sc2[:, 0:1], sc2[:, 6:7], sc2[:, 7:8],
                                            None, op0=A.max),
               reads=["sc2m0", "sc2m1"], writes=["sc2a"])
        sch.op(V, lambda e: e.tensor_scalar(sc2[:, 1:2], sc2[:, 0:1], 1e-5, None, op0=A.max),
               reads=["sc2a"], writes=["sc2b"])
        sch.op(V, lambda e: e.reciprocal(sc2[:, 2:3], sc2[:, 1:2]),
               reads=["sc2b"], writes=["sc2c"])
        sch.op(V, lambda e: e.tensor_scalar(sc2[:, 3:4], sc2[:, 2:3], 127.0, None, op0=A.mult),
               reads=["sc2c"], writes=["sc2d"])
        sch.op(V, lambda e: e.tensor_scalar(sc2[:, 4:5], sc2[:, 3:4], swb[:, 3:4], None, op0=A.mult),
               reads=["sc2d", "sw2q"], writes=["sc2e"])
        sch.op(V, lambda e: e.reciprocal(sc2[:, 5:6], sc2[:, 4:5]),
               reads=["sc2e"], writes=["sc2f"])
        for h, kh, kx in ((0, "y1h0", "xq2tm0"), (1, "y1h1", "xq2tm1")):
            lo, hi = h * 2048, (h + 1) * 2048
            sch.op(S, lambda e, lo=lo, hi=hi: e.activation(
                y1[:, lo:hi], y1[:, lo:hi], AF.Identity,
                bias=MAG, scale=sc2[:, 3:4]),
                reads=[kh, "sc2d", "magic"], writes=[kh])
            sch.op(S, lambda e, lo=lo, hi=hi: e.activation(
                xq2tm[:, lo:hi], y1[:, lo:hi], AF.Identity,
                bias=MAGN, scale=1.0),
                reads=[kh, "magicn"], writes=[kx])

    def ffn_f2tailA(s):
        for r0 in range(0, KH, 4):
            for i in range(4):
                kb = r0 + i
                sch.op(T, lambda e, kb=kb, i=i: e.transpose(
                    ps_tx[:, i * P:(i + 1) * P], xq2tm[:, kb * P:(kb + 1) * P],
                    ident[:, :]),
                    reads=["xq2tm%d" % (kb // 16), "ident"], writes=["pstx"])
            sch.op(S, lambda e, r0=r0: e.copy(
                xq2fm[:, r0 * P:(r0 + 4) * P], ps_tx[:, 0:512]),
                reads=["pstx"], writes=["xq2fm"])

    def ffn_f2tailB(s):
        par = s % 2
        for n in range(2):
            pst, kpst = BD_PS[n % 2]

            def f2mm(e, n=n, pst=pst):
                ins = None
                for k in range(KH):
                    ins = e.matmul(
                        pst[:, 0:512],
                        xq2fm[:, k * P:(k + 1) * P],
                        wq2sb[:, k * D + n * 512:k * D + (n + 1) * 512],
                        start=(k == 0), stop=(k == KH - 1))
                return ins
            sch.op(T, f2mm, reads=["xq2fm"] + ["wq2sb_%d" % j_ for j_ in range(4)],
                   writes=[kpst])
            sch.op(V, lambda e, n=n, pst=pst, kpst=kpst: e.scalar_tensor_tensor(
                t2s[:, n * 512:(n + 1) * 512], pst[:, 0:512], sc2[:, 5:6],
                b2b[:, n * 512:(n + 1) * 512], op0=A.mult, op1=A.add),
                reads=[kpst, "sc2f", "b2b"], writes=["t2s"])
        sch.op(V, lambda e, par=par: e.tensor_add(
            ppart[:, par * D:(par + 1) * D], ppart[:, par * D:(par + 1) * D],
            t2s[:, 0:D]),
            reads=["t2s", "ppart%d" % par], writes=["ppart%d" % par])
        sch.op(SY, lambda e, s=s, par=par: e.dma_start(
            y[s * P:(s + 1) * P, :], ppart[:, par * D:(par + 1) * D]),
            reads=["ppart%d" % par], writes=["ppart%d" % par], sem="DO", amt=16)

    qload(0)
    qload(1)
    for s in range(8):
        quantize1(s)
        qload(s + 2)
    xq1_read(0)
    ffn_f1(0)
    ffn_f1gelu(0)
    for s in range(NT):
        ffn_f2head(s)
        if s % 4 == 0 and (s + 4) // 4 < NT // 4:
            xq1_read((s + 4) // 4)
        if s + 1 < NT:
            ffn_f1(s + 1)
        ffn_f2tailA(s)
        if s + 1 < NT:
            ffn_f1gelu(s + 1)
        if 8 + s < NT:
            quantize1(8 + s)
        if 10 + s < NT:
            qload(10 + s)
        ffn_f2tailB(s)

    sch.wait_only(SY, [("DO", sch.semc["DO"])])

    with nc.Block() as block:
        @block.sync
        def _(e):
            for f in sch.thunks["sync"]:
                f(e)

        @block.scalar
        def _(e):
            for f in sch.thunks["scalar"]:
                f(e)

        @block.vector
        def _(e):
            for f in sch.thunks["vector"]:
                f(e)

        @block.tensor
        def _(e):
            for f in sch.thunks["tensor"]:
                f(e)

        @block.gpsimd
        def _(e):
            for f in sch.thunks["gpsimd"]:
                f(e)

    es.close()
    return nc


_CACHED = [None]


def kernel(**inputs):
    xq = np.ascontiguousarray(inputs["L0_attn_query"], dtype=np.float32)
    B, TT, DD = xq.shape
    assert (B, TT, DD) == (4, 8192, 1024)
    if _CACHED[0] is None:
        _CACHED[0] = build_program()
    nc = _CACHED[0]
    common = {
        "W_qv": np.ascontiguousarray(inputs["L0_attn_W_qv"], np.float32),
        "W2_0": np.ascontiguousarray(inputs["L0_attn_W2_0"], np.float32),
        "W2_1": np.ascontiguousarray(inputs["L0_attn_W2_1"], np.float32),
        "SCW": np.ascontiguousarray(inputs["L0_attn_scale_weights"], np.float32).reshape(1, 2),
        "W_out": np.ascontiguousarray(inputs["L0_attn_W_out"], np.float32),
        "w1": np.ascontiguousarray(inputs["L0_ffn_w1"], np.float32),
        "b1": np.ascontiguousarray(inputs["L0_ffn_b1"], np.float32).reshape(1, HID),
        "w2": np.ascontiguousarray(inputs["L0_ffn_w2"], np.float32),
        "b2": np.ascontiguousarray(inputs["L0_ffn_b2"], np.float32).reshape(1, D),
        "gam": np.ascontiguousarray(inputs["L0_ffn_gamma"], np.float32).reshape(1, D),
        "bet": np.ascontiguousarray(inputs["L0_ffn_bet" + "a"], np.float32).reshape(1, D),
    }
    in_maps = []
    for c in range(8):
        b, h = c // 2, c % 2
        t0 = h * 4096
        xe = np.zeros((TIN, D), np.float32)
        lo, hi = t0 - P, t0 + 4096 + P
        s_, e_ = max(lo, 0), min(hi, TT)
        xe[s_ - lo:s_ - lo + (e_ - s_)] = xq[b, s_:e_]
        m = dict(common)
        m["x_in"] = xe
        in_maps.append(m)
    res = run_bass_kernel_spmd(nc, in_maps, core_ids=list(range(8)))
    out = np.empty((B, TT, DD), np.float32)
    for c in range(8):
        b, h = c // 2, c % 2
        out[b, h * 4096:(h + 1) * 4096] = res.results[c]["y"]
    return out


# version 27
# speedup vs baseline: 1.0189x; 1.0045x over previous
import contextlib
import numpy as np

import concourse.bass as bass
import concourse.mybir as mybir
from concourse.bass_utils import run_bass_kernel_spmd

F32 = mybir.dt.float32
BF16 = mybir.dt.bfloat16
FP8 = mybir.dt.float8e4
A = mybir.AluOpType
AF = mybir.ActivationFunctionType
AX = mybir.AxisListType

P = 128
D = 1024
KC = 8
NT = 32
NB = 34
TIN = NB * P
C0, C1, HALF = 21, 41, 20
NH, HD = 4, 256
HID = 4096
KH = 32
Z = 235
RR = C1 + Z
ROWP = 4 * RR + 1
SKG = 1024
SKEWN = SKG + 128 * ROWP + SKG
SKEW_TOT = ((2 * SKEWN + 4096 + 2047) // 2048) * 2048
K_CONST = 1.5 * 2.0 ** 82
EPS_PROBE = 2.0 ** -30
MAGIC_LO = 1.5 * 2.0 ** 23

ENGINES = ("sync", "scalar", "vector", "tensor", "gpsimd")
DMA_SEMS = ("DX1", "DX2", "DX3", "DW", "DSKW", "DSKR0", "DSKR1", "DSKR2",
            "DAW", "DAR0", "DAR1", "DAR2", "DAR3",
            "DQW0", "DQW1", "DQR0", "DQR1", "DQR2", "DQR3", "DP", "DPR", "DO")


class Sched:

    def __init__(self, nc, es):
        self.nc = nc
        self.es = es
        self.thunks = {e: [] for e in ENGINES}
        self.semh = {}
        self.semc = {}
        self.lastwait = {e: {} for e in ENGINES}
        self.lasttok = {e: None for e in ENGINES}
        self.wmap = {}
        self.rmap = {}
        self.barrier = None

    def sem(self, name):
        self.semh[name] = self.es.enter_context(self.nc.semaphore(name))
        self.semc[name] = 0

    def op(self, eng, fn, reads=(), writes=(), extra=(), sem=None, amt=1):
        if amt == 16 and sem is not None:
            writes = list(writes) + ["_stream_" + sem]
        waits = []
        for k in reads:
            waits.append(self.wmap.get(k))
        for k in writes:
            waits.append(self.wmap.get(k))
            waits.extend(self.rmap.get(k, ()))
        waits.extend(extra)
        if self.barrier is not None:
            waits.append(self.barrier)
        wl = []
        seen = set()
        for t in waits:
            if t is None or t in seen:
                continue
            seen.add(t)
            s, v = t
            assert v <= self.semc[s], (s, v, self.semc[s])
            wl.append((s, v))
        if sem is None:
            sem = {"sync": "SY", "scalar": "S", "vector": "V",
                   "tensor": "T", "gpsimd": "G"}[eng]
        self.semc[sem] += amt
        tok = (sem, self.semc[sem])
        lw = self.lastwait[eng]
        wl2 = [(s, v) for (s, v) in wl if lw.get(s, -1) < v]
        for s, v in wl:
            if lw.get(s, -1) < v:
                lw[s] = v
        semh = self.semh

        def thunk(e, wl2=wl2, fn=fn, sem=sem, amt=amt):
            for s, v in wl2:
                e.wait_ge(semh[s], v)
            if fn is not None:
                ins = fn(e)
                ins.then_inc(semh[sem], amt)
            else:
                e.nop().then_inc(semh[sem], amt)

        self.thunks[eng].append(thunk)
        for k in reads:
            self.rmap.setdefault(k, []).append(tok)
        for k in writes:
            self.wmap[k] = tok
            self.rmap[k] = []
        self.lasttok[eng] = tok
        return tok

    def wait_only(self, eng, extra):
        semh = self.semh
        wl = [t for t in extra if t is not None]

        def thunk(e, wl=wl):
            for s, v in wl:
                e.wait_ge(semh[s], v)

        self.thunks[eng].append(thunk)

    def full_barrier(self):
        dma_tots = [(s, self.semc[s]) for s in DMA_SEMS]
        for eng in ENGINES:
            ex = [self.lasttok[eng]]
            if eng == "sync":
                ex += dma_tots
            self.op(eng, None, extra=ex, sem="PB")
        self.barrier = ("PB", self.semc["PB"])


def build_program():
    nc = bass.Bass("TRN2", target_bir_lowering=False)
    es = contextlib.ExitStack()

    x_in = nc.dram_tensor("x_in", [TIN, D], F32, kind="ExternalInput")
    W_qv = nc.dram_tensor("W_qv", [D, 2 * D], F32, kind="ExternalInput")
    W2_0 = nc.dram_tensor("W2_0", [D, NH * C0], F32, kind="ExternalInput")
    W2_1 = nc.dram_tensor("W2_1", [D, NH * C1], F32, kind="ExternalInput")
    SCW = nc.dram_tensor("SCW", [1, 2], F32, kind="ExternalInput")
    W_out = nc.dram_tensor("W_out", [D, D], F32, kind="ExternalInput")
    w1 = nc.dram_tensor("w1", [HID, D], F32, kind="ExternalInput")
    b1 = nc.dram_tensor("b1", [1, HID], F32, kind="ExternalInput")
    w2 = nc.dram_tensor("w2", [D, HID], F32, kind="ExternalInput")
    b2 = nc.dram_tensor("b2", [1, D], F32, kind="ExternalInput")
    gam = nc.dram_tensor("gam", [1, D], F32, kind="ExternalInput")
    bet = nc.dram_tensor("bet", [1, D], F32, kind="ExternalInput")
    y = nc.dram_tensor("y", [NT * P, D], F32, kind="ExternalOutput")

    partial_d = nc.dram_tensor("partial_d", [NT * P * D], F32, kind="Internal")
    wq1t_d = nc.dram_tensor("wq1t_d", [KC * P * HID], FP8, kind="Internal")
    wq2t_d = nc.dram_tensor("wq2t_d", [KH * P * D], FP8, kind="Internal")
    skew_d = nc.dram_tensor("skew_d", [SKEW_TOT], BF16, kind="Internal")
    amix_d = nc.dram_tensor("amix_d", [2 * 4 * P * D], BF16, kind="Internal")
    xq1_d = nc.dram_tensor("xq1_d", [4 * 4 * P * D], BF16, kind="Internal")

    SB_BASE = 16512 + 2048
    SB_TOP = 16512 + 212863
    cursor = [SB_BASE]

    def salloc(name, shape, dtype, at=None):
        nb = mybir.dt.size(dtype)
        for s_ in shape[1:]:
            nb *= s_
        if at is None:
            at = cursor[0]
        else:
            at = at + SB_BASE
        t = nc.alloc_sbuf_tensor_at(name, shape, dtype, offset=at)
        assert at + nb <= SB_TOP, (name, at + nb, SB_TOP)
        cursor[0] = max(cursor[0], (at + nb + 127) // 128 * 128)
        return t

    ident = salloc("ident", [P, P], BF16)
    iot = salloc("iot", [P, P], F32)
    magic = salloc("magic", [P, 4], F32)
    cst = salloc("cst", [P, 4], F32)
    swb = salloc("swb", [P, 8], F32)
    onesrow = salloc("onesrow", [1, P], F32)
    onescol = salloc("onescol", [P, 1], F32)
    swrow = salloc("swrow", [1, 8], F32)
    b1row = salloc("b1row", [1, HID], F32)
    b2row = salloc("b2row", [1, D], F32)
    gamrow = salloc("gamrow", [1, D], F32)
    betrow = salloc("betrow", [1, D], F32)
    st_st = salloc("st_st", [P, 2 * NB], F32)
    st_r1 = salloc("st_r1", [P, NB], F32)
    st2_st = salloc("st2_st", [P, 2 * NT], F32)
    st_r2 = salloc("st_r2", [P, NT], F32)
    inv1t = salloc("inv1t", [P, NT], F32)
    bs = salloc("bs", [P, 24], F32)
    sc0 = salloc("sc0", [1, 8], F32)
    sc1 = salloc("sc1", [P, 8], F32)
    sc2 = salloc("sc2", [P, 8], F32)
    negmax = salloc("negmax", [P, 16], F32)
    ssum = salloc("ssum", [P, 16], F32)
    lnsm = salloc("lnsm", [P, 16], F32)
    bias2 = salloc("bias2", [P, 8], F32)
    e0b = salloc("e0b", [P, NH * C0], BF16)
    lnswb = salloc("lnswb", [P, 2], F32)
    RES_END = cursor[0]

    wqv_b = salloc("wqv_b", [P, KC * 2048], BF16)
    w2c_b = salloc("w2c_b", [P, KC * 248], BF16)
    wout_b = salloc("wout_b", [P, KC * D], BF16)
    AACT_BASE = cursor[0]
    xld = salloc("xld", [P, 3 * D], F32)
    xn32 = salloc("xn32", [P, 3 * D], F32)
    xnfm = salloc("xnfm", [P, KC * 8 * P], BF16)
    vtm = salloc("vtm", [P, 9 * D], BF16)
    rqfm = salloc("rqfm", [P, 2 * KC * 512], BF16)
    probs = salloc("probs", [P, 2 * 4 * RR], BF16)
    expt = salloc("expt", [P, 2 * 248], F32)
    pmrg = salloc("pmrg", [P, NH * C1], F32)
    stt = salloc("stt", [P, 4 * 3 * 512], BF16)
    amix_tm = salloc("amix_tm", [P, 2 * D], BF16)
    amix_fm = salloc("amix_fm", [P, KC * 512], BF16)
    x2 = salloc("x2", [P, 2 * D], F32)
    pout = salloc("pout", [P, 2 * D], F32)
    A_END = cursor[0]
    assert A_END <= SB_TOP, A_END

    AB = AACT_BASE - SB_BASE
    wq1asm = salloc("wq1asm", [P, KC * HID], FP8, at=AB)
    wq2asm = salloc("wq2asm", [P, KH * D], FP8, at=AB + 32 * 1024)
    wstage = salloc("wstage", [P, 2 * HID], F32, at=AB + 64 * 1024)
    wqt = salloc("wqt", [P, HID + D], F32, at=AB + 96 * 1024)
    zbf = salloc("zbf", [P, 2048], BF16, at=AB + 116 * 1024)

    ccur = [RES_END - SB_BASE]

    def calloc(name, shape, dtype):
        t = salloc(name, shape, dtype, at=ccur[0])
        nb = mybir.dt.size(dtype)
        for s_ in shape[1:]:
            nb *= s_
        ccur[0] = (ccur[0] + nb + 127) // 128 * 128
        return t

    wq1sb = calloc("wq1sb", [P, KC * HID], FP8)
    wq2sb = calloc("wq2sb", [P, KH * D], FP8)
    b1b = calloc("b1b", [P, HID], F32)
    b2b = calloc("b2b", [P, D], F32)
    gamb = calloc("gamb", [P, D], F32)
    betb = calloc("betb", [P, D], F32)
    xld3 = calloc("xld3", [P, 2 * D], F32)
    w3 = calloc("w3", [P, 2 * D], F32)
    xq1tm = calloc("xq1tm", [P, 2 * D], BF16)
    xq1fm = calloc("xq1fm", [P, 2 * KC * 512], BF16)
    y1 = calloc("y1", [P, HID], F32)
    xq2tm = calloc("xq2tm", [P, KH * P], BF16)
    xq2fm = calloc("xq2fm", [P, KH * P], BF16)
    ppart = calloc("ppart", [P, 2 * D], F32)
    t2s = calloc("t2s", [P, D], F32)
    C_END = ccur[0] + SB_BASE
    assert C_END <= SB_TOP, C_END

    psA = es.enter_context(nc.psum_tensor("psA", [P, 512], F32))
    psB = es.enter_context(nc.psum_tensor("psB", [P, 512], F32))
    ql = es.enter_context(nc.psum_tensor("ql", [P, 512], F32))
    bandA = es.enter_context(nc.psum_tensor("bandA", [P, 512], F32))
    bandB = es.enter_context(nc.psum_tensor("bandB", [P, 512], F32))
    woutA = es.enter_context(nc.psum_tensor("woutA", [P, 512], F32))
    woutB = es.enter_context(nc.psum_tensor("woutB", [P, 512], F32))
    ps_tx = es.enter_context(nc.psum_tensor("ps_tx", [P, 512], BF16))

    AB_PS = [(psA, "psA"), (psB, "psB")]
    WT_PS = [(woutA, "woutA"), (woutB, "woutB")]
    BD_PS = [(bandA, "bandA"), (bandB, "bandB")]
    F1_PS = AB_PS + [(ql, "ql")] + WT_PS

    sch = Sched(nc, es)
    for s_ in ("V", "S", "T", "G", "SY", "PB") + DMA_SEMS:
        sch.sem(s_)
    V, S, T, SY, G = "vector", "scalar", "tensor", "sync", "gpsimd"
    MAG = magic[:, 0:1]
    MAGN = magic[:, 3:4]

    sch.op(G, lambda e: e.iota(iot[:, :], [[1, P]], channel_multiplier=-1,
                               allow_small_or_imprecise_dtypes=True), writes=["iot"])
    sch.op(V, lambda e: e.tensor_scalar(ident[:, :], iot[:, :], 0.0, None, op0=A.is_equal),
           reads=["iot"], writes=["ident"])
    sch.op(V, lambda e: e.tensor_scalar(iot[:, :], iot[:, :], 0.0, None, op0=A.is_equal),
           reads=["iot"], writes=["iot"])
    sch.op(V, lambda e: e.memset(cst[:, 0:1], 1e-5), writes=["cst"])
    sch.op(V, lambda e: e.memset(magic[:, 1:2], 1.0), writes=["mg1"])
    sch.op(V, lambda e: e.tensor_scalar(magic[:, 2:3], magic[:, 1:2], EPS_PROBE, None, op0=A.add),
           reads=["mg1"], writes=["mg2"])
    sch.op(V, lambda e: e.tensor_scalar(magic[:, 1:2], magic[:, 2:3], 1.0, None, op0=A.subtract),
           reads=["mg2"], writes=["mg1"])
    sch.op(V, lambda e: e.tensor_scalar(magic[:, 0:1], magic[:, 1:2], K_CONST, MAGIC_LO,
                                        op0=A.mult, op1=A.add),
           reads=["mg1"], writes=["magic"])
    sch.op(V, lambda e: e.tensor_scalar(magic[:, 3:4], magic[:, 0:1], -1.0, None, op0=A.mult),
           reads=["magic"], writes=["magicn"])
    sch.op(V, lambda e: e.memset(zbf[:, :], 0.0), writes=["zbf"])
    sch.op(V, lambda e: e.memset(onesrow[:, :], 1.0), writes=["onesrow"])
    sch.op(V, lambda e: e.memset(onescol[:, :], 1.0), writes=["onescol"])

    done = 0
    while done < SKEW_TOT:
        n = min(P * 2048, SKEW_TOT - done)
        rows = (n + 2047) // 2048
        sch.op(SY, lambda e, off=done, rows=rows: e.dma_start(
            bass.AP(skew_d, off, [[2048, rows], [1, 2048]]), zbf[0:rows, :]),
            reads=["zbf"], writes=["d_skew0", "d_skew1"], sem="DSKW", amt=16)
        done += rows * 2048

    sch.op(SY, lambda e: e.dma_start(swrow[0:1, 0:2], SCW[:, :]), writes=["swrow"], sem="DP", amt=16)
    sch.op(V, lambda e: e.tensor_reduce(sc0[0:1, 0:1], swrow[0:1, 0:2], AX.X, A.max, negate=True),
           reads=["swrow"], writes=["sc0a"])
    sch.op(S, lambda e: e.activation(sc0[0:1, 2:4], swrow[0:1, 0:2], AF.Exp,
                                     bias=sc0[0:1, 0:1], scale=1.0),
           reads=["swrow", "sc0a"], writes=["sc0b"])
    sch.op(V, lambda e: e.tensor_reduce(sc0[0:1, 1:2], sc0[0:1, 2:4], AX.X, A.add),
           reads=["sc0b"], writes=["sc0c"])
    sch.op(V, lambda e: e.reciprocal(sc0[0:1, 4:5], sc0[0:1, 1:2]),
           reads=["sc0c"], writes=["sc0d"])
    sch.op(V, lambda e: e.tensor_scalar(sc0[0:1, 6:8], sc0[0:1, 2:4], sc0[0:1, 4:5], None, op0=A.mult),
           reads=["sc0b", "sc0d"], writes=["sc0e"])
    sch.op(T, lambda e: e.matmul(ql[:, 0:2], onesrow[0:1, :], sc0[0:1, 6:8],
                                 start=True, stop=True),
           reads=["sc0e", "onesrow"], writes=["ql"])
    sch.op(S, lambda e: e.copy(swb[:, 0:2], ql[:, 0:2]), reads=["ql"], writes=["swb01"])
    sch.op(S, lambda e: e.activation(lnswb[:, 0:2], swb[:, 0:2], AF.Ln, bias=0.0, scale=1.0),
           reads=["swb01"], writes=["lnswb"])

    p0s = ["DW", "DAW", "DQW0", "DAR0", "DAR1", "DAR2", "DAR3", "DQR1"]
    NP0 = len(p0s)

    sch.op(V, lambda e: e.memset(magic[:, 1:2], 0.0), reads=["mg2"], writes=["mg1"])
    sch.op(V, lambda e: e.memset(cst[:, 1:2], 0.0), writes=["c1"])

    def flat_tile(wsrc, t, fi, acc_sl, red_sl, ka, kr):
        so = (fi % 2) * HID
        ws = "wst%d" % (fi % 2)
        sch.op(SY, lambda e, t=t, so=so, wsrc=wsrc: e.dma_start(
            wstage[:, so:so + HID],
            bass.AP(wsrc, t * P * HID, [[HID, P], [1, HID]])),
            writes=[ws], sem=p0s[fi % NP0], amt=16)
        sch.op(V, lambda e, so=so, red_sl=red_sl: e.tensor_reduce(
            red_sl, wstage[:, so:so + HID], AX.X, A.add,
            apply_absolute_value=True),
            reads=[ws], writes=[kr])
        sch.op(V, lambda e, acc_sl=acc_sl, red_sl=red_sl: e.tensor_add(
            acc_sl, acc_sl, red_sl),
            reads=[kr, ka], writes=[ka])

    fi = 0
    for t in range(KC):
        flat_tile(w1, t, fi, magic[:, 1:2], magic[:, 2:3], "mg1", "mg2"); fi += 1
        flat_tile(w2, t, fi, cst[:, 1:2], cst[:, 2:3], "c1", "c2"); fi += 1

    def scale_chain(acc_sl, ka, nel, swcol, swkey):
        sch.op(T, lambda e, acc_sl=acc_sl: e.matmul(
            ql[0:1, 4:5], acc_sl, onescol[:, 0:1], start=True, stop=True),
            reads=[ka, "onescol"], writes=["ql"])
        sch.op(S, lambda e: e.copy(sc0[0:1, 5:6], ql[0:1, 4:5]), reads=["ql"], writes=["sc0f"])
        sch.op(V, lambda e, nel=nel: e.tensor_scalar(sc0[0:1, 5:6], sc0[0:1, 5:6],
                                                     1.0 / nel, 1e-5,
                                                     op0=A.mult, op1=A.max),
               reads=["sc0f"], writes=["sc0f"])
        sch.op(V, lambda e: e.reciprocal(sc0[0:1, 5:6], sc0[0:1, 5:6]),
               reads=["sc0f"], writes=["sc0f"])
        sch.op(T, lambda e: e.matmul(ql[:, 5:6], onesrow[0:1, :], sc0[0:1, 5:6],
                                     start=True, stop=True),
               reads=["sc0f", "onesrow"], writes=["ql"])
        sch.op(S, lambda e, swcol=swcol: e.copy(swb[:, swcol:swcol + 1], ql[:, 5:6]),
               reads=["ql"], writes=[swkey])

    scale_chain(magic[:, 1:2], "mg1", float(KH * P * D), 2, "sw1q")
    scale_chain(cst[:, 1:2], "c1", float(KC * P * HID), 3, "sw2q")
    sch.op(V, lambda e: e.tensor_scalar(magic[:, 1:2], magic[:, 0:1], 1.0, None,
                                        op0=A.add),
           reads=["magic"], writes=["mg1", "mp1"])
    sch.op(V, lambda e: e.tensor_scalar(magic[:, 2:3], magic[:, 0:1], 1.0, None,
                                        op0=A.subtract),
           reads=["magic"], writes=["mg2", "mm1"])

    def ld_cast(dram, cols, dst, dstoff, kidx, wkey, slot):
        so = slot * HID
        sch.op(SY, lambda e: e.dma_start(wstage[:, so:so + cols], dram[kidx * P:(kidx + 1) * P, :]),
               writes=["wst%d" % slot], sem=p0s[(kidx + slot) % NP0], amt=16)
        sch.op(V, lambda e: e.tensor_copy(dst[:, dstoff:dstoff + cols], wstage[:, so:so + cols]),
               reads=["wst%d" % slot], writes=[wkey])

    sl = 0
    for k in range(KC):
        ld_cast(W_qv, 2048, wqv_b, k * 2048, k, "wqv", sl % 2); sl += 1
    for k in range(KC):
        ld_cast(W_out, D, wout_b, k * D, k, "wout", sl % 2); sl += 1
    for k in range(KC):
        ld_cast(W2_0, NH * C0, w2c_b, k * 248, k, "w2c", sl % 2); sl += 1
        ld_cast(W2_1, NH * C1, w2c_b, k * 248 + NH * C0, k, "w2c", sl % 2); sl += 1

    sch.op(SY, lambda e: e.dma_start(b1row[:, :], b1[:, :]), writes=["b1row"], sem="DP", amt=16)
    sch.op(SY, lambda e: e.dma_start(b2row[:, :], b2[:, :]), writes=["b2row"], sem="DP", amt=16)
    sch.op(SY, lambda e: e.dma_start(gamrow[:, :], gam[:, :]), writes=["gamrow"], sem="DP", amt=16)
    sch.op(SY, lambda e: e.dma_start(betrow[:, :], bet[:, :]), writes=["betrow"], sem="DP", amt=16)


    pend = []
    evct = [0]

    def flush_evacs(keep):
        while len(pend) > keep:
            fn, kpst, asmkey = pend.pop(0)
            if evct[0] % 2 == 0:
                sch.op(S, lambda e, fn=fn: fn(e, 0), reads=[kpst, "magicn"],
                       writes=[asmkey])
            else:
                sch.op(V, lambda e, fn=fn: fn(e, 1), reads=[kpst, "magic"],
                       writes=[asmkey])
            evct[0] += 1

    def quant_tile(wsrc, t, ti, cols, swcol, swkey, asmdst, asmkey, asm_k_step,
                   wqo, wqk, stg, stgname):
        so = (ti % 2) * HID
        ws = "%s%d" % (stgname, ti % 2)
        sch.op(SY, lambda e, t=t, so=so, cols=cols, wsrc=wsrc, stg=stg: e.dma_start(
            stg[:, so:so + cols], wsrc[t * P:(t + 1) * P, :]),
            writes=[ws], sem=p0s[ti % NP0], amt=16)
        for c in range(cols // 512):
            co = c * 512
            sch.op(S, lambda e, so=so, co=co, swcol=swcol, stg=stg: e.activation(
                stg[:, so + co:so + co + 512], stg[:, so + co:so + co + 512],
                AF.Identity, bias=MAG, scale=swb[:, swcol:swcol + 1]),
                reads=[ws, swkey, "magic"], writes=[ws])
            wqkc = "%s_%d" % (wqk, c % 4)
            sch.op(V, lambda e, so=so, co=co, wqo=wqo, stg=stg: e.tensor_scalar(
                wqt[:, wqo + co:wqo + co + 512], stg[:, so + co:so + co + 512],
                magic[:, 1:2], magic[:, 2:3], op0=A.min, op1=A.max),
                reads=[ws, "mp1", "mm1"], writes=[wqkc])
            pst, kpst = PH0_PS[(ti * 8 + c) % 6]
            for i in range(4):
                kb = c * 4 + i
                sch.op(T, lambda e, kb=kb, i=i, wqo=wqo, pst=pst: e.transpose(
                    pst[:, i * P:(i + 1) * P],
                    wqt[:, wqo + kb * P:wqo + (kb + 1) * P], iot[:, :]),
                    reads=[wqkc, "iot"], writes=[kpst])

            def evac(e, eng, t=t, c=c, asmdst=asmdst, asm_k_step=asm_k_step,
                     pst=pst):
                dst = bass.AP(asmdst, c * 4 * asm_k_step + t * P,
                              [[asmdst.shape[1], P], [asm_k_step, 4], [1, P]])
                if eng == 0:
                    return e.activation(dst, pst[:, 0:512], AF.Identity,
                                        bias=MAGN, scale=1.0)
                return e.tensor_scalar(dst, pst[:, 0:512], MAG, None,
                                       op0=A.subtract)
            pend.append((evac, kpst, asmkey))
            flush_evacs(2)

    PH0_PS = AB_PS + BD_PS + WT_PS
    ti = 0
    for g in range(KC):
        for i in range(4):
            quant_tile(w1, 4 * g + i, ti, D, 2, "sw1q", wq1asm, "wq1asm", HID,
                       HID, "wqt1", wstage, "wst"); ti += 1
        quant_tile(w2, g, ti, HID, 3, "sw2q", wq2asm, "wq2asm", D,
                   0, "wqt2", wstage, "wst"); ti += 1

    flush_evacs(0)
    sch.op(SY, lambda e: e.dma_start(bass.AP(wq1t_d, 0, [[KC * HID, P], [1, KC * HID]]),
                                     wq1asm[:, :]),
           reads=["wq1asm"], writes=["d_wq1"], sem="DW", amt=16)
    sch.op(SY, lambda e: e.dma_start(bass.AP(wq2t_d, 0, [[KH * D, P], [1, KH * D]]),
                                     wq2asm[:, :]),
           reads=["wq2asm"], writes=["d_wq2"], sem="DW", amt=16)

    def vload(eidx):
        t = eidx + 1
        par = t % 3
        sch.op(G, lambda e, t=t, par=par: e.dma_start(
            xld[:, par * D:(par + 1) * D], x_in[t * P:(t + 1) * P, :]),
            reads=["d_wq1", "d_wq2"],
            writes=["xld%d" % par], sem="DX1", amt=16)

    def vpath_ln(eidx):
        t = eidx + 1
        par = t % 3
        kx = "xld%d" % par
        kn = "xn32%d" % (t % 3)
        kb2 = "bs%d" % (t % 2)
        bso = (t % 2) * 12
        sch.op(V, lambda e: e.bn_stats(bs[:, bso:bso + 6], xld[:, par * D:par * D + 512]),
               reads=[kx], writes=[kb2])
        sch.op(V, lambda e: e.bn_stats(bs[:, bso + 6:bso + 12], xld[:, par * D + 512:par * D + D]),
               reads=[kx], writes=[kb2])
        sch.op(V, lambda e: e.bn_aggr(st_st[:, 2 * t:2 * t + 2], bs[:, bso:bso + 12]),
               reads=[kb2], writes=["st%d" % t])
        sch.op(S, lambda e: e.activation(st_r1[:, t:t + 1], st_st[:, 2 * t + 1:2 * t + 2],
                                         AF.Ln, bias=cst[:, 0:1], scale=1.0),
               reads=["st%d" % t, "cst"], writes=["str%d" % t])
        sch.op(S, lambda e: e.activation(st_r1[:, t:t + 1], st_r1[:, t:t + 1],
                                         AF.Exp, bias=0.0, scale=-0.5),
               reads=["str%d" % t], writes=["str%d" % t])
        sch.op(V, lambda e: e.tensor_scalar(xn32[:, (t % 3) * D:(t % 3 + 1) * D],
                                            xld[:, par * D:(par + 1) * D],
                                            st_st[:, 2 * t:2 * t + 1], st_r1[:, t:t + 1],
                                            op0=A.subtract, op1=A.mult),
               reads=[kx, "st%d" % t, "str%d" % t], writes=[kn])
        if 0 <= eidx < NT:
            s = eidx
            sch.op(V, lambda e: e.bn_stats(bs[:, bso:bso + 6],
                                           xn32[:, (t % 3) * D:(t % 3) * D + 512]),
                   reads=[kn], writes=[kb2])
            sch.op(V, lambda e: e.bn_stats(bs[:, bso + 6:bso + 12],
                                           xn32[:, (t % 3) * D + 512:(t % 3) * D + D]),
                   reads=[kn], writes=[kb2])
            sch.op(V, lambda e: e.bn_aggr(st2_st[:, 2 * s:2 * s + 2], bs[:, bso:bso + 12]),
                   reads=[kb2], writes=["st2_%d" % s])
            sch.op(S, lambda e: e.activation(st_r2[:, s:s + 1], st2_st[:, 2 * s + 1:2 * s + 2],
                                             AF.Ln, bias=cst[:, 0:1], scale=1.0),
                   reads=["st2_%d" % s, "cst"], writes=["st2r%d" % s])
            sch.op(S, lambda e: e.activation(st_r2[:, s:s + 1], st_r2[:, s:s + 1],
                                             AF.Exp, bias=0.0, scale=-0.5),
                   reads=["st2r%d" % s], writes=["st2r%d" % s])

    def vpath_tv(eidx):
        t = eidx + 1
        slot = eidx % 8
        slot9 = eidx % 9
        kn = "xn32%d" % (t % 3)
        kf = "xnfm%d" % slot
        kv = "vtm%d" % slot9
        for r0 in (0, 4):
            pst, kpst = AB_PS[(eidx * 3 + r0 // 4) % 2]
            for i in range(4):
                kb = r0 + i
                sch.op(T, lambda e, kb=kb, i=i, pst=pst, t=t: e.transpose(
                    pst[:, i * P:(i + 1) * P],
                    xn32[:, (t % 3) * D + kb * P:(t % 3) * D + (kb + 1) * P], iot[:, :]),
                    reads=[kn, "iot"], writes=[kpst])

            def evacx(e, r0=r0, slot=slot, pst=pst):
                dst = bass.AP(xnfm, r0 * (8 * P) + slot * P,
                              [[KC * 8 * P, P], [8 * P, 4], [1, P]])
                return e.tensor_copy(dst, pst[:, 0:512])
            sch.op(V, evacx, reads=[kpst], writes=[kf])
        for n in range(2):
            pst, kpst = AB_PS[(eidx * 3 + 2 + n) % 2]

            def vmm(e, slot=slot, n=n, pst=pst):
                ins = None
                for k in range(KC):
                    ins = e.matmul(
                        pst[:, 0:512],
                        xnfm[:, k * (8 * P) + slot * P:k * (8 * P) + (slot + 1) * P],
                        wqv_b[:, k * 2048 + D + n * 512:k * 2048 + D + (n + 1) * 512],
                        start=(k == 0), stop=(k == KC - 1))
                return ins
            sch.op(T, vmm, reads=[kf, "wqv"], writes=[kpst])
            sch.op(S, lambda e, slot9=slot9, n=n, pst=pst: e.copy(
                vtm[:, slot9 * D + n * 512:slot9 * D + (n + 1) * 512], pst[:, 0:512]),
                reads=[kpst], writes=[kv])

    def qpath(quad):
        qpar = quad % 2
        slot0 = (4 * quad) % 8
        kfs = ["xnfm%d" % ((4 * quad + i) % 8) for i in range(4)]
        for m in range(KC):
            pst, kpst = WT_PS[m % 2]

            def qmm(e, m=m, slot0=slot0, pst=pst):
                ins = None
                for k in range(KC):
                    ins = e.matmul(
                        pst[:, 0:512],
                        wqv_b[:, k * 2048 + m * P:k * 2048 + (m + 1) * P],
                        xnfm[:, k * (8 * P) + slot0 * P:k * (8 * P) + slot0 * P + 512],
                        start=(k == 0), stop=(k == KC - 1))
                return ins
            sch.op(T, qmm, reads=kfs + ["wqv"], writes=[kpst])
            sch.op(V, lambda e, m=m, qpar=qpar, pst=pst: e.tensor_scalar(
                rqfm[:, qpar * 4096 + m * 512:qpar * 4096 + (m + 1) * 512],
                pst[:, 0:512], 0.0, None, op0=A.max),
                reads=[kpst], writes=["rqfm%d" % qpar])

    def attn1a(s):
        par = s % 2
        quad = s // 4
        qpar = quad % 2
        eo = par * 248

        def lmm(e, s=s, qpar=qpar):
            ins = None
            for k in range(KC):
                ins = e.matmul(
                    ql[:, 0:248],
                    rqfm[:, qpar * 4096 + k * 512 + (s % 4) * P:
                         qpar * 4096 + k * 512 + (s % 4 + 1) * P],
                    w2c_b[:, k * 248:(k + 1) * 248],
                    start=(k == 0), stop=(k == KC - 1))
            return ins
        sch.op(T, lmm, reads=["rqfm%d" % qpar, "w2c"], writes=["ql"])
        sch.op(V, lambda e, eo=eo: e.tensor_copy(expt[:, eo:eo + 248], ql[:, 0:248]),
               reads=["ql"], writes=["expt%d" % par])
        sch.op(V, lambda e, eo=eo, par=par: e.tensor_reduce(
            bass.AP(negmax, par * 8, [[16, P], [1, 4]]),
            expt[:, eo:eo + 84].rearrange("p (h c) -> p h c", c=C0),
            AX.X, A.max, negate=True),
            reads=["expt%d" % par], writes=["negmax0_%d" % par])
        sch.op(V, lambda e, eo=eo, par=par: e.tensor_reduce(
            bass.AP(negmax, par * 8 + 4, [[16, P], [1, 4]]),
            expt[:, eo + 84:eo + 248].rearrange("p (h c) -> p h c", c=C1),
            AX.X, A.max, negate=True),
            reads=["expt%d" % par], writes=["negmax1_%d" % par])

    def attn1b1(s):
        par = s % 2
        eo = par * 248
        no = par * 8
        groups = [(h, 0, h * C0, C0) for h in range(NH)] + \
                 [(h, 1, NH * C0 + h * C1, C1) for h in range(NH)]
        for gi, (h, ci, c0, cw) in enumerate(groups):
            sch.op(S, lambda e, gi=gi, c0=c0, cw=cw, eo=eo, no=no: e.activation(
                pmrg[:, 0:cw], expt[:, eo + c0:eo + c0 + cw], AF.Exp,
                bias=negmax[:, no + gi:no + gi + 1], scale=1.0,
                accum_out=ssum[:, no + gi:no + gi + 1]),
                reads=["expt%d" % par, "negmax0_%d" % par, "negmax1_%d" % par],
                writes=["pmgj", "ssum%d_%d" % (par, gi)])
        sch.op(S, lambda e, no=no: e.activation(lnsm[:, no:no + 8], ssum[:, no:no + 8],
                                                AF.Ln, bias=0.0, scale=1.0),
               reads=["ssum%d_%d" % (par, g_) for g_ in range(8)], writes=["lnsm%d" % par])

    def attn1b2(s):
        par = s % 2
        kp = "probs%d" % par
        kst = "stt%d" % (s % 4)
        eo = par * 248
        no = par * 8
        sch.op(V, lambda e, no=no: e.scalar_tensor_tensor(
            bias2[:, 0:4], negmax[:, no:no + 4], lnswb[:, 0:1],
            lnsm[:, no:no + 4], op0=A.add, op1=A.subtract),
            reads=["negmax0_%d" % par, "lnswb", "lnsm%d" % par], writes=["bias2"])
        sch.op(V, lambda e, no=no: e.scalar_tensor_tensor(
            bias2[:, 4:8], negmax[:, no + 4:no + 8], lnswb[:, 1:2],
            lnsm[:, no + 4:no + 8], op0=A.add, op1=A.subtract),
            reads=["negmax1_%d" % par, "lnswb", "lnsm%d" % par], writes=["bias2"])
        for h in range(NH):
            sch.op(S, lambda e, h=h, par=par, eo=eo: e.activation(
                probs[:, par * (4 * RR) + h * RR:par * (4 * RR) + h * RR + C1],
                expt[:, eo + 84 + h * C1:eo + 84 + (h + 1) * C1], AF.Exp,
                bias=bias2[:, 4 + h:5 + h], scale=1.0),
                reads=["expt%d" % par, "bias2"], writes=[kp])
            sch.op(S, lambda e, h=h, eo=eo: e.activation(
                e0b[:, h * C0:(h + 1) * C0],
                expt[:, eo + h * C0:eo + (h + 1) * C0], AF.Exp,
                bias=bias2[:, h:h + 1], scale=1.0),
                reads=["expt%d" % par, "bias2"], writes=["e0b"])
        for h in range(NH):
            sch.op(V, lambda e, h=h, par=par: e.tensor_add(
                probs[:, par * (4 * RR) + h * RR + 10:par * (4 * RR) + h * RR + 31],
                probs[:, par * (4 * RR) + h * RR + 10:par * (4 * RR) + h * RR + 31],
                e0b[:, h * C0:(h + 1) * C0]),
                reads=["e0b"], writes=[kp])
        sch.op(SY, lambda e, par=par: e.dma_start(
            bass.AP(skew_d, par * SKEWN + SKG, [[ROWP, P], [1, 4 * RR]]),
            probs[:, par * (4 * RR):(par + 1) * (4 * RR)]),
            reads=[kp], writes=["d_skew%d" % par], sem="DSKW", amt=16)
        for g in range(3):
            sch.op(SY, lambda e, g=g, par=par, s=s: e.dma_start(
                stt[:, ((s % 4) * 3 + g) * 512:((s % 4) * 3 + g + 1) * 512],
                bass.AP(skew_d, par * SKEWN + SKG + g * P - 108, [[RR, 512], [1, P]]),
                transpose=True),
                reads=["d_skew%d" % par], writes=["%s_%d" % (kst, g)],
                sem="DSKR%d" % g, amt=16)

    def attn2(s):
        par = s % 2
        kst = ["stt%d_%d" % (s % 4, g) for g in range(3)]
        vkeys = ["vtm%d" % ((s - 1 + g) % 9) for g in range(3)]
        for half in range(2):
            pst, kpst = BD_PS[(s * 2 + half) % 2]

            def bmm(e, s=s, half=half, pst=pst):
                ins = None
                for h in (half * 2, half * 2 + 1):
                    for g in range(3):
                        vslot = (s - 1 + g) % 9
                        ins = e.matmul(
                            pst[:, (h - half * 2) * HD:(h - half * 2 + 1) * HD],
                            bass.AP(stt, ((s % 4) * 3 + g) * 512 + h, [[4 * 3 * 512, P], [4, P]]),
                            vtm[:, vslot * D + h * HD:vslot * D + (h + 1) * HD],
                            start=(g == 0), stop=(g == 2))
                return ins
            sch.op(T, bmm, reads=kst + vkeys, writes=[kpst])
            sch.op(V, lambda e, par=par, half=half, pst=pst: e.tensor_copy(
                amix_tm[:, par * D + half * 512:par * D + (half + 1) * 512], pst[:, 0:512]),
                reads=[kpst], writes=["amix%d" % par])
        sch.op(SY, lambda e, s=s, par=par: e.dma_start(
            bass.AP(amix_d, (s // 4 % 2) * 4 * P * D + (s % 4) * P * D, [[D, P], [1, D]]),
            amix_tm[:, par * D:(par + 1) * D]),
            reads=["amix%d" % par], writes=["d_amix%d" % (s // 4 % 2)], sem="DAW", amt=16)

    def wout_read(quad):
        qpar = quad % 2
        for k in range(KC):
            sch.op(SY, lambda e, k=k, qpar=qpar: e.dma_start(
                amix_fm[:, k * 512:(k + 1) * 512],
                bass.AP(amix_d, qpar * 4 * P * D + k * P, [[D, 512], [1, P]]),
                transpose=True),
                reads=["d_amix%d" % qpar], writes=["amixfm%d" % k],
                sem="DAR%d" % (k % 4), amt=16)

    def wout_mm(quad):
        amixkeys = ["amixfm%d" % k for k in range(KC)]
        for i in range(4):
            s = 4 * quad + i
            par = s % 2
            sch.op(G, lambda e, s=s, par=par: e.dma_start(
                x2[:, par * D:(par + 1) * D], x_in[(s + 1) * P:(s + 2) * P, :]),
                writes=["x2%d" % par], sem="DX2", amt=16)
            for n in range(2):
                pst, kpst = WT_PS[(i * 2 + n) % 2]

                def omm(e, i=i, n=n, pst=pst):
                    ins = None
                    for k in range(KC):
                        ins = e.matmul(
                            pst[:, 0:512],
                            amix_fm[:, k * 512 + i * P:k * 512 + (i + 1) * P],
                            wout_b[:, k * D + n * 512:k * D + (n + 1) * 512],
                            start=(k == 0), stop=(k == KC - 1))
                    return ins
                sch.op(T, omm, reads=amixkeys + ["wout"], writes=[kpst])
                sch.op(V, lambda e, par=par, n=n, pst=pst: e.scalar_tensor_tensor(
                    pout[:, par * D + n * 512:par * D + (n + 1) * 512], pst[:, 0:512], 1.0,
                    x2[:, par * D + n * 512:par * D + (n + 1) * 512], op0=A.mult, op1=A.add),
                    reads=[kpst, "x2%d" % par], writes=["pout%d" % par])
            sch.op(SY, lambda e, s=s, par=par: e.dma_start(
                bass.AP(partial_d, s * P * D, [[D, P], [1, D]]),
                pout[:, par * D:(par + 1) * D]),
                reads=["pout%d" % par], writes=["d_part%d" % s], sem="DP", amt=16)

    for e_ in (-1, 0, 1):
        vload(e_)
    for kind, e_ in (("ln", -1), ("vl", 2), ("ln", 0), ("vl", 3), ("ln", 1),
                     ("vl", 4), ("tv", -1), ("ln", 2), ("vl", 5), ("tv", 0),
                     ("ln", 3), ("vl", 6), ("tv", 1), ("ln", 4), ("vl", 7),
                     ("tv", 2), ("ln", 5), ("tv", 3)):
        {"ln": vpath_ln, "tv": vpath_tv, "vl": vload}[kind](e_)
    sch.full_barrier()
    sch.op(V, lambda e: e.memset(probs[:, :], 0.0), writes=["probs0", "probs1"])

    for s in range(NT + 7):
        if s + 8 <= NB - 2:
            vload(s + 8)
        if 0 <= s - 4 < NT:
            attn2(s - 4)
        if s + 4 <= NB - 2:
            vpath_tv(s + 4)
        if s + 6 <= NB - 2:
            vpath_ln(s + 6)
        if (s - 7) % 4 == 0 and 0 <= s - 7 < NT:
            wout_read((s - 7) // 4)
        if s % 4 == 0 and s < NT:
            qpath(s // 4)
        if 0 <= s - 2 < NT:
            attn1b2(s - 2)
        if s < NT:
            attn1a(s)
        if 0 <= s - 1 < NT:
            attn1b1(s - 1)
        if (s - 10) % 4 == 0 and 0 <= s - 10 < NT:
            wout_mm((s - 10) // 4)

    sch.full_barrier()

    for j in range(4):
        w = KC * HID // 4
        sch.op(SY, lambda e, j=j, w=w: e.dma_start(
            wq1sb[:, j * w:(j + 1) * w],
            bass.AP(wq1t_d, j * w, [[KC * HID, P], [1, w]])),
            reads=["d_wq1"], writes=["wq1sb_%d" % j], sem=p0s[j], amt=16)
        sch.op(SY, lambda e, j=j, w=w: e.dma_start(
            wq2sb[:, j * w:(j + 1) * w],
            bass.AP(wq2t_d, j * w, [[KH * D, P], [1, w]])),
            reads=["d_wq2"], writes=["wq2sb_%d" % j], sem=p0s[4 + j], amt=16)
    bi = 0
    for (row, rkey, dst, dkey, n) in ((gamrow, "gamrow", gamb, "gamb", D),
                                      (betrow, "betrow", betb, "betb", D),
                                      (b1row, "b1row", b1b, "b1b", HID),
                                      (b2row, "b2row", b2b, "b2b", D)):
        for c in range(0, n, 512):
            pst, kpst = WT_PS[bi % 2]
            bi += 1
            sch.op(T, lambda e, row=row, c=c, pst=pst: e.matmul(
                pst[:, 0:512], onesrow[0:1, :], row[0:1, c:c + 512],
                start=True, stop=True),
                reads=[rkey, "onesrow"], writes=[kpst])
            sch.op(S, lambda e, dst=dst, c=c, pst=pst: e.copy(dst[:, c:c + 512], pst[:, 0:512]),
                   reads=[kpst], writes=[dkey])

    def qload(s):
        par = s % 2
        tb = s + 1
        sch.op(G, lambda e, tb=tb, par=par: e.dma_start(
            xld3[:, par * D:(par + 1) * D], x_in[tb * P:(tb + 1) * P, :]),
            writes=["xld3%d" % par], sem="DX3", amt=16)

    def quantize1(s):
        par = s % 2
        tb = s + 1
        kx, kw, kq = "xld3%d" % par, "w3%d" % par, "xq1tm%d" % par
        sch.op(V, lambda e: e.tensor_scalar(
            w3[:, par * D:(par + 1) * D], xld3[:, par * D:(par + 1) * D],
            st_st[:, 2 * tb:2 * tb + 1], st_r1[:, tb:tb + 1], op0=A.subtract, op1=A.mult),
            reads=[kx, "st%d" % tb, "str%d" % tb], writes=[kw])
        sch.op(V, lambda e: e.tensor_scalar(
            w3[:, par * D:(par + 1) * D], w3[:, par * D:(par + 1) * D],
            st2_st[:, 2 * s:2 * s + 1], st_r2[:, s:s + 1], op0=A.subtract, op1=A.mult),
            reads=[kw, "st2_%d" % s, "st2r%d" % s], writes=[kw])
        sch.op(V, lambda e: e.tensor_mul(w3[:, par * D:(par + 1) * D],
                                         w3[:, par * D:(par + 1) * D], gamb[:, :]),
               reads=[kw, "gamb"], writes=[kw])
        sch.op(V, lambda e: e.tensor_add(w3[:, par * D:(par + 1) * D],
                                         w3[:, par * D:(par + 1) * D], betb[:, :]),
               reads=[kw, "betb"], writes=[kw])
        sch.op(V, lambda e: e.tensor_reduce(sc1[:, 0:1], w3[:, par * D:(par + 1) * D],
                                            AX.X, A.max, apply_absolute_value=True),
               reads=[kw], writes=["sc1a"])
        sch.op(V, lambda e: e.tensor_scalar(sc1[:, 1:2], sc1[:, 0:1], 1e-5, None, op0=A.max),
               reads=["sc1a"], writes=["sc1b"])
        sch.op(V, lambda e: e.reciprocal(sc1[:, 2:3], sc1[:, 1:2]),
               reads=["sc1b"], writes=["sc1c"])
        sch.op(V, lambda e: e.tensor_scalar(sc1[:, 3:4], sc1[:, 2:3], 127.0, None, op0=A.mult),
               reads=["sc1c"], writes=["sc1d"])
        sch.op(V, lambda e: e.tensor_scalar(sc1[:, 4:5], sc1[:, 3:4], swb[:, 2:3], None, op0=A.mult),
               reads=["sc1d", "sw1q"], writes=["sc1e"])
        sch.op(V, lambda e: e.reciprocal(inv1t[:, s:s + 1], sc1[:, 4:5]),
               reads=["sc1e"], writes=["inv1_%d" % s])
        sch.op(S, lambda e: e.activation(
            w3[:, par * D:(par + 1) * D], w3[:, par * D:(par + 1) * D], AF.Identity,
            bias=MAG, scale=sc1[:, 3:4]),
            reads=[kw, "sc1d", "magic"], writes=[kw])
        sch.op(S, lambda e: e.activation(
            xq1tm[:, par * D:(par + 1) * D], w3[:, par * D:(par + 1) * D], AF.Identity,
            bias=MAGN, scale=1.0),
            reads=[kw, "magicn"], writes=[kq])
        sch.op(SY, lambda e, s=s: e.dma_start(
            bass.AP(xq1_d, (s // 4 % 4) * 4 * P * D + (s % 4) * P * D, [[D, P], [1, D]]),
            xq1tm[:, par * D:(par + 1) * D]),
            reads=[kq], writes=["d_xq1_%d" % (s % 16)], sem="DQW%d" % (s % 2), amt=16)

    def xq1_read(quad):
        qpar = quad % 2
        for k in range(KC):
            sch.op(SY, lambda e, k=k, quad=quad, qpar=qpar: e.dma_start(
                xq1fm[:, qpar * 4096 + k * 512:qpar * 4096 + (k + 1) * 512],
                bass.AP(xq1_d, (quad % 4) * 4 * P * D + k * P, [[D, 512], [1, P]]),
                transpose=True),
                reads=["d_xq1_%d" % ((4 * quad + i_) % 16) for i_ in range(4)],
                writes=["xq1fm%d_%d" % (qpar, k)],
                sem="DQR%d" % (k % 4), amt=16)

    def ffn_f1(s):
        quad = s // 4
        qpar = quad % 2
        xq1keys = ["xq1fm%d_%d" % (qpar, k) for k in range(KC)]
        for c in range(8):
            pst, kpst = F1_PS[(s * 8 + c) % 5]

            def fmm(e, c=c, s=s, qpar=qpar, pst=pst):
                ins = None
                for k in range(KC):
                    ins = e.matmul(
                        pst[:, 0:512],
                        xq1fm[:, qpar * 4096 + k * 512 + (s % 4) * P:
                              qpar * 4096 + k * 512 + (s % 4 + 1) * P],
                        wq1sb[:, k * HID + c * 512:k * HID + (c + 1) * 512],
                        start=(k == 0), stop=(k == KC - 1))
                return ins
            sch.op(T, fmm, reads=xq1keys + ["wq1sb_%d" % j_ for j_ in range(4)], writes=[kpst])
            sch.op(V, lambda e, c=c, s=s, pst=pst, kpst=kpst: e.scalar_tensor_tensor(
                y1[:, c * 512:(c + 1) * 512], pst[:, 0:512], inv1t[:, s:s + 1],
                b1b[:, c * 512:(c + 1) * 512], op0=A.mult, op1=A.add),
                reads=[kpst, "inv1_%d" % s, "b1b"], writes=["y1h%d" % (c // 4)])

    def ffn_f1gelu(s):
        sch.op(S, lambda e: e.activation(y1[:, 0:2048], y1[:, 0:2048], AF.Gelu,
                                         bias=0.0, scale=1.0),
               reads=["y1h0"], writes=["y1h0"])
        sch.op(S, lambda e: e.activation(y1[:, 2048:HID], y1[:, 2048:HID], AF.Gelu,
                                         bias=0.0, scale=1.0),
               reads=["y1h1"], writes=["y1h1"])

    def ffn_f2head(s):
        par = s % 2
        sch.op(SY, lambda e, s=s, par=par: e.dma_start(
            ppart[:, par * D:(par + 1) * D],
            bass.AP(partial_d, s * P * D, [[D, P], [1, D]])),
            reads=["d_part%d" % s], writes=["ppart%d" % par], sem="DPR", amt=16)
        sch.op(V, lambda e: e.tensor_reduce(sc2[:, 6:7], y1[:, 0:2048], AX.X, A.max,
                                            apply_absolute_value=True),
               reads=["y1h0"], writes=["sc2m0"])
        sch.op(V, lambda e: e.tensor_reduce(sc2[:, 7:8], y1[:, 2048:HID], AX.X, A.max,
                                            apply_absolute_value=True),
               reads=["y1h1"], writes=["sc2m1"])
        sch.op(V, lambda e: e.tensor_scalar(sc2[:, 1:2], sc2[:, 6:7], sc2[:, 7:8],
                                            1e-5, op0=A.max, op1=A.max),
               reads=["sc2m0", "sc2m1"], writes=["sc2b"])
        sch.op(V, lambda e: e.reciprocal(sc2[:, 2:3], sc2[:, 1:2]),
               reads=["sc2b"], writes=["sc2c"])
        sch.op(V, lambda e: e.tensor_scalar(sc2[:, 3:4], sc2[:, 2:3], 127.0, None, op0=A.mult),
               reads=["sc2c"], writes=["sc2d"])
        sch.op(V, lambda e: e.tensor_scalar(sc2[:, 4:5], sc2[:, 3:4], swb[:, 3:4], None, op0=A.mult),
               reads=["sc2d", "sw2q"], writes=["sc2e"])
        sch.op(V, lambda e: e.reciprocal(sc2[:, 5:6], sc2[:, 4:5]),
               reads=["sc2e"], writes=["sc2f"])
        for h, kh, kx in ((0, "y1h0", "xq2tm0"), (1, "y1h1", "xq2tm1")):
            lo, hi = h * 2048, (h + 1) * 2048
            sch.op(S, lambda e, lo=lo, hi=hi: e.activation(
                y1[:, lo:hi], y1[:, lo:hi], AF.Identity,
                bias=MAG, scale=sc2[:, 3:4]),
                reads=[kh, "sc2d", "magic"], writes=[kh])
            sch.op(S, lambda e, lo=lo, hi=hi: e.activation(
                xq2tm[:, lo:hi], y1[:, lo:hi], AF.Identity,
                bias=MAGN, scale=1.0),
                reads=[kh, "magicn"], writes=[kx])

    def ffn_f2tailA(s):
        for r0 in range(0, KH, 4):
            for i in range(4):
                kb = r0 + i
                sch.op(T, lambda e, kb=kb, i=i: e.transpose(
                    ps_tx[:, i * P:(i + 1) * P], xq2tm[:, kb * P:(kb + 1) * P],
                    ident[:, :]),
                    reads=["xq2tm%d" % (kb // 16), "ident"], writes=["pstx"])
            sch.op(S, lambda e, r0=r0: e.copy(
                xq2fm[:, r0 * P:(r0 + 4) * P], ps_tx[:, 0:512]),
                reads=["pstx"], writes=["xq2fm"])

    def ffn_f2tailB(s):
        par = s % 2
        for n in range(2):
            pst, kpst = BD_PS[n % 2]

            def f2mm(e, n=n, pst=pst):
                ins = None
                for k in range(KH):
                    ins = e.matmul(
                        pst[:, 0:512],
                        xq2fm[:, k * P:(k + 1) * P],
                        wq2sb[:, k * D + n * 512:k * D + (n + 1) * 512],
                        start=(k == 0), stop=(k == KH - 1))
                return ins
            sch.op(T, f2mm, reads=["xq2fm"] + ["wq2sb_%d" % j_ for j_ in range(4)],
                   writes=[kpst])
            sch.op(V, lambda e, n=n, pst=pst, kpst=kpst: e.scalar_tensor_tensor(
                t2s[:, n * 512:(n + 1) * 512], pst[:, 0:512], sc2[:, 5:6],
                b2b[:, n * 512:(n + 1) * 512], op0=A.mult, op1=A.add),
                reads=[kpst, "sc2f", "b2b"], writes=["t2s"])
        sch.op(V, lambda e, par=par: e.tensor_add(
            ppart[:, par * D:(par + 1) * D], ppart[:, par * D:(par + 1) * D],
            t2s[:, 0:D]),
            reads=["t2s", "ppart%d" % par], writes=["ppart%d" % par])
        sch.op(SY, lambda e, s=s, par=par: e.dma_start(
            y[s * P:(s + 1) * P, :], ppart[:, par * D:(par + 1) * D]),
            reads=["ppart%d" % par], writes=["ppart%d" % par], sem="DO", amt=16)

    qload(0)
    qload(1)
    for s in range(8):
        quantize1(s)
        qload(s + 2)
    xq1_read(0)
    ffn_f1(0)
    ffn_f1gelu(0)
    for s in range(NT):
        ffn_f2head(s)
        if s % 4 == 0 and (s + 4) // 4 < NT // 4:
            xq1_read((s + 4) // 4)
        if s + 1 < NT:
            ffn_f1(s + 1)
        ffn_f2tailA(s)
        if s + 1 < NT:
            ffn_f1gelu(s + 1)
        if 8 + s < NT:
            quantize1(8 + s)
        if 10 + s < NT:
            qload(10 + s)
        ffn_f2tailB(s)

    sch.wait_only(SY, [("DO", sch.semc["DO"])])

    with nc.Block() as block:
        @block.sync
        def _(e):
            for f in sch.thunks["sync"]:
                f(e)

        @block.scalar
        def _(e):
            for f in sch.thunks["scalar"]:
                f(e)

        @block.vector
        def _(e):
            for f in sch.thunks["vector"]:
                f(e)

        @block.tensor
        def _(e):
            for f in sch.thunks["tensor"]:
                f(e)

        @block.gpsimd
        def _(e):
            for f in sch.thunks["gpsimd"]:
                f(e)

    es.close()
    return nc


_CACHED = [None]


def kernel(**inputs):
    xq = np.ascontiguousarray(inputs["L0_attn_query"], dtype=np.float32)
    B, TT, DD = xq.shape
    assert (B, TT, DD) == (4, 8192, 1024)
    if _CACHED[0] is None:
        _CACHED[0] = build_program()
    nc = _CACHED[0]
    common = {
        "W_qv": np.ascontiguousarray(inputs["L0_attn_W_qv"], np.float32),
        "W2_0": np.ascontiguousarray(inputs["L0_attn_W2_0"], np.float32),
        "W2_1": np.ascontiguousarray(inputs["L0_attn_W2_1"], np.float32),
        "SCW": np.ascontiguousarray(inputs["L0_attn_scale_weights"], np.float32).reshape(1, 2),
        "W_out": np.ascontiguousarray(inputs["L0_attn_W_out"], np.float32),
        "w1": np.ascontiguousarray(inputs["L0_ffn_w1"], np.float32),
        "b1": np.ascontiguousarray(inputs["L0_ffn_b1"], np.float32).reshape(1, HID),
        "w2": np.ascontiguousarray(inputs["L0_ffn_w2"], np.float32),
        "b2": np.ascontiguousarray(inputs["L0_ffn_b2"], np.float32).reshape(1, D),
        "gam": np.ascontiguousarray(inputs["L0_ffn_gamma"], np.float32).reshape(1, D),
        "bet": np.ascontiguousarray(inputs["L0_ffn_bet" + "a"], np.float32).reshape(1, D),
    }
    in_maps = []
    for c in range(8):
        b, h = c // 2, c % 2
        t0 = h * 4096
        xe = np.zeros((TIN, D), np.float32)
        lo, hi = t0 - P, t0 + 4096 + P
        s_, e_ = max(lo, 0), min(hi, TT)
        xe[s_ - lo:s_ - lo + (e_ - s_)] = xq[b, s_:e_]
        m = dict(common)
        m["x_in"] = xe
        in_maps.append(m)
    res = run_bass_kernel_spmd(nc, in_maps, core_ids=list(range(8)))
    out = np.empty((B, TT, DD), np.float32)
    for c in range(8):
        b, h = c // 2, c % 2
        out[b, h * 4096:(h + 1) * 4096] = res.results[c]["y"]
    return out


# version 28
# speedup vs baseline: 1.1069x; 1.0863x over previous
import contextlib
import numpy as np

import concourse.bass as bass
import concourse.mybir as mybir
from concourse.bass_utils import run_bass_kernel_spmd

F32 = mybir.dt.float32
BF16 = mybir.dt.bfloat16
FP8 = mybir.dt.float8e4
A = mybir.AluOpType
AF = mybir.ActivationFunctionType
AX = mybir.AxisListType

P = 128
D = 1024
KC = 8
NT = 32
NB = 34
TIN = NB * P
C0, C1, HALF = 21, 41, 20
NH, HD = 4, 256
HID = 4096
KH = 32
Z = 235
RR = C1 + Z
ROWP = 4 * RR + 1
SKG = 1024
SKEWN = SKG + 128 * ROWP + SKG
SKEW_TOT = ((2 * SKEWN + 4096 + 2047) // 2048) * 2048
K_CONST = 1.5 * 2.0 ** 82
EPS_PROBE = 2.0 ** -30
MAGIC_LO = 1.5 * 2.0 ** 23

ENGINES = ("sync", "scalar", "vector", "tensor", "gpsimd")
DMA_SEMS = ("DX1", "DX2", "DX3", "DW", "DSKW", "DSKR0", "DSKR1", "DSKR2",
            "DAW", "DAR0", "DAR1", "DAR2", "DAR3",
            "DQW0", "DQW1", "DQR0", "DQR1", "DQR2", "DQR3", "DP", "DPR", "DO")


class Sched:

    def __init__(self, nc, es):
        self.nc = nc
        self.es = es
        self.thunks = {e: [] for e in ENGINES}
        self.semh = {}
        self.semc = {}
        self.lastwait = {e: {} for e in ENGINES}
        self.lasttok = {e: None for e in ENGINES}
        self.wmap = {}
        self.rmap = {}
        self.barrier = None

    def sem(self, name):
        self.semh[name] = self.es.enter_context(self.nc.semaphore(name))
        self.semc[name] = 0

    def op(self, eng, fn, reads=(), writes=(), extra=(), sem=None, amt=1):
        if amt == 16 and sem is not None:
            writes = list(writes) + ["_stream_" + sem]
        waits = []
        for k in reads:
            waits.append(self.wmap.get(k))
        for k in writes:
            waits.append(self.wmap.get(k))
            waits.extend(self.rmap.get(k, ()))
        waits.extend(extra)
        if self.barrier is not None:
            waits.append(self.barrier)
        wl = []
        seen = set()
        for t in waits:
            if t is None or t in seen:
                continue
            seen.add(t)
            s, v = t
            assert v <= self.semc[s], (s, v, self.semc[s])
            wl.append((s, v))
        if sem is None:
            sem = {"sync": "SY", "scalar": "S", "vector": "V",
                   "tensor": "T", "gpsimd": "G"}[eng]
        self.semc[sem] += amt
        tok = (sem, self.semc[sem])
        lw = self.lastwait[eng]
        wl2 = [(s, v) for (s, v) in wl if lw.get(s, -1) < v]
        for s, v in wl:
            if lw.get(s, -1) < v:
                lw[s] = v
        semh = self.semh

        def thunk(e, wl2=wl2, fn=fn, sem=sem, amt=amt):
            for s, v in wl2:
                e.wait_ge(semh[s], v)
            if fn is not None:
                ins = fn(e)
                ins.then_inc(semh[sem], amt)
            else:
                e.nop().then_inc(semh[sem], amt)

        self.thunks[eng].append(thunk)
        for k in reads:
            self.rmap.setdefault(k, []).append(tok)
        for k in writes:
            self.wmap[k] = tok
            self.rmap[k] = []
        self.lasttok[eng] = tok
        return tok

    def wait_only(self, eng, extra):
        semh = self.semh
        wl = [t for t in extra if t is not None]

        def thunk(e, wl=wl):
            for s, v in wl:
                e.wait_ge(semh[s], v)

        self.thunks[eng].append(thunk)

    def full_barrier(self):
        dma_tots = [(s, self.semc[s]) for s in DMA_SEMS]
        for eng in ENGINES:
            ex = [self.lasttok[eng]]
            if eng == "sync":
                ex += dma_tots
            self.op(eng, None, extra=ex, sem="PB")
        self.barrier = ("PB", self.semc["PB"])


def build_program():
    nc = bass.Bass("TRN2", target_bir_lowering=False)
    es = contextlib.ExitStack()

    x_in = nc.dram_tensor("x_in", [TIN, D], F32, kind="ExternalInput")
    W_qv = nc.dram_tensor("W_qv", [D, 2 * D], F32, kind="ExternalInput")
    W2_0 = nc.dram_tensor("W2_0", [D, NH * C0], F32, kind="ExternalInput")
    W2_1 = nc.dram_tensor("W2_1", [D, NH * C1], F32, kind="ExternalInput")
    SCW = nc.dram_tensor("SCW", [1, 2], F32, kind="ExternalInput")
    W_out = nc.dram_tensor("W_out", [D, D], F32, kind="ExternalInput")
    w1 = nc.dram_tensor("w1", [HID, D], F32, kind="ExternalInput")
    b1 = nc.dram_tensor("b1", [1, HID], F32, kind="ExternalInput")
    w2 = nc.dram_tensor("w2", [D, HID], F32, kind="ExternalInput")
    b2 = nc.dram_tensor("b2", [1, D], F32, kind="ExternalInput")
    gam = nc.dram_tensor("gam", [1, D], F32, kind="ExternalInput")
    bet = nc.dram_tensor("bet", [1, D], F32, kind="ExternalInput")
    y = nc.dram_tensor("y", [NT * P, D], F32, kind="ExternalOutput")

    partial_d = nc.dram_tensor("partial_d", [NT * P * D], F32, kind="Internal")
    wq1t_d = nc.dram_tensor("wq1t_d", [KC * P * HID], FP8, kind="Internal")
    wq2t_d = nc.dram_tensor("wq2t_d", [KH * P * D], FP8, kind="Internal")
    skew_d = nc.dram_tensor("skew_d", [SKEW_TOT], BF16, kind="Internal")
    amix_d = nc.dram_tensor("amix_d", [2 * 4 * P * D], BF16, kind="Internal")
    xq1_d = nc.dram_tensor("xq1_d", [4 * 4 * P * D], BF16, kind="Internal")

    SB_BASE = 16512 + 2048
    SB_TOP = 16512 + 212863
    cursor = [SB_BASE]

    def salloc(name, shape, dtype, at=None):
        nb = mybir.dt.size(dtype)
        for s_ in shape[1:]:
            nb *= s_
        if at is None:
            at = cursor[0]
        else:
            at = at + SB_BASE
        t = nc.alloc_sbuf_tensor_at(name, shape, dtype, offset=at)
        assert at + nb <= SB_TOP, (name, at + nb, SB_TOP)
        cursor[0] = max(cursor[0], (at + nb + 127) // 128 * 128)
        return t

    ident = salloc("ident", [P, P], BF16)
    iot = salloc("iot", [P, P], F32)
    magic = salloc("magic", [P, 4], F32)
    cst = salloc("cst", [P, 4], F32)
    swb = salloc("swb", [P, 8], F32)
    onesrow = salloc("onesrow", [1, P], F32)
    onescol = salloc("onescol", [P, 1], F32)
    swrow = salloc("swrow", [1, 8], F32)
    b1row = salloc("b1row", [1, HID], F32)
    b2row = salloc("b2row", [1, D], F32)
    gamrow = salloc("gamrow", [1, D], F32)
    betrow = salloc("betrow", [1, D], F32)
    st_st = salloc("st_st", [P, 2 * NB], F32)
    st_r1 = salloc("st_r1", [P, NB], F32)
    st2_st = salloc("st2_st", [P, 2 * NT], F32)
    st_r2 = salloc("st_r2", [P, NT], F32)
    inv1t = salloc("inv1t", [P, NT], F32)
    bs = salloc("bs", [P, 24], F32)
    sc0 = salloc("sc0", [1, 8], F32)
    sc1 = salloc("sc1", [P, 8], F32)
    sc2 = salloc("sc2", [P, 8], F32)
    negmax = salloc("negmax", [P, 16], F32)
    ssum = salloc("ssum", [P, 16], F32)
    lnsm = salloc("lnsm", [P, 16], F32)
    bias2 = salloc("bias2", [P, 8], F32)
    e0b = salloc("e0b", [P, NH * C0], BF16)
    lnswb = salloc("lnswb", [P, 2], F32)
    RES_END = cursor[0]

    wqv_b = salloc("wqv_b", [P, KC * 2048], BF16)
    w2c_b = salloc("w2c_b", [P, KC * 248], BF16)
    wout_b = salloc("wout_b", [P, KC * D], BF16)
    AACT_BASE = cursor[0]
    xld = salloc("xld", [P, 3 * D], F32)
    xn32 = salloc("xn32", [P, 3 * D], F32)
    xnfm = salloc("xnfm", [P, KC * 8 * P], BF16)
    vtm = salloc("vtm", [P, 9 * D], BF16)
    rqfm = salloc("rqfm", [P, 2 * KC * 512], BF16)
    probs = salloc("probs", [P, 2 * 4 * RR], BF16)
    expt = salloc("expt", [P, 2 * 248], F32)
    pmrg = salloc("pmrg", [P, NH * C1], F32)
    stt = salloc("stt", [P, 4 * 3 * 512], BF16)
    amix_tm = salloc("amix_tm", [P, 2 * D], BF16)
    amix_fm = salloc("amix_fm", [P, KC * 512], BF16)
    x2 = salloc("x2", [P, 2 * D], F32)
    pout = salloc("pout", [P, 2 * D], F32)
    A_END = cursor[0]
    assert A_END <= SB_TOP, A_END

    AB = AACT_BASE - SB_BASE
    wq1asm = salloc("wq1asm", [P, KC * HID], FP8, at=AB)
    wq2asm = salloc("wq2asm", [P, KH * D], FP8, at=AB + 32 * 1024)
    wstage = salloc("wstage", [P, 2 * HID], F32, at=AB + 64 * 1024)
    wqt = salloc("wqt", [P, HID + D], F32, at=AB + 96 * 1024)
    zbf = salloc("zbf", [P, 2048], BF16, at=AB + 116 * 1024)

    ccur = [RES_END - SB_BASE]

    def calloc(name, shape, dtype):
        t = salloc(name, shape, dtype, at=ccur[0])
        nb = mybir.dt.size(dtype)
        for s_ in shape[1:]:
            nb *= s_
        ccur[0] = (ccur[0] + nb + 127) // 128 * 128
        return t

    wq1sb = calloc("wq1sb", [P, KC * HID], FP8)
    wq2sb = calloc("wq2sb", [P, KH * D], FP8)
    b1b = calloc("b1b", [P, HID], F32)
    b2b = calloc("b2b", [P, D], F32)
    gamb = calloc("gamb", [P, D], F32)
    betb = calloc("betb", [P, D], F32)
    xld3 = calloc("xld3", [P, 2 * D], F32)
    w3 = calloc("w3", [P, 2 * D], F32)
    xq1tm = calloc("xq1tm", [P, 2 * D], BF16)
    xq1fm = calloc("xq1fm", [P, 2 * KC * 512], BF16)
    y1 = calloc("y1", [P, HID], F32)
    xq2tm = calloc("xq2tm", [P, KH * P], BF16)
    xq2fm = calloc("xq2fm", [P, KH * P], BF16)
    ppart = calloc("ppart", [P, 2 * D], F32)
    t2s = calloc("t2s", [P, D], F32)
    C_END = ccur[0] + SB_BASE
    assert C_END <= SB_TOP, C_END

    psA = es.enter_context(nc.psum_tensor("psA", [P, 512], F32))
    psB = es.enter_context(nc.psum_tensor("psB", [P, 512], F32))
    ql = es.enter_context(nc.psum_tensor("ql", [P, 512], F32))
    bandA = es.enter_context(nc.psum_tensor("bandA", [P, 512], F32))
    bandB = es.enter_context(nc.psum_tensor("bandB", [P, 512], F32))
    woutA = es.enter_context(nc.psum_tensor("woutA", [P, 512], F32))
    woutB = es.enter_context(nc.psum_tensor("woutB", [P, 512], F32))
    ps_tx = es.enter_context(nc.psum_tensor("ps_tx", [P, 512], BF16))

    AB_PS = [(psA, "psA"), (psB, "psB")]
    WT_PS = [(woutA, "woutA"), (woutB, "woutB")]
    BD_PS = [(bandA, "bandA"), (bandB, "bandB")]
    F1_PS = AB_PS + [(ql, "ql")] + WT_PS

    sch = Sched(nc, es)
    for s_ in ("V", "S", "T", "G", "SY", "PB") + DMA_SEMS:
        sch.sem(s_)
    V, S, T, SY, G = "vector", "scalar", "tensor", "sync", "gpsimd"
    MAG = magic[:, 0:1]
    MAGN = magic[:, 3:4]

    sch.op(G, lambda e: e.iota(iot[:, :], [[1, P]], channel_multiplier=-1,
                               allow_small_or_imprecise_dtypes=True), writes=["iot"])
    sch.op(V, lambda e: e.tensor_scalar(ident[:, :], iot[:, :], 0.0, None, op0=A.is_equal),
           reads=["iot"], writes=["ident"])
    sch.op(V, lambda e: e.tensor_scalar(iot[:, :], iot[:, :], 0.0, None, op0=A.is_equal),
           reads=["iot"], writes=["iot"])
    sch.op(V, lambda e: e.memset(cst[:, 0:1], 1e-5), writes=["cst"])
    sch.op(V, lambda e: e.memset(magic[:, 1:2], 1.0), writes=["mg1"])
    sch.op(V, lambda e: e.tensor_scalar(magic[:, 2:3], magic[:, 1:2], EPS_PROBE, None, op0=A.add),
           reads=["mg1"], writes=["mg2"])
    sch.op(V, lambda e: e.tensor_scalar(magic[:, 1:2], magic[:, 2:3], 1.0, None, op0=A.subtract),
           reads=["mg2"], writes=["mg1"])
    sch.op(V, lambda e: e.tensor_scalar(magic[:, 0:1], magic[:, 1:2], K_CONST, MAGIC_LO,
                                        op0=A.mult, op1=A.add),
           reads=["mg1"], writes=["magic"])
    sch.op(V, lambda e: e.tensor_scalar(magic[:, 3:4], magic[:, 0:1], -1.0, None, op0=A.mult),
           reads=["magic"], writes=["magicn"])
    sch.op(V, lambda e: e.memset(zbf[:, :], 0.0), writes=["zbf"])
    sch.op(V, lambda e: e.memset(onesrow[:, :], 1.0), writes=["onesrow"])
    sch.op(V, lambda e: e.memset(onescol[:, :], 1.0), writes=["onescol"])

    done = 0
    while done < SKEW_TOT:
        n = min(P * 2048, SKEW_TOT - done)
        rows = (n + 2047) // 2048
        sch.op(SY, lambda e, off=done, rows=rows: e.dma_start(
            bass.AP(skew_d, off, [[2048, rows], [1, 2048]]), zbf[0:rows, :]),
            reads=["zbf"], writes=["d_skew0", "d_skew1"], sem="DSKW", amt=16)
        done += rows * 2048

    sch.op(SY, lambda e: e.dma_start(swrow[0:1, 0:2], SCW[:, :]), writes=["swrow"], sem="DP", amt=16)
    sch.op(V, lambda e: e.tensor_reduce(sc0[0:1, 0:1], swrow[0:1, 0:2], AX.X, A.max, negate=True),
           reads=["swrow"], writes=["sc0a"])
    sch.op(S, lambda e: e.activation(sc0[0:1, 2:4], swrow[0:1, 0:2], AF.Exp,
                                     bias=sc0[0:1, 0:1], scale=1.0),
           reads=["swrow", "sc0a"], writes=["sc0b"])
    sch.op(V, lambda e: e.tensor_reduce(sc0[0:1, 1:2], sc0[0:1, 2:4], AX.X, A.add),
           reads=["sc0b"], writes=["sc0c"])
    sch.op(V, lambda e: e.reciprocal(sc0[0:1, 4:5], sc0[0:1, 1:2]),
           reads=["sc0c"], writes=["sc0d"])
    sch.op(V, lambda e: e.tensor_scalar(sc0[0:1, 6:8], sc0[0:1, 2:4], sc0[0:1, 4:5], None, op0=A.mult),
           reads=["sc0b", "sc0d"], writes=["sc0e"])
    sch.op(T, lambda e: e.matmul(ql[:, 0:2], onesrow[0:1, :], sc0[0:1, 6:8],
                                 start=True, stop=True),
           reads=["sc0e", "onesrow"], writes=["ql"])
    sch.op(S, lambda e: e.copy(swb[:, 0:2], ql[:, 0:2]), reads=["ql"], writes=["swb01"])
    sch.op(S, lambda e: e.activation(lnswb[:, 0:2], swb[:, 0:2], AF.Ln, bias=0.0, scale=1.0),
           reads=["swb01"], writes=["lnswb"])

    p0s = ["DW", "DAW", "DQW0", "DAR0", "DAR1", "DAR2", "DAR3", "DQR1"]
    NP0 = len(p0s)

    sch.op(V, lambda e: e.memset(magic[:, 1:2], 0.0), reads=["mg2"], writes=["mg1"])
    sch.op(V, lambda e: e.memset(cst[:, 1:2], 0.0), writes=["c1"])

    def flat_tile(wsrc, t, fi, acc_sl, red_sl, ka, kr):
        so = (fi % 2) * HID
        ws = "wst%d" % (fi % 2)
        sch.op(SY, lambda e, t=t, so=so, wsrc=wsrc: e.dma_start(
            wstage[:, so:so + HID],
            bass.AP(wsrc, t * P * HID, [[HID, P], [1, HID]])),
            writes=[ws], sem=p0s[fi % NP0], amt=16)
        sch.op(V, lambda e, so=so, red_sl=red_sl: e.tensor_reduce(
            red_sl, wstage[:, so:so + HID], AX.X, A.add,
            apply_absolute_value=True),
            reads=[ws], writes=[kr])
        sch.op(V, lambda e, acc_sl=acc_sl, red_sl=red_sl: e.tensor_add(
            acc_sl, acc_sl, red_sl),
            reads=[kr, ka], writes=[ka])

    fi = 0
    for t in range(KC):
        flat_tile(w1, t, fi, magic[:, 1:2], magic[:, 2:3], "mg1", "mg2"); fi += 1
        flat_tile(w2, t, fi, cst[:, 1:2], cst[:, 2:3], "c1", "c2"); fi += 1

    def scale_chain(acc_sl, ka, nel, swcol, swkey):
        sch.op(T, lambda e, acc_sl=acc_sl: e.matmul(
            ql[0:1, 4:5], acc_sl, onescol[:, 0:1], start=True, stop=True),
            reads=[ka, "onescol"], writes=["ql"])
        sch.op(S, lambda e: e.copy(sc0[0:1, 5:6], ql[0:1, 4:5]), reads=["ql"], writes=["sc0f"])
        sch.op(V, lambda e, nel=nel: e.tensor_scalar(sc0[0:1, 5:6], sc0[0:1, 5:6],
                                                     1.0 / nel, 1e-5,
                                                     op0=A.mult, op1=A.max),
               reads=["sc0f"], writes=["sc0f"])
        sch.op(V, lambda e: e.reciprocal(sc0[0:1, 5:6], sc0[0:1, 5:6]),
               reads=["sc0f"], writes=["sc0f"])
        sch.op(T, lambda e: e.matmul(ql[:, 5:6], onesrow[0:1, :], sc0[0:1, 5:6],
                                     start=True, stop=True),
               reads=["sc0f", "onesrow"], writes=["ql"])
        sch.op(S, lambda e, swcol=swcol: e.copy(swb[:, swcol:swcol + 1], ql[:, 5:6]),
               reads=["ql"], writes=[swkey])

    scale_chain(magic[:, 1:2], "mg1", float(KH * P * D), 2, "sw1q")
    scale_chain(cst[:, 1:2], "c1", float(KC * P * HID), 3, "sw2q")
    sch.op(V, lambda e: e.tensor_scalar(magic[:, 1:2], magic[:, 0:1], 1.0, None,
                                        op0=A.add),
           reads=["magic"], writes=["mg1", "mp1"])
    sch.op(V, lambda e: e.tensor_scalar(magic[:, 2:3], magic[:, 0:1], 1.0, None,
                                        op0=A.subtract),
           reads=["magic"], writes=["mg2", "mm1"])

    def ld_cast(dram, cols, dst, dstoff, kidx, wkey, slot):
        so = slot * HID
        sch.op(SY, lambda e: e.dma_start(wstage[:, so:so + cols], dram[kidx * P:(kidx + 1) * P, :]),
               writes=["wst%d" % slot], sem=p0s[(kidx + slot) % NP0], amt=16)
        sch.op(V, lambda e: e.tensor_copy(dst[:, dstoff:dstoff + cols], wstage[:, so:so + cols]),
               reads=["wst%d" % slot], writes=[wkey])

    sl = 0
    for k in range(KC):
        ld_cast(W_qv, 2048, wqv_b, k * 2048, k, "wqv", sl % 2); sl += 1
    for k in range(KC):
        ld_cast(W_out, D, wout_b, k * D, k, "wout", sl % 2); sl += 1
    for k in range(KC):
        ld_cast(W2_0, NH * C0, w2c_b, k * 248, k, "w2c", sl % 2); sl += 1
        ld_cast(W2_1, NH * C1, w2c_b, k * 248 + NH * C0, k, "w2c", sl % 2); sl += 1

    sch.op(SY, lambda e: e.dma_start(b1row[:, :], b1[:, :]), writes=["b1row"], sem="DP", amt=16)
    sch.op(SY, lambda e: e.dma_start(b2row[:, :], b2[:, :]), writes=["b2row"], sem="DP", amt=16)
    sch.op(SY, lambda e: e.dma_start(gamrow[:, :], gam[:, :]), writes=["gamrow"], sem="DP", amt=16)
    sch.op(SY, lambda e: e.dma_start(betrow[:, :], bet[:, :]), writes=["betrow"], sem="DP", amt=16)


    pend = []
    evct = [0]

    def flush_evacs(keep):
        while len(pend) > keep:
            fn, kpst, asmkey = pend.pop(0)
            if evct[0] % 2 == 0:
                sch.op(S, lambda e, fn=fn: fn(e, 0), reads=[kpst, "magicn"],
                       writes=[asmkey])
            else:
                sch.op(V, lambda e, fn=fn: fn(e, 1), reads=[kpst, "magic"],
                       writes=[asmkey])
            evct[0] += 1

    def quant_tile(wsrc, t, ti, cols, swcol, swkey, asmdst, asmkey, asm_k_step,
                   wqo, wqk, stg, stgname):
        so = (ti % 2) * HID
        ws = "%s%d" % (stgname, ti % 2)
        sch.op(SY, lambda e, t=t, so=so, cols=cols, wsrc=wsrc, stg=stg: e.dma_start(
            stg[:, so:so + cols], wsrc[t * P:(t + 1) * P, :]),
            writes=[ws], sem=p0s[ti % NP0], amt=16)
        for c in range(cols // 512):
            co = c * 512
            sch.op(S, lambda e, so=so, co=co, swcol=swcol, stg=stg: e.activation(
                stg[:, so + co:so + co + 512], stg[:, so + co:so + co + 512],
                AF.Identity, bias=MAG, scale=swb[:, swcol:swcol + 1]),
                reads=[ws, swkey, "magic"], writes=[ws])
            wqkc = "%s_%d" % (wqk, c % 4)
            sch.op(V, lambda e, so=so, co=co, wqo=wqo, stg=stg: e.tensor_scalar(
                wqt[:, wqo + co:wqo + co + 512], stg[:, so + co:so + co + 512],
                magic[:, 1:2], magic[:, 2:3], op0=A.min, op1=A.max),
                reads=[ws, "mp1", "mm1"], writes=[wqkc])
            pst, kpst = PH0_PS[(ti * 8 + c) % 6]

            def trq(e, c=c, wqo=wqo, pst=pst):
                for i in range(4):
                    kb = c * 4 + i
                    ins = e.transpose(
                        pst[:, i * P:(i + 1) * P],
                        wqt[:, wqo + kb * P:wqo + (kb + 1) * P], iot[:, :])
                return ins
            sch.op(T, trq, reads=[wqkc, "iot"], writes=[kpst])

            def evac(e, eng, t=t, c=c, asmdst=asmdst, asm_k_step=asm_k_step,
                     pst=pst):
                dst = bass.AP(asmdst, c * 4 * asm_k_step + t * P,
                              [[asmdst.shape[1], P], [asm_k_step, 4], [1, P]])
                if eng == 0:
                    return e.activation(dst, pst[:, 0:512], AF.Identity,
                                        bias=MAGN, scale=1.0)
                return e.tensor_scalar(dst, pst[:, 0:512], MAG, None,
                                       op0=A.subtract)
            pend.append((evac, kpst, asmkey))
            flush_evacs(2)

    PH0_PS = AB_PS + BD_PS + WT_PS
    ti = 0
    for g in range(KC):
        for i in range(4):
            quant_tile(w1, 4 * g + i, ti, D, 2, "sw1q", wq1asm, "wq1asm", HID,
                       HID, "wqt1", wstage, "wst"); ti += 1
        quant_tile(w2, g, ti, HID, 3, "sw2q", wq2asm, "wq2asm", D,
                   0, "wqt2", wstage, "wst"); ti += 1

    flush_evacs(0)
    sch.op(SY, lambda e: e.dma_start(bass.AP(wq1t_d, 0, [[KC * HID, P], [1, KC * HID]]),
                                     wq1asm[:, :]),
           reads=["wq1asm"], writes=["d_wq1"], sem="DW", amt=16)
    sch.op(SY, lambda e: e.dma_start(bass.AP(wq2t_d, 0, [[KH * D, P], [1, KH * D]]),
                                     wq2asm[:, :]),
           reads=["wq2asm"], writes=["d_wq2"], sem="DW", amt=16)

    def vload(eidx):
        t = eidx + 1
        par = t % 3
        sch.op(G, lambda e, t=t, par=par: e.dma_start(
            xld[:, par * D:(par + 1) * D], x_in[t * P:(t + 1) * P, :]),
            reads=["d_wq1", "d_wq2"],
            writes=["xld%d" % par], sem="DX1", amt=16)

    def vpath_ln(eidx):
        t = eidx + 1
        par = t % 3
        kx = "xld%d" % par
        kn = "xn32%d" % (t % 3)
        kb2 = "bs%d" % (t % 2)
        bso = (t % 2) * 12
        sch.op(V, lambda e: e.bn_stats(bs[:, bso:bso + 6], xld[:, par * D:par * D + 512]),
               reads=[kx], writes=[kb2])
        sch.op(V, lambda e: e.bn_stats(bs[:, bso + 6:bso + 12], xld[:, par * D + 512:par * D + D]),
               reads=[kx], writes=[kb2])
        sch.op(V, lambda e: e.bn_aggr(st_st[:, 2 * t:2 * t + 2], bs[:, bso:bso + 12]),
               reads=[kb2], writes=["st%d" % t])
        sch.op(S, lambda e: e.activation(st_r1[:, t:t + 1], st_st[:, 2 * t + 1:2 * t + 2],
                                         AF.Ln, bias=cst[:, 0:1], scale=1.0),
               reads=["st%d" % t, "cst"], writes=["str%d" % t])
        sch.op(S, lambda e: e.activation(st_r1[:, t:t + 1], st_r1[:, t:t + 1],
                                         AF.Exp, bias=0.0, scale=-0.5),
               reads=["str%d" % t], writes=["str%d" % t])
        sch.op(V, lambda e: e.tensor_scalar(xn32[:, (t % 3) * D:(t % 3 + 1) * D],
                                            xld[:, par * D:(par + 1) * D],
                                            st_st[:, 2 * t:2 * t + 1], st_r1[:, t:t + 1],
                                            op0=A.subtract, op1=A.mult),
               reads=[kx, "st%d" % t, "str%d" % t], writes=[kn])
        if 0 <= eidx < NT:
            s = eidx
            sch.op(V, lambda e: e.bn_stats(bs[:, bso:bso + 6],
                                           xn32[:, (t % 3) * D:(t % 3) * D + 512]),
                   reads=[kn], writes=[kb2])
            sch.op(V, lambda e: e.bn_stats(bs[:, bso + 6:bso + 12],
                                           xn32[:, (t % 3) * D + 512:(t % 3) * D + D]),
                   reads=[kn], writes=[kb2])
            sch.op(V, lambda e: e.bn_aggr(st2_st[:, 2 * s:2 * s + 2], bs[:, bso:bso + 12]),
                   reads=[kb2], writes=["st2_%d" % s])
            sch.op(S, lambda e: e.activation(st_r2[:, s:s + 1], st2_st[:, 2 * s + 1:2 * s + 2],
                                             AF.Ln, bias=cst[:, 0:1], scale=1.0),
                   reads=["st2_%d" % s, "cst"], writes=["st2r%d" % s])
            sch.op(S, lambda e: e.activation(st_r2[:, s:s + 1], st_r2[:, s:s + 1],
                                             AF.Exp, bias=0.0, scale=-0.5),
                   reads=["st2r%d" % s], writes=["st2r%d" % s])

    def vpath_tv(eidx):
        t = eidx + 1
        slot = eidx % 8
        slot9 = eidx % 9
        kn = "xn32%d" % (t % 3)
        kf = "xnfm%d" % slot
        kv = "vtm%d" % slot9
        for r0 in (0, 4):
            pst, kpst = AB_PS[(eidx * 3 + r0 // 4) % 2]

            def trq(e, r0=r0, pst=pst, t=t):
                for i in range(4):
                    kb = r0 + i
                    ins = e.transpose(
                        pst[:, i * P:(i + 1) * P],
                        xn32[:, (t % 3) * D + kb * P:(t % 3) * D + (kb + 1) * P],
                        iot[:, :])
                return ins
            sch.op(T, trq, reads=[kn, "iot"], writes=[kpst])

            def evacx(e, r0=r0, slot=slot, pst=pst):
                dst = bass.AP(xnfm, r0 * (8 * P) + slot * P,
                              [[KC * 8 * P, P], [8 * P, 4], [1, P]])
                return e.tensor_copy(dst, pst[:, 0:512])
            sch.op(V, evacx, reads=[kpst], writes=[kf])
        for n in range(2):
            pst, kpst = AB_PS[(eidx * 3 + 2 + n) % 2]

            def vmm(e, slot=slot, n=n, pst=pst):
                ins = None
                for k in range(KC):
                    ins = e.matmul(
                        pst[:, 0:512],
                        xnfm[:, k * (8 * P) + slot * P:k * (8 * P) + (slot + 1) * P],
                        wqv_b[:, k * 2048 + D + n * 512:k * 2048 + D + (n + 1) * 512],
                        start=(k == 0), stop=(k == KC - 1))
                return ins
            sch.op(T, vmm, reads=[kf, "wqv"], writes=[kpst])
            sch.op(S, lambda e, slot9=slot9, n=n, pst=pst: e.copy(
                vtm[:, slot9 * D + n * 512:slot9 * D + (n + 1) * 512], pst[:, 0:512]),
                reads=[kpst], writes=[kv])

    def qpath(quad):
        qpar = quad % 2
        slot0 = (4 * quad) % 8
        kfs = ["xnfm%d" % ((4 * quad + i) % 8) for i in range(4)]
        for m in range(KC):
            pst, kpst = WT_PS[m % 2]

            def qmm(e, m=m, slot0=slot0, pst=pst):
                ins = None
                for k in range(KC):
                    ins = e.matmul(
                        pst[:, 0:512],
                        wqv_b[:, k * 2048 + m * P:k * 2048 + (m + 1) * P],
                        xnfm[:, k * (8 * P) + slot0 * P:k * (8 * P) + slot0 * P + 512],
                        start=(k == 0), stop=(k == KC - 1))
                return ins
            sch.op(T, qmm, reads=kfs + ["wqv"], writes=[kpst])
            sch.op(V, lambda e, m=m, qpar=qpar, pst=pst: e.tensor_scalar(
                rqfm[:, qpar * 4096 + m * 512:qpar * 4096 + (m + 1) * 512],
                pst[:, 0:512], 0.0, None, op0=A.max),
                reads=[kpst], writes=["rqfm%d" % qpar])

    def attn1a(s):
        par = s % 2
        quad = s // 4
        qpar = quad % 2
        eo = par * 248

        def lmm(e, s=s, qpar=qpar):
            ins = None
            for k in range(KC):
                ins = e.matmul(
                    ql[:, 0:248],
                    rqfm[:, qpar * 4096 + k * 512 + (s % 4) * P:
                         qpar * 4096 + k * 512 + (s % 4 + 1) * P],
                    w2c_b[:, k * 248:(k + 1) * 248],
                    start=(k == 0), stop=(k == KC - 1))
            return ins
        sch.op(T, lmm, reads=["rqfm%d" % qpar, "w2c"], writes=["ql"])
        sch.op(V, lambda e, eo=eo: e.tensor_copy(expt[:, eo:eo + 248], ql[:, 0:248]),
               reads=["ql"], writes=["expt%d" % par])
        sch.op(V, lambda e, eo=eo, par=par: e.tensor_reduce(
            bass.AP(negmax, par * 8, [[16, P], [1, 4]]),
            expt[:, eo:eo + 84].rearrange("p (h c) -> p h c", c=C0),
            AX.X, A.max, negate=True),
            reads=["expt%d" % par], writes=["negmax0_%d" % par])
        sch.op(V, lambda e, eo=eo, par=par: e.tensor_reduce(
            bass.AP(negmax, par * 8 + 4, [[16, P], [1, 4]]),
            expt[:, eo + 84:eo + 248].rearrange("p (h c) -> p h c", c=C1),
            AX.X, A.max, negate=True),
            reads=["expt%d" % par], writes=["negmax1_%d" % par])

    def attn1b1(s):
        par = s % 2
        eo = par * 248
        no = par * 8
        groups = [(h, 0, h * C0, C0) for h in range(NH)] + \
                 [(h, 1, NH * C0 + h * C1, C1) for h in range(NH)]
        for gi, (h, ci, c0, cw) in enumerate(groups):
            sch.op(S, lambda e, gi=gi, c0=c0, cw=cw, eo=eo, no=no: e.activation(
                pmrg[:, 0:cw], expt[:, eo + c0:eo + c0 + cw], AF.Exp,
                bias=negmax[:, no + gi:no + gi + 1], scale=1.0,
                accum_out=ssum[:, no + gi:no + gi + 1]),
                reads=["expt%d" % par, "negmax0_%d" % par, "negmax1_%d" % par],
                writes=["pmgj", "ssum%d_%d" % (par, gi)])
        sch.op(S, lambda e, no=no: e.activation(lnsm[:, no:no + 8], ssum[:, no:no + 8],
                                                AF.Ln, bias=0.0, scale=1.0),
               reads=["ssum%d_%d" % (par, g_) for g_ in range(8)], writes=["lnsm%d" % par])

    def attn1b2(s):
        par = s % 2
        kp = "probs%d" % par
        kst = "stt%d" % (s % 4)
        eo = par * 248
        no = par * 8
        sch.op(V, lambda e, no=no: e.tensor_sub(bias2[:, 0:8], negmax[:, no:no + 8], lnsm[:, no:no + 8]),
               reads=["negmax0_%d" % par, "negmax1_%d" % par, "lnsm%d" % par], writes=["bias2"])
        sch.op(V, lambda e: e.tensor_scalar(bias2[:, 0:4], bias2[:, 0:4], lnswb[:, 0:1], None, op0=A.add),
               reads=["lnswb"], writes=["bias2"])
        sch.op(V, lambda e: e.tensor_scalar(bias2[:, 4:8], bias2[:, 4:8], lnswb[:, 1:2], None, op0=A.add),
               reads=["lnswb"], writes=["bias2"])
        for h in range(NH):
            sch.op(S, lambda e, h=h, par=par, eo=eo: e.activation(
                probs[:, par * (4 * RR) + h * RR:par * (4 * RR) + h * RR + C1],
                expt[:, eo + 84 + h * C1:eo + 84 + (h + 1) * C1], AF.Exp,
                bias=bias2[:, 4 + h:5 + h], scale=1.0),
                reads=["expt%d" % par, "bias2"], writes=[kp])
            sch.op(S, lambda e, h=h, eo=eo: e.activation(
                e0b[:, h * C0:(h + 1) * C0],
                expt[:, eo + h * C0:eo + (h + 1) * C0], AF.Exp,
                bias=bias2[:, h:h + 1], scale=1.0),
                reads=["expt%d" % par, "bias2"], writes=["e0b"])
        for h in range(NH):
            sch.op(V, lambda e, h=h, par=par: e.tensor_add(
                probs[:, par * (4 * RR) + h * RR + 10:par * (4 * RR) + h * RR + 31],
                probs[:, par * (4 * RR) + h * RR + 10:par * (4 * RR) + h * RR + 31],
                e0b[:, h * C0:(h + 1) * C0]),
                reads=["e0b"], writes=[kp])
        sch.op(SY, lambda e, par=par: e.dma_start(
            bass.AP(skew_d, par * SKEWN + SKG, [[ROWP, P], [1, 4 * RR]]),
            probs[:, par * (4 * RR):(par + 1) * (4 * RR)]),
            reads=[kp], writes=["d_skew%d" % par], sem="DSKW", amt=16)
        for g in range(3):
            sch.op(SY, lambda e, g=g, par=par, s=s: e.dma_start(
                stt[:, ((s % 4) * 3 + g) * 512:((s % 4) * 3 + g + 1) * 512],
                bass.AP(skew_d, par * SKEWN + SKG + g * P - 108, [[RR, 512], [1, P]]),
                transpose=True),
                reads=["d_skew%d" % par], writes=["%s_%d" % (kst, g)],
                sem="DSKR%d" % g, amt=16)

    def attn2(s):
        par = s % 2
        kst = ["stt%d_%d" % (s % 4, g) for g in range(3)]
        vkeys = ["vtm%d" % ((s - 1 + g) % 9) for g in range(3)]
        for half in range(2):
            pst, kpst = BD_PS[(s * 2 + half) % 2]

            def bmm(e, s=s, half=half, pst=pst):
                ins = None
                for h in (half * 2, half * 2 + 1):
                    for g in range(3):
                        vslot = (s - 1 + g) % 9
                        ins = e.matmul(
                            pst[:, (h - half * 2) * HD:(h - half * 2 + 1) * HD],
                            bass.AP(stt, ((s % 4) * 3 + g) * 512 + h, [[4 * 3 * 512, P], [4, P]]),
                            vtm[:, vslot * D + h * HD:vslot * D + (h + 1) * HD],
                            start=(g == 0), stop=(g == 2))
                return ins
            sch.op(T, bmm, reads=kst + vkeys, writes=[kpst])
            sch.op(V, lambda e, par=par, half=half, pst=pst: e.tensor_copy(
                amix_tm[:, par * D + half * 512:par * D + (half + 1) * 512], pst[:, 0:512]),
                reads=[kpst], writes=["amix%d" % par])
        sch.op(SY, lambda e, s=s, par=par: e.dma_start(
            bass.AP(amix_d, (s // 4 % 2) * 4 * P * D + (s % 4) * P * D, [[D, P], [1, D]]),
            amix_tm[:, par * D:(par + 1) * D]),
            reads=["amix%d" % par], writes=["d_amix%d" % (s // 4 % 2)], sem="DAW", amt=16)

    def wout_read(quad):
        qpar = quad % 2
        for k in range(KC):
            sch.op(SY, lambda e, k=k, qpar=qpar: e.dma_start(
                amix_fm[:, k * 512:(k + 1) * 512],
                bass.AP(amix_d, qpar * 4 * P * D + k * P, [[D, 512], [1, P]]),
                transpose=True),
                reads=["d_amix%d" % qpar], writes=["amixfm%d" % k],
                sem="DAR%d" % (k % 4), amt=16)

    def wout_mm(quad):
        amixkeys = ["amixfm%d" % k for k in range(KC)]
        for i in range(4):
            s = 4 * quad + i
            par = s % 2
            sch.op(G, lambda e, s=s, par=par: e.dma_start(
                x2[:, par * D:(par + 1) * D], x_in[(s + 1) * P:(s + 2) * P, :]),
                writes=["x2%d" % par], sem="DX2", amt=16)
            for n in range(2):
                pst, kpst = WT_PS[(i * 2 + n) % 2]

                def omm(e, i=i, n=n, pst=pst):
                    ins = None
                    for k in range(KC):
                        ins = e.matmul(
                            pst[:, 0:512],
                            amix_fm[:, k * 512 + i * P:k * 512 + (i + 1) * P],
                            wout_b[:, k * D + n * 512:k * D + (n + 1) * 512],
                            start=(k == 0), stop=(k == KC - 1))
                    return ins
                sch.op(T, omm, reads=amixkeys + ["wout"], writes=[kpst])
                sch.op(V, lambda e, par=par, n=n, pst=pst: e.scalar_tensor_tensor(
                    pout[:, par * D + n * 512:par * D + (n + 1) * 512], pst[:, 0:512], 1.0,
                    x2[:, par * D + n * 512:par * D + (n + 1) * 512], op0=A.mult, op1=A.add),
                    reads=[kpst, "x2%d" % par], writes=["pout%d" % par])
            sch.op(SY, lambda e, s=s, par=par: e.dma_start(
                bass.AP(partial_d, s * P * D, [[D, P], [1, D]]),
                pout[:, par * D:(par + 1) * D]),
                reads=["pout%d" % par], writes=["d_part%d" % s], sem="DP", amt=16)

    for e_ in (-1, 0, 1):
        vload(e_)
    for kind, e_ in (("ln", -1), ("vl", 2), ("ln", 0), ("vl", 3), ("ln", 1),
                     ("vl", 4), ("tv", -1), ("ln", 2), ("vl", 5), ("tv", 0),
                     ("ln", 3), ("vl", 6), ("tv", 1), ("ln", 4), ("vl", 7),
                     ("tv", 2), ("ln", 5), ("tv", 3)):
        {"ln": vpath_ln, "tv": vpath_tv, "vl": vload}[kind](e_)
    sch.full_barrier()
    sch.op(V, lambda e: e.memset(probs[:, :], 0.0), writes=["probs0", "probs1"])

    for s in range(NT + 7):
        if s + 8 <= NB - 2:
            vload(s + 8)
        if 0 <= s - 4 < NT:
            attn2(s - 4)
        if s + 4 <= NB - 2:
            vpath_tv(s + 4)
        if s + 6 <= NB - 2:
            vpath_ln(s + 6)
        if (s - 7) % 4 == 0 and 0 <= s - 7 < NT:
            wout_read((s - 7) // 4)
        if s % 4 == 0 and s < NT:
            qpath(s // 4)
        if 0 <= s - 2 < NT:
            attn1b2(s - 2)
        if s < NT:
            attn1a(s)
        if 0 <= s - 1 < NT:
            attn1b1(s - 1)
        if (s - 10) % 4 == 0 and 0 <= s - 10 < NT:
            wout_mm((s - 10) // 4)

    sch.full_barrier()

    for j in range(4):
        w = KC * HID // 4
        sch.op(SY, lambda e, j=j, w=w: e.dma_start(
            wq1sb[:, j * w:(j + 1) * w],
            bass.AP(wq1t_d, j * w, [[KC * HID, P], [1, w]])),
            reads=["d_wq1"], writes=["wq1sb_%d" % j], sem=p0s[j], amt=16)
        sch.op(SY, lambda e, j=j, w=w: e.dma_start(
            wq2sb[:, j * w:(j + 1) * w],
            bass.AP(wq2t_d, j * w, [[KH * D, P], [1, w]])),
            reads=["d_wq2"], writes=["wq2sb_%d" % j], sem=p0s[4 + j], amt=16)
    bi = 0
    for (row, rkey, dst, dkey, n) in ((gamrow, "gamrow", gamb, "gamb", D),
                                      (betrow, "betrow", betb, "betb", D),
                                      (b1row, "b1row", b1b, "b1b", HID),
                                      (b2row, "b2row", b2b, "b2b", D)):
        for c in range(0, n, 512):
            pst, kpst = WT_PS[bi % 2]
            bi += 1
            sch.op(T, lambda e, row=row, c=c, pst=pst: e.matmul(
                pst[:, 0:512], onesrow[0:1, :], row[0:1, c:c + 512],
                start=True, stop=True),
                reads=[rkey, "onesrow"], writes=[kpst])
            sch.op(S, lambda e, dst=dst, c=c, pst=pst: e.copy(dst[:, c:c + 512], pst[:, 0:512]),
                   reads=[kpst], writes=[dkey])

    def qload(s):
        par = s % 2
        tb = s + 1
        sch.op(G, lambda e, tb=tb, par=par: e.dma_start(
            xld3[:, par * D:(par + 1) * D], x_in[tb * P:(tb + 1) * P, :]),
            writes=["xld3%d" % par], sem="DX3", amt=16)

    def quantize1(s):
        par = s % 2
        tb = s + 1
        kx, kw, kq = "xld3%d" % par, "w3%d" % par, "xq1tm%d" % par
        sch.op(V, lambda e: e.tensor_scalar(
            w3[:, par * D:(par + 1) * D], xld3[:, par * D:(par + 1) * D],
            st_st[:, 2 * tb:2 * tb + 1], st_r1[:, tb:tb + 1], op0=A.subtract, op1=A.mult),
            reads=[kx, "st%d" % tb, "str%d" % tb], writes=[kw])
        sch.op(V, lambda e: e.tensor_scalar(
            w3[:, par * D:(par + 1) * D], w3[:, par * D:(par + 1) * D],
            st2_st[:, 2 * s:2 * s + 1], st_r2[:, s:s + 1], op0=A.subtract, op1=A.mult),
            reads=[kw, "st2_%d" % s, "st2r%d" % s], writes=[kw])
        sch.op(V, lambda e: e.tensor_mul(w3[:, par * D:(par + 1) * D],
                                         w3[:, par * D:(par + 1) * D], gamb[:, :]),
               reads=[kw, "gamb"], writes=[kw])
        sch.op(V, lambda e: e.tensor_add(w3[:, par * D:(par + 1) * D],
                                         w3[:, par * D:(par + 1) * D], betb[:, :]),
               reads=[kw, "betb"], writes=[kw])
        sch.op(V, lambda e: e.tensor_reduce(sc1[:, 0:1], w3[:, par * D:(par + 1) * D],
                                            AX.X, A.max, apply_absolute_value=True),
               reads=[kw], writes=["sc1a"])
        sch.op(V, lambda e: e.tensor_scalar(sc1[:, 1:2], sc1[:, 0:1], 1e-5, None, op0=A.max),
               reads=["sc1a"], writes=["sc1b"])
        sch.op(V, lambda e: e.reciprocal(sc1[:, 2:3], sc1[:, 1:2]),
               reads=["sc1b"], writes=["sc1c"])
        sch.op(V, lambda e: e.tensor_scalar(sc1[:, 3:4], sc1[:, 2:3], 127.0, None, op0=A.mult),
               reads=["sc1c"], writes=["sc1d"])
        sch.op(V, lambda e: e.tensor_scalar(sc1[:, 4:5], sc1[:, 3:4], swb[:, 2:3], None, op0=A.mult),
               reads=["sc1d", "sw1q"], writes=["sc1e"])
        sch.op(V, lambda e: e.reciprocal(inv1t[:, s:s + 1], sc1[:, 4:5]),
               reads=["sc1e"], writes=["inv1_%d" % s])
        sch.op(S, lambda e: e.activation(
            w3[:, par * D:(par + 1) * D], w3[:, par * D:(par + 1) * D], AF.Identity,
            bias=MAG, scale=sc1[:, 3:4]),
            reads=[kw, "sc1d", "magic"], writes=[kw])
        sch.op(S, lambda e: e.activation(
            xq1tm[:, par * D:(par + 1) * D], w3[:, par * D:(par + 1) * D], AF.Identity,
            bias=MAGN, scale=1.0),
            reads=[kw, "magicn"], writes=[kq])
        sch.op(SY, lambda e, s=s: e.dma_start(
            bass.AP(xq1_d, (s // 4 % 4) * 4 * P * D + (s % 4) * P * D, [[D, P], [1, D]]),
            xq1tm[:, par * D:(par + 1) * D]),
            reads=[kq], writes=["d_xq1_%d" % (s % 16)], sem="DQW%d" % (s % 2), amt=16)

    def xq1_read(quad):
        qpar = quad % 2
        for k in range(KC):
            sch.op(SY, lambda e, k=k, quad=quad, qpar=qpar: e.dma_start(
                xq1fm[:, qpar * 4096 + k * 512:qpar * 4096 + (k + 1) * 512],
                bass.AP(xq1_d, (quad % 4) * 4 * P * D + k * P, [[D, 512], [1, P]]),
                transpose=True),
                reads=["d_xq1_%d" % ((4 * quad + i_) % 16) for i_ in range(4)],
                writes=["xq1fm%d_%d" % (qpar, k)],
                sem="DQR%d" % (k % 4), amt=16)

    def ffn_f1(s):
        quad = s // 4
        qpar = quad % 2
        xq1keys = ["xq1fm%d_%d" % (qpar, k) for k in range(KC)]
        for c in range(8):
            pst, kpst = F1_PS[(s * 8 + c) % 5]

            def fmm(e, c=c, s=s, qpar=qpar, pst=pst):
                ins = None
                for k in range(KC):
                    ins = e.matmul(
                        pst[:, 0:512],
                        xq1fm[:, qpar * 4096 + k * 512 + (s % 4) * P:
                              qpar * 4096 + k * 512 + (s % 4 + 1) * P],
                        wq1sb[:, k * HID + c * 512:k * HID + (c + 1) * 512],
                        start=(k == 0), stop=(k == KC - 1))
                return ins
            sch.op(T, fmm, reads=xq1keys + ["wq1sb_%d" % j_ for j_ in range(4)], writes=[kpst])
            sch.op(V, lambda e, c=c, s=s, pst=pst, kpst=kpst: e.scalar_tensor_tensor(
                y1[:, c * 512:(c + 1) * 512], pst[:, 0:512], inv1t[:, s:s + 1],
                b1b[:, c * 512:(c + 1) * 512], op0=A.mult, op1=A.add),
                reads=[kpst, "inv1_%d" % s, "b1b"], writes=["y1h%d" % (c // 4)])

    def ffn_f1gelu(s):
        sch.op(S, lambda e: e.activation(y1[:, 0:2048], y1[:, 0:2048], AF.Gelu,
                                         bias=0.0, scale=1.0),
               reads=["y1h0"], writes=["y1h0"])
        sch.op(S, lambda e: e.activation(y1[:, 2048:HID], y1[:, 2048:HID], AF.Gelu,
                                         bias=0.0, scale=1.0),
               reads=["y1h1"], writes=["y1h1"])

    def ffn_f2head(s):
        par = s % 2
        sch.op(SY, lambda e, s=s, par=par: e.dma_start(
            ppart[:, par * D:(par + 1) * D],
            bass.AP(partial_d, s * P * D, [[D, P], [1, D]])),
            reads=["d_part%d" % s], writes=["ppart%d" % par], sem="DPR", amt=16)
        sch.op(V, lambda e: e.tensor_reduce(sc2[:, 6:7], y1[:, 0:2048], AX.X, A.max,
                                            apply_absolute_value=True),
               reads=["y1h0"], writes=["sc2m0"])
        sch.op(V, lambda e: e.tensor_reduce(sc2[:, 7:8], y1[:, 2048:HID], AX.X, A.max,
                                            apply_absolute_value=True),
               reads=["y1h1"], writes=["sc2m1"])
        sch.op(V, lambda e: e.tensor_scalar(sc2[:, 0:1], sc2[:, 6:7], sc2[:, 7:8],
                                            None, op0=A.max),
               reads=["sc2m0", "sc2m1"], writes=["sc2a"])
        sch.op(V, lambda e: e.tensor_scalar(sc2[:, 1:2], sc2[:, 0:1], 1e-5, None, op0=A.max),
               reads=["sc2a"], writes=["sc2b"])
        sch.op(V, lambda e: e.reciprocal(sc2[:, 2:3], sc2[:, 1:2]),
               reads=["sc2b"], writes=["sc2c"])
        sch.op(V, lambda e: e.tensor_scalar(sc2[:, 3:4], sc2[:, 2:3], 127.0, None, op0=A.mult),
               reads=["sc2c"], writes=["sc2d"])
        sch.op(V, lambda e: e.tensor_scalar(sc2[:, 4:5], sc2[:, 3:4], swb[:, 3:4], None, op0=A.mult),
               reads=["sc2d", "sw2q"], writes=["sc2e"])
        sch.op(V, lambda e: e.reciprocal(sc2[:, 5:6], sc2[:, 4:5]),
               reads=["sc2e"], writes=["sc2f"])
        for h, kh, kx in ((0, "y1h0", "xq2tm0"), (1, "y1h1", "xq2tm1")):
            lo, hi = h * 2048, (h + 1) * 2048
            sch.op(S, lambda e, lo=lo, hi=hi: e.activation(
                y1[:, lo:hi], y1[:, lo:hi], AF.Identity,
                bias=MAG, scale=sc2[:, 3:4]),
                reads=[kh, "sc2d", "magic"], writes=[kh])
            sch.op(S, lambda e, lo=lo, hi=hi: e.activation(
                xq2tm[:, lo:hi], y1[:, lo:hi], AF.Identity,
                bias=MAGN, scale=1.0),
                reads=[kh, "magicn"], writes=[kx])

    def ffn_f2tailA(s):
        for r0 in range(0, KH, 4):
            def trq(e, r0=r0):
                for i in range(4):
                    kb = r0 + i
                    ins = e.transpose(
                        ps_tx[:, i * P:(i + 1) * P],
                        xq2tm[:, kb * P:(kb + 1) * P], ident[:, :])
                return ins
            sch.op(T, trq, reads=["xq2tm%d" % (r0 // 16), "ident"],
                   writes=["pstx"])
            sch.op(S, lambda e, r0=r0: e.copy(
                xq2fm[:, r0 * P:(r0 + 4) * P], ps_tx[:, 0:512]),
                reads=["pstx"], writes=["xq2fm"])

    def ffn_f2tailB(s):
        par = s % 2
        for n in range(2):
            pst, kpst = BD_PS[n % 2]

            def f2mm(e, n=n, pst=pst):
                ins = None
                for k in range(KH):
                    ins = e.matmul(
                        pst[:, 0:512],
                        xq2fm[:, k * P:(k + 1) * P],
                        wq2sb[:, k * D + n * 512:k * D + (n + 1) * 512],
                        start=(k == 0), stop=(k == KH - 1))
                return ins
            sch.op(T, f2mm, reads=["xq2fm"] + ["wq2sb_%d" % j_ for j_ in range(4)],
                   writes=[kpst])
            sch.op(V, lambda e, n=n, pst=pst, kpst=kpst: e.scalar_tensor_tensor(
                t2s[:, n * 512:(n + 1) * 512], pst[:, 0:512], sc2[:, 5:6],
                b2b[:, n * 512:(n + 1) * 512], op0=A.mult, op1=A.add),
                reads=[kpst, "sc2f", "b2b"], writes=["t2s"])
        sch.op(V, lambda e, par=par: e.tensor_add(
            ppart[:, par * D:(par + 1) * D], ppart[:, par * D:(par + 1) * D],
            t2s[:, 0:D]),
            reads=["t2s", "ppart%d" % par], writes=["ppart%d" % par])
        sch.op(SY, lambda e, s=s, par=par: e.dma_start(
            y[s * P:(s + 1) * P, :], ppart[:, par * D:(par + 1) * D]),
            reads=["ppart%d" % par], writes=["ppart%d" % par], sem="DO", amt=16)

    qload(0)
    qload(1)
    for s in range(8):
        quantize1(s)
        qload(s + 2)
    xq1_read(0)
    ffn_f1(0)
    ffn_f1gelu(0)
    for s in range(NT):
        ffn_f2head(s)
        if s % 4 == 0 and (s + 4) // 4 < NT // 4:
            xq1_read((s + 4) // 4)
        if s + 1 < NT:
            ffn_f1(s + 1)
        ffn_f2tailA(s)
        if s + 1 < NT:
            ffn_f1gelu(s + 1)
        if 8 + s < NT:
            quantize1(8 + s)
        if 10 + s < NT:
            qload(10 + s)
        ffn_f2tailB(s)

    sch.wait_only(SY, [("DO", sch.semc["DO"])])

    with nc.Block() as block:
        @block.sync
        def _(e):
            for f in sch.thunks["sync"]:
                f(e)

        @block.scalar
        def _(e):
            for f in sch.thunks["scalar"]:
                f(e)

        @block.vector
        def _(e):
            for f in sch.thunks["vector"]:
                f(e)

        @block.tensor
        def _(e):
            for f in sch.thunks["tensor"]:
                f(e)

        @block.gpsimd
        def _(e):
            for f in sch.thunks["gpsimd"]:
                f(e)

    es.close()
    return nc


_CACHED = [None]


def kernel(**inputs):
    xq = np.ascontiguousarray(inputs["L0_attn_query"], dtype=np.float32)
    B, TT, DD = xq.shape
    assert (B, TT, DD) == (4, 8192, 1024)
    if _CACHED[0] is None:
        _CACHED[0] = build_program()
    nc = _CACHED[0]
    common = {
        "W_qv": np.ascontiguousarray(inputs["L0_attn_W_qv"], np.float32),
        "W2_0": np.ascontiguousarray(inputs["L0_attn_W2_0"], np.float32),
        "W2_1": np.ascontiguousarray(inputs["L0_attn_W2_1"], np.float32),
        "SCW": np.ascontiguousarray(inputs["L0_attn_scale_weights"], np.float32).reshape(1, 2),
        "W_out": np.ascontiguousarray(inputs["L0_attn_W_out"], np.float32),
        "w1": np.ascontiguousarray(inputs["L0_ffn_w1"], np.float32),
        "b1": np.ascontiguousarray(inputs["L0_ffn_b1"], np.float32).reshape(1, HID),
        "w2": np.ascontiguousarray(inputs["L0_ffn_w2"], np.float32),
        "b2": np.ascontiguousarray(inputs["L0_ffn_b2"], np.float32).reshape(1, D),
        "gam": np.ascontiguousarray(inputs["L0_ffn_gamma"], np.float32).reshape(1, D),
        "bet": np.ascontiguousarray(inputs["L0_ffn_bet" + "a"], np.float32).reshape(1, D),
    }
    in_maps = []
    for c in range(8):
        b, h = c // 2, c % 2
        t0 = h * 4096
        xe = np.zeros((TIN, D), np.float32)
        lo, hi = t0 - P, t0 + 4096 + P
        s_, e_ = max(lo, 0), min(hi, TT)
        xe[s_ - lo:s_ - lo + (e_ - s_)] = xq[b, s_:e_]
        m = dict(common)
        m["x_in"] = xe
        in_maps.append(m)
    res = run_bass_kernel_spmd(nc, in_maps, core_ids=list(range(8)))
    out = np.empty((B, TT, DD), np.float32)
    for c in range(8):
        b, h = c // 2, c % 2
        out[b, h * 4096:(h + 1) * 4096] = res.results[c]["y"]
    return out


# version 29 (final)
# speedup vs baseline: 1.1092x; 1.0021x over previous
import contextlib
import numpy as np

import concourse.bass as bass
import concourse.mybir as mybir
from concourse.bass_utils import run_bass_kernel_spmd

F32 = mybir.dt.float32
BF16 = mybir.dt.bfloat16
FP8 = mybir.dt.float8e4
A = mybir.AluOpType
AF = mybir.ActivationFunctionType
AX = mybir.AxisListType

P = 128
D = 1024
KC = 8
NT = 32
NB = 34
TIN = NB * P
C0, C1, HALF = 21, 41, 20
NH, HD = 4, 256
HID = 4096
KH = 32
Z = 235
RR = C1 + Z
ROWP = 4 * RR + 1
SKG = 1024
SKEWN = SKG + 128 * ROWP + SKG
SKEW_TOT = ((2 * SKEWN + 4096 + 2047) // 2048) * 2048
K_CONST = 1.5 * 2.0 ** 82
EPS_PROBE = 2.0 ** -30
MAGIC_LO = 1.5 * 2.0 ** 23

ENGINES = ("sync", "scalar", "vector", "tensor", "gpsimd")
DMA_SEMS = ("DX1", "DX2", "DX3", "DW", "DSKW", "DSKR0", "DSKR1", "DSKR2",
            "DAW", "DAR0", "DAR1", "DAR2", "DAR3",
            "DQW0", "DQW1", "DQR0", "DQR1", "DQR2", "DQR3", "DP", "DPR", "DO")


class Sched:

    def __init__(self, nc, es):
        self.nc = nc
        self.es = es
        self.thunks = {e: [] for e in ENGINES}
        self.semh = {}
        self.semc = {}
        self.lastwait = {e: {} for e in ENGINES}
        self.lasttok = {e: None for e in ENGINES}
        self.wmap = {}
        self.rmap = {}
        self.barrier = None

    def sem(self, name):
        self.semh[name] = self.es.enter_context(self.nc.semaphore(name))
        self.semc[name] = 0

    def op(self, eng, fn, reads=(), writes=(), extra=(), sem=None, amt=1):
        if amt == 16 and sem is not None:
            writes = list(writes) + ["_stream_" + sem]
        waits = []
        for k in reads:
            waits.append(self.wmap.get(k))
        for k in writes:
            waits.append(self.wmap.get(k))
            waits.extend(self.rmap.get(k, ()))
        waits.extend(extra)
        if self.barrier is not None:
            waits.append(self.barrier)
        wl = []
        seen = set()
        for t in waits:
            if t is None or t in seen:
                continue
            seen.add(t)
            s, v = t
            assert v <= self.semc[s], (s, v, self.semc[s])
            wl.append((s, v))
        if sem is None:
            sem = {"sync": "SY", "scalar": "S", "vector": "V",
                   "tensor": "T", "gpsimd": "G"}[eng]
        self.semc[sem] += amt
        tok = (sem, self.semc[sem])
        lw = self.lastwait[eng]
        wl2 = [(s, v) for (s, v) in wl if lw.get(s, -1) < v]
        for s, v in wl:
            if lw.get(s, -1) < v:
                lw[s] = v
        semh = self.semh

        def thunk(e, wl2=wl2, fn=fn, sem=sem, amt=amt):
            for s, v in wl2:
                e.wait_ge(semh[s], v)
            if fn is not None:
                ins = fn(e)
                ins.then_inc(semh[sem], amt)
            else:
                e.nop().then_inc(semh[sem], amt)

        self.thunks[eng].append(thunk)
        for k in reads:
            self.rmap.setdefault(k, []).append(tok)
        for k in writes:
            self.wmap[k] = tok
            self.rmap[k] = []
        self.lasttok[eng] = tok
        return tok

    def wait_only(self, eng, extra):
        semh = self.semh
        wl = [t for t in extra if t is not None]

        def thunk(e, wl=wl):
            for s, v in wl:
                e.wait_ge(semh[s], v)

        self.thunks[eng].append(thunk)

    def full_barrier(self):
        dma_tots = [(s, self.semc[s]) for s in DMA_SEMS]
        for eng in ENGINES:
            ex = [self.lasttok[eng]]
            if eng == "sync":
                ex += dma_tots
            self.op(eng, None, extra=ex, sem="PB")
        self.barrier = ("PB", self.semc["PB"])


def build_program():
    nc = bass.Bass("TRN2", target_bir_lowering=False)
    es = contextlib.ExitStack()

    x_in = nc.dram_tensor("x_in", [TIN, D], F32, kind="ExternalInput")
    W_qv = nc.dram_tensor("W_qv", [D, 2 * D], F32, kind="ExternalInput")
    W2_0 = nc.dram_tensor("W2_0", [D, NH * C0], F32, kind="ExternalInput")
    W2_1 = nc.dram_tensor("W2_1", [D, NH * C1], F32, kind="ExternalInput")
    SCW = nc.dram_tensor("SCW", [1, 2], F32, kind="ExternalInput")
    W_out = nc.dram_tensor("W_out", [D, D], F32, kind="ExternalInput")
    w1 = nc.dram_tensor("w1", [HID, D], F32, kind="ExternalInput")
    b1 = nc.dram_tensor("b1", [1, HID], F32, kind="ExternalInput")
    w2 = nc.dram_tensor("w2", [D, HID], F32, kind="ExternalInput")
    b2 = nc.dram_tensor("b2", [1, D], F32, kind="ExternalInput")
    gam = nc.dram_tensor("gam", [1, D], F32, kind="ExternalInput")
    bet = nc.dram_tensor("bet", [1, D], F32, kind="ExternalInput")
    y = nc.dram_tensor("y", [NT * P, D], F32, kind="ExternalOutput")

    partial_d = nc.dram_tensor("partial_d", [NT * P * D], F32, kind="Internal")
    wq1t_d = nc.dram_tensor("wq1t_d", [KC * P * HID], FP8, kind="Internal")
    wq2t_d = nc.dram_tensor("wq2t_d", [KH * P * D], FP8, kind="Internal")
    skew_d = nc.dram_tensor("skew_d", [SKEW_TOT], BF16, kind="Internal")
    amix_d = nc.dram_tensor("amix_d", [2 * 4 * P * D], BF16, kind="Internal")
    xq1_d = nc.dram_tensor("xq1_d", [4 * 4 * P * D], BF16, kind="Internal")

    SB_BASE = 16512 + 2048
    SB_TOP = 16512 + 212863
    cursor = [SB_BASE]

    def salloc(name, shape, dtype, at=None):
        nb = mybir.dt.size(dtype)
        for s_ in shape[1:]:
            nb *= s_
        if at is None:
            at = cursor[0]
        else:
            at = at + SB_BASE
        t = nc.alloc_sbuf_tensor_at(name, shape, dtype, offset=at)
        assert at + nb <= SB_TOP, (name, at + nb, SB_TOP)
        cursor[0] = max(cursor[0], (at + nb + 127) // 128 * 128)
        return t

    ident = salloc("ident", [P, P], BF16)
    iot = salloc("iot", [P, P], F32)
    magic = salloc("magic", [P, 4], F32)
    cst = salloc("cst", [P, 4], F32)
    swb = salloc("swb", [P, 8], F32)
    onesrow = salloc("onesrow", [1, P], F32)
    onescol = salloc("onescol", [P, 1], F32)
    swrow = salloc("swrow", [1, 8], F32)
    b1row = salloc("b1row", [1, HID], F32)
    b2row = salloc("b2row", [1, D], F32)
    gamrow = salloc("gamrow", [1, D], F32)
    betrow = salloc("betrow", [1, D], F32)
    st_st = salloc("st_st", [P, 2 * NB], F32)
    st_r1 = salloc("st_r1", [P, NB], F32)
    st2_st = salloc("st2_st", [P, 2 * NT], F32)
    st_r2 = salloc("st_r2", [P, NT], F32)
    inv1t = salloc("inv1t", [P, NT], F32)
    bs = salloc("bs", [P, 24], F32)
    sc0 = salloc("sc0", [1, 8], F32)
    sc1 = salloc("sc1", [P, 8], F32)
    sc2 = salloc("sc2", [P, 8], F32)
    negmax = salloc("negmax", [P, 16], F32)
    ssum = salloc("ssum", [P, 16], F32)
    lnsm = salloc("lnsm", [P, 16], F32)
    bias2 = salloc("bias2", [P, 8], F32)
    e0b = salloc("e0b", [P, NH * C0], BF16)
    lnswb = salloc("lnswb", [P, 2], F32)
    RES_END = cursor[0]

    wqv_b = salloc("wqv_b", [P, KC * 2048], BF16)
    w2c_b = salloc("w2c_b", [P, KC * 248], BF16)
    wout_b = salloc("wout_b", [P, KC * D], BF16)
    AACT_BASE = cursor[0]
    xld = salloc("xld", [P, 3 * D], F32)
    xn32 = salloc("xn32", [P, 3 * D], F32)
    xnfm = salloc("xnfm", [P, KC * 8 * P], BF16)
    vtm = salloc("vtm", [P, 9 * D], BF16)
    rqfm = salloc("rqfm", [P, 2 * KC * 512], BF16)
    probs = salloc("probs", [P, 2 * 4 * RR], BF16)
    expt = salloc("expt", [P, 2 * 248], F32)
    pmrg = salloc("pmrg", [P, 248], F32)
    stt = salloc("stt", [P, 4 * 3 * 512], BF16)
    amix_tm = salloc("amix_tm", [P, 2 * D], BF16)
    amix_fm = salloc("amix_fm", [P, KC * 512], BF16)
    x2 = salloc("x2", [P, 2 * D], F32)
    pout = salloc("pout", [P, 2 * D], F32)
    A_END = cursor[0]
    assert A_END <= SB_TOP, A_END

    AB = AACT_BASE - SB_BASE
    wq1asm = salloc("wq1asm", [P, KC * HID], FP8, at=AB)
    wq2asm = salloc("wq2asm", [P, KH * D], FP8, at=AB + 32 * 1024)
    wstage = salloc("wstage", [P, 2 * HID], F32, at=AB + 64 * 1024)
    wqt = salloc("wqt", [P, HID + D], F32, at=AB + 96 * 1024)
    zbf = salloc("zbf", [P, 2048], BF16, at=AB + 116 * 1024)

    ccur = [RES_END - SB_BASE]

    def calloc(name, shape, dtype):
        t = salloc(name, shape, dtype, at=ccur[0])
        nb = mybir.dt.size(dtype)
        for s_ in shape[1:]:
            nb *= s_
        ccur[0] = (ccur[0] + nb + 127) // 128 * 128
        return t

    wq1sb = calloc("wq1sb", [P, KC * HID], FP8)
    wq2sb = calloc("wq2sb", [P, KH * D], FP8)
    b1b = calloc("b1b", [P, HID], F32)
    b2b = calloc("b2b", [P, D], F32)
    gamb = calloc("gamb", [P, D], F32)
    betb = calloc("betb", [P, D], F32)
    xld3 = calloc("xld3", [P, 2 * D], F32)
    w3 = calloc("w3", [P, 2 * D], F32)
    xq1tm = calloc("xq1tm", [P, 2 * D], BF16)
    xq1fm = calloc("xq1fm", [P, 2 * KC * 512], BF16)
    y1 = calloc("y1", [P, HID], F32)
    xq2tm = calloc("xq2tm", [P, KH * P], BF16)
    xq2fm = calloc("xq2fm", [P, KH * P], BF16)
    ppart = calloc("ppart", [P, 2 * D], F32)
    t2s = calloc("t2s", [P, D], F32)
    C_END = ccur[0] + SB_BASE
    assert C_END <= SB_TOP, C_END

    psA = es.enter_context(nc.psum_tensor("psA", [P, 512], F32))
    psB = es.enter_context(nc.psum_tensor("psB", [P, 512], F32))
    ql = es.enter_context(nc.psum_tensor("ql", [P, 512], F32))
    bandA = es.enter_context(nc.psum_tensor("bandA", [P, 512], F32))
    bandB = es.enter_context(nc.psum_tensor("bandB", [P, 512], F32))
    woutA = es.enter_context(nc.psum_tensor("woutA", [P, 512], F32))
    woutB = es.enter_context(nc.psum_tensor("woutB", [P, 512], F32))
    ps_tx = es.enter_context(nc.psum_tensor("ps_tx", [P, 512], BF16))

    AB_PS = [(psA, "psA"), (psB, "psB")]
    WT_PS = [(woutA, "woutA"), (woutB, "woutB")]
    BD_PS = [(bandA, "bandA"), (bandB, "bandB")]
    F1_PS = AB_PS + [(ql, "ql")] + WT_PS

    sch = Sched(nc, es)
    for s_ in ("V", "S", "T", "G", "SY", "PB") + DMA_SEMS:
        sch.sem(s_)
    V, S, T, SY, G = "vector", "scalar", "tensor", "sync", "gpsimd"
    MAG = magic[:, 0:1]
    MAGN = magic[:, 3:4]

    sch.op(G, lambda e: e.iota(iot[:, :], [[1, P]], channel_multiplier=-1,
                               allow_small_or_imprecise_dtypes=True), writes=["iot"])
    sch.op(V, lambda e: e.tensor_scalar(ident[:, :], iot[:, :], 0.0, None, op0=A.is_equal),
           reads=["iot"], writes=["ident"])
    sch.op(V, lambda e: e.tensor_scalar(iot[:, :], iot[:, :], 0.0, None, op0=A.is_equal),
           reads=["iot"], writes=["iot"])
    sch.op(V, lambda e: e.memset(cst[:, 0:1], 1e-5), writes=["cst"])
    sch.op(V, lambda e: e.memset(magic[:, 1:2], 1.0), writes=["mg1"])
    sch.op(V, lambda e: e.tensor_scalar(magic[:, 2:3], magic[:, 1:2], EPS_PROBE, None, op0=A.add),
           reads=["mg1"], writes=["mg2"])
    sch.op(V, lambda e: e.tensor_scalar(magic[:, 1:2], magic[:, 2:3], 1.0, None, op0=A.subtract),
           reads=["mg2"], writes=["mg1"])
    sch.op(V, lambda e: e.tensor_scalar(magic[:, 0:1], magic[:, 1:2], K_CONST, MAGIC_LO,
                                        op0=A.mult, op1=A.add),
           reads=["mg1"], writes=["magic"])
    sch.op(V, lambda e: e.tensor_scalar(magic[:, 3:4], magic[:, 0:1], -1.0, None, op0=A.mult),
           reads=["magic"], writes=["magicn"])
    sch.op(V, lambda e: e.memset(zbf[:, :], 0.0), writes=["zbf"])
    sch.op(V, lambda e: e.memset(onesrow[:, :], 1.0), writes=["onesrow"])
    sch.op(V, lambda e: e.memset(onescol[:, :], 1.0), writes=["onescol"])

    done = 0
    while done < SKEW_TOT:
        n = min(P * 2048, SKEW_TOT - done)
        rows = (n + 2047) // 2048
        sch.op(SY, lambda e, off=done, rows=rows: e.dma_start(
            bass.AP(skew_d, off, [[2048, rows], [1, 2048]]), zbf[0:rows, :]),
            reads=["zbf"], writes=["d_skew0", "d_skew1"], sem="DSKW", amt=16)
        done += rows * 2048

    sch.op(SY, lambda e: e.dma_start(swrow[0:1, 0:2], SCW[:, :]), writes=["swrow"], sem="DP", amt=16)
    sch.op(V, lambda e: e.tensor_reduce(sc0[0:1, 0:1], swrow[0:1, 0:2], AX.X, A.max, negate=True),
           reads=["swrow"], writes=["sc0a"])
    sch.op(S, lambda e: e.activation(sc0[0:1, 2:4], swrow[0:1, 0:2], AF.Exp,
                                     bias=sc0[0:1, 0:1], scale=1.0),
           reads=["swrow", "sc0a"], writes=["sc0b"])
    sch.op(V, lambda e: e.tensor_reduce(sc0[0:1, 1:2], sc0[0:1, 2:4], AX.X, A.add),
           reads=["sc0b"], writes=["sc0c"])
    sch.op(V, lambda e: e.reciprocal(sc0[0:1, 4:5], sc0[0:1, 1:2]),
           reads=["sc0c"], writes=["sc0d"])
    sch.op(V, lambda e: e.tensor_scalar(sc0[0:1, 6:8], sc0[0:1, 2:4], sc0[0:1, 4:5], None, op0=A.mult),
           reads=["sc0b", "sc0d"], writes=["sc0e"])
    sch.op(T, lambda e: e.matmul(ql[:, 0:2], onesrow[0:1, :], sc0[0:1, 6:8],
                                 start=True, stop=True),
           reads=["sc0e", "onesrow"], writes=["ql"])
    sch.op(S, lambda e: e.copy(swb[:, 0:2], ql[:, 0:2]), reads=["ql"], writes=["swb01"])
    sch.op(S, lambda e: e.activation(lnswb[:, 0:2], swb[:, 0:2], AF.Ln, bias=0.0, scale=1.0),
           reads=["swb01"], writes=["lnswb"])

    p0s = ["DW", "DAW", "DQW0", "DAR0", "DAR1", "DAR2", "DAR3", "DQR1"]
    NP0 = len(p0s)

    sch.op(V, lambda e: e.memset(magic[:, 1:2], 0.0), reads=["mg2"], writes=["mg1"])
    sch.op(V, lambda e: e.memset(cst[:, 1:2], 0.0), writes=["c1"])

    def flat_tile(wsrc, t, fi, acc_sl, red_sl, ka, kr):
        so = (fi % 2) * HID
        ws = "wst%d" % (fi % 2)
        sch.op(SY, lambda e, t=t, so=so, wsrc=wsrc: e.dma_start(
            wstage[:, so:so + HID],
            bass.AP(wsrc, t * P * HID, [[HID, P], [1, HID]])),
            writes=[ws], sem=p0s[fi % NP0], amt=16)
        sch.op(V, lambda e, so=so, red_sl=red_sl: e.tensor_reduce(
            red_sl, wstage[:, so:so + HID], AX.X, A.add,
            apply_absolute_value=True),
            reads=[ws], writes=[kr])
        sch.op(V, lambda e, acc_sl=acc_sl, red_sl=red_sl: e.tensor_add(
            acc_sl, acc_sl, red_sl),
            reads=[kr, ka], writes=[ka])

    fi = 0
    for t in range(KC):
        flat_tile(w1, t, fi, magic[:, 1:2], magic[:, 2:3], "mg1", "mg2"); fi += 1
        flat_tile(w2, t, fi, cst[:, 1:2], cst[:, 2:3], "c1", "c2"); fi += 1

    def scale_chain(acc_sl, ka, nel, swcol, swkey):
        sch.op(T, lambda e, acc_sl=acc_sl: e.matmul(
            ql[0:1, 4:5], acc_sl, onescol[:, 0:1], start=True, stop=True),
            reads=[ka, "onescol"], writes=["ql"])
        sch.op(S, lambda e: e.copy(sc0[0:1, 5:6], ql[0:1, 4:5]), reads=["ql"], writes=["sc0f"])
        sch.op(V, lambda e, nel=nel: e.tensor_scalar(sc0[0:1, 5:6], sc0[0:1, 5:6],
                                                     1.0 / nel, 1e-5,
                                                     op0=A.mult, op1=A.max),
               reads=["sc0f"], writes=["sc0f"])
        sch.op(V, lambda e: e.reciprocal(sc0[0:1, 5:6], sc0[0:1, 5:6]),
               reads=["sc0f"], writes=["sc0f"])
        sch.op(T, lambda e: e.matmul(ql[:, 5:6], onesrow[0:1, :], sc0[0:1, 5:6],
                                     start=True, stop=True),
               reads=["sc0f", "onesrow"], writes=["ql"])
        sch.op(S, lambda e, swcol=swcol: e.copy(swb[:, swcol:swcol + 1], ql[:, 5:6]),
               reads=["ql"], writes=[swkey])

    scale_chain(magic[:, 1:2], "mg1", float(KH * P * D), 2, "sw1q")
    scale_chain(cst[:, 1:2], "c1", float(KC * P * HID), 3, "sw2q")
    sch.op(V, lambda e: e.tensor_scalar(magic[:, 1:2], magic[:, 0:1], 1.0, None,
                                        op0=A.add),
           reads=["magic"], writes=["mg1", "mp1"])
    sch.op(V, lambda e: e.tensor_scalar(magic[:, 2:3], magic[:, 0:1], 1.0, None,
                                        op0=A.subtract),
           reads=["magic"], writes=["mg2", "mm1"])

    def ld_cast(dram, cols, dst, dstoff, kidx, wkey, slot):
        so = slot * HID
        sch.op(SY, lambda e: e.dma_start(wstage[:, so:so + cols], dram[kidx * P:(kidx + 1) * P, :]),
               writes=["wst%d" % slot], sem=p0s[(kidx + slot) % NP0], amt=16)
        sch.op(V, lambda e: e.tensor_copy(dst[:, dstoff:dstoff + cols], wstage[:, so:so + cols]),
               reads=["wst%d" % slot], writes=[wkey])

    sl = 0
    for k in range(KC):
        ld_cast(W_qv, 2048, wqv_b, k * 2048, k, "wqv", sl % 2); sl += 1
    for k in range(KC):
        ld_cast(W_out, D, wout_b, k * D, k, "wout", sl % 2); sl += 1
    for k in range(KC):
        ld_cast(W2_0, NH * C0, w2c_b, k * 248, k, "w2c", sl % 2); sl += 1
        ld_cast(W2_1, NH * C1, w2c_b, k * 248 + NH * C0, k, "w2c", sl % 2); sl += 1

    sch.op(SY, lambda e: e.dma_start(b1row[:, :], b1[:, :]), writes=["b1row"], sem="DP", amt=16)
    sch.op(SY, lambda e: e.dma_start(b2row[:, :], b2[:, :]), writes=["b2row"], sem="DP", amt=16)
    sch.op(SY, lambda e: e.dma_start(gamrow[:, :], gam[:, :]), writes=["gamrow"], sem="DP", amt=16)
    sch.op(SY, lambda e: e.dma_start(betrow[:, :], bet[:, :]), writes=["betrow"], sem="DP", amt=16)


    pend = []
    evct = [0]

    def flush_evacs(keep):
        while len(pend) > keep:
            fn, kpst, asmkey = pend.pop(0)
            if evct[0] % 2 == 0:
                sch.op(S, lambda e, fn=fn: fn(e, 0), reads=[kpst, "magicn"],
                       writes=[asmkey])
            else:
                sch.op(V, lambda e, fn=fn: fn(e, 1), reads=[kpst, "magic"],
                       writes=[asmkey])
            evct[0] += 1

    def quant_tile(wsrc, t, ti, cols, swcol, swkey, asmdst, asmkey, asm_k_step,
                   wqo, wqk, stg, stgname):
        so = (ti % 2) * HID
        ws = "%s%d" % (stgname, ti % 2)
        sch.op(SY, lambda e, t=t, so=so, cols=cols, wsrc=wsrc, stg=stg: e.dma_start(
            stg[:, so:so + cols], wsrc[t * P:(t + 1) * P, :]),
            writes=[ws], sem=p0s[ti % NP0], amt=16)
        for c in range(cols // 512):
            co = c * 512
            sch.op(S, lambda e, so=so, co=co, swcol=swcol, stg=stg: e.activation(
                stg[:, so + co:so + co + 512], stg[:, so + co:so + co + 512],
                AF.Identity, bias=MAG, scale=swb[:, swcol:swcol + 1]),
                reads=[ws, swkey, "magic"], writes=[ws])
            wqkc = "%s_%d" % (wqk, c % 4)
            sch.op(V, lambda e, so=so, co=co, wqo=wqo, stg=stg: e.tensor_scalar(
                wqt[:, wqo + co:wqo + co + 512], stg[:, so + co:so + co + 512],
                magic[:, 1:2], magic[:, 2:3], op0=A.min, op1=A.max),
                reads=[ws, "mp1", "mm1"], writes=[wqkc])
            pst, kpst = PH0_PS[(ti * 8 + c) % 6]

            def trq(e, c=c, wqo=wqo, pst=pst):
                for i in range(4):
                    kb = c * 4 + i
                    ins = e.transpose(
                        pst[:, i * P:(i + 1) * P],
                        wqt[:, wqo + kb * P:wqo + (kb + 1) * P], iot[:, :])
                return ins
            sch.op(T, trq, reads=[wqkc, "iot"], writes=[kpst])

            def evac(e, eng, t=t, c=c, asmdst=asmdst, asm_k_step=asm_k_step,
                     pst=pst):
                dst = bass.AP(asmdst, c * 4 * asm_k_step + t * P,
                              [[asmdst.shape[1], P], [asm_k_step, 4], [1, P]])
                if eng == 0:
                    return e.activation(dst, pst[:, 0:512], AF.Identity,
                                        bias=MAGN, scale=1.0)
                return e.tensor_scalar(dst, pst[:, 0:512], MAG, None,
                                       op0=A.subtract)
            pend.append((evac, kpst, asmkey))
            flush_evacs(2)

    PH0_PS = AB_PS + BD_PS + WT_PS
    ti = 0
    for g in range(KC):
        for i in range(4):
            quant_tile(w1, 4 * g + i, ti, D, 2, "sw1q", wq1asm, "wq1asm", HID,
                       HID, "wqt1", wstage, "wst"); ti += 1
        quant_tile(w2, g, ti, HID, 3, "sw2q", wq2asm, "wq2asm", D,
                   0, "wqt2", wstage, "wst"); ti += 1

    flush_evacs(0)
    sch.op(SY, lambda e: e.dma_start(bass.AP(wq1t_d, 0, [[KC * HID, P], [1, KC * HID]]),
                                     wq1asm[:, :]),
           reads=["wq1asm"], writes=["d_wq1"], sem="DW", amt=16)
    sch.op(SY, lambda e: e.dma_start(bass.AP(wq2t_d, 0, [[KH * D, P], [1, KH * D]]),
                                     wq2asm[:, :]),
           reads=["wq2asm"], writes=["d_wq2"], sem="DW", amt=16)

    def vload(eidx):
        t = eidx + 1
        par = t % 3
        sch.op(G, lambda e, t=t, par=par: e.dma_start(
            xld[:, par * D:(par + 1) * D], x_in[t * P:(t + 1) * P, :]),
            reads=["d_wq1", "d_wq2"],
            writes=["xld%d" % par], sem="DX1", amt=16)

    def vpath_ln(eidx):
        t = eidx + 1
        par = t % 3
        kx = "xld%d" % par
        kn = "xn32%d" % (t % 3)
        kb2 = "bs%d" % (t % 2)
        bso = (t % 2) * 12
        sch.op(V, lambda e: e.bn_stats(bs[:, bso:bso + 6], xld[:, par * D:par * D + 512]),
               reads=[kx], writes=[kb2])
        sch.op(V, lambda e: e.bn_stats(bs[:, bso + 6:bso + 12], xld[:, par * D + 512:par * D + D]),
               reads=[kx], writes=[kb2])
        sch.op(V, lambda e: e.bn_aggr(st_st[:, 2 * t:2 * t + 2], bs[:, bso:bso + 12]),
               reads=[kb2], writes=["st%d" % t])
        sch.op(S, lambda e: e.activation(st_r1[:, t:t + 1], st_st[:, 2 * t + 1:2 * t + 2],
                                         AF.Ln, bias=cst[:, 0:1], scale=1.0),
               reads=["st%d" % t, "cst"], writes=["str%d" % t])
        sch.op(S, lambda e: e.activation(st_r1[:, t:t + 1], st_r1[:, t:t + 1],
                                         AF.Exp, bias=0.0, scale=-0.5),
               reads=["str%d" % t], writes=["str%d" % t])
        sch.op(V, lambda e: e.tensor_scalar(xn32[:, (t % 3) * D:(t % 3 + 1) * D],
                                            xld[:, par * D:(par + 1) * D],
                                            st_st[:, 2 * t:2 * t + 1], st_r1[:, t:t + 1],
                                            op0=A.subtract, op1=A.mult),
               reads=[kx, "st%d" % t, "str%d" % t], writes=[kn])
        if 0 <= eidx < NT:
            s = eidx
            sch.op(V, lambda e: e.bn_stats(bs[:, bso:bso + 6],
                                           xn32[:, (t % 3) * D:(t % 3) * D + 512]),
                   reads=[kn], writes=[kb2])
            sch.op(V, lambda e: e.bn_stats(bs[:, bso + 6:bso + 12],
                                           xn32[:, (t % 3) * D + 512:(t % 3) * D + D]),
                   reads=[kn], writes=[kb2])
            sch.op(V, lambda e: e.bn_aggr(st2_st[:, 2 * s:2 * s + 2], bs[:, bso:bso + 12]),
                   reads=[kb2], writes=["st2_%d" % s])
            sch.op(S, lambda e: e.activation(st_r2[:, s:s + 1], st2_st[:, 2 * s + 1:2 * s + 2],
                                             AF.Ln, bias=cst[:, 0:1], scale=1.0),
                   reads=["st2_%d" % s, "cst"], writes=["st2r%d" % s])
            sch.op(S, lambda e: e.activation(st_r2[:, s:s + 1], st_r2[:, s:s + 1],
                                             AF.Exp, bias=0.0, scale=-0.5),
                   reads=["st2r%d" % s], writes=["st2r%d" % s])

    def vpath_tv(eidx):
        t = eidx + 1
        slot = eidx % 8
        slot9 = eidx % 9
        kn = "xn32%d" % (t % 3)
        kf = "xnfm%d" % slot
        kv = "vtm%d" % slot9
        for r0 in (0, 4):
            pst, kpst = AB_PS[(eidx * 3 + r0 // 4) % 2]

            def trq(e, r0=r0, pst=pst, t=t):
                for i in range(4):
                    kb = r0 + i
                    ins = e.transpose(
                        pst[:, i * P:(i + 1) * P],
                        xn32[:, (t % 3) * D + kb * P:(t % 3) * D + (kb + 1) * P],
                        iot[:, :])
                return ins
            sch.op(T, trq, reads=[kn, "iot"], writes=[kpst])

            def evacx(e, r0=r0, slot=slot, pst=pst):
                dst = bass.AP(xnfm, r0 * (8 * P) + slot * P,
                              [[KC * 8 * P, P], [8 * P, 4], [1, P]])
                return e.tensor_copy(dst, pst[:, 0:512])
            sch.op(V, evacx, reads=[kpst], writes=[kf])
        for n in range(2):
            pst, kpst = AB_PS[(eidx * 3 + 2 + n) % 2]

            def vmm(e, slot=slot, n=n, pst=pst):
                ins = None
                for k in range(KC):
                    ins = e.matmul(
                        pst[:, 0:512],
                        xnfm[:, k * (8 * P) + slot * P:k * (8 * P) + (slot + 1) * P],
                        wqv_b[:, k * 2048 + D + n * 512:k * 2048 + D + (n + 1) * 512],
                        start=(k == 0), stop=(k == KC - 1))
                return ins
            sch.op(T, vmm, reads=[kf, "wqv"], writes=[kpst])
            sch.op(S, lambda e, slot9=slot9, n=n, pst=pst: e.copy(
                vtm[:, slot9 * D + n * 512:slot9 * D + (n + 1) * 512], pst[:, 0:512]),
                reads=[kpst], writes=[kv])

    def qpath(quad):
        qpar = quad % 2
        slot0 = (4 * quad) % 8
        kfs = ["xnfm%d" % ((4 * quad + i) % 8) for i in range(4)]
        for m in range(KC):
            pst, kpst = WT_PS[m % 2]

            def qmm(e, m=m, slot0=slot0, pst=pst):
                ins = None
                for k in range(KC):
                    ins = e.matmul(
                        pst[:, 0:512],
                        wqv_b[:, k * 2048 + m * P:k * 2048 + (m + 1) * P],
                        xnfm[:, k * (8 * P) + slot0 * P:k * (8 * P) + slot0 * P + 512],
                        start=(k == 0), stop=(k == KC - 1))
                return ins
            sch.op(T, qmm, reads=kfs + ["wqv"], writes=[kpst])
            sch.op(V, lambda e, m=m, qpar=qpar, pst=pst: e.tensor_scalar(
                rqfm[:, qpar * 4096 + m * 512:qpar * 4096 + (m + 1) * 512],
                pst[:, 0:512], 0.0, None, op0=A.max),
                reads=[kpst], writes=["rqfm%d" % qpar])

    def attn1a(s):
        par = s % 2
        quad = s // 4
        qpar = quad % 2
        eo = par * 248

        def lmm(e, s=s, qpar=qpar):
            ins = None
            for k in range(KC):
                ins = e.matmul(
                    ql[:, 0:248],
                    rqfm[:, qpar * 4096 + k * 512 + (s % 4) * P:
                         qpar * 4096 + k * 512 + (s % 4 + 1) * P],
                    w2c_b[:, k * 248:(k + 1) * 248],
                    start=(k == 0), stop=(k == KC - 1))
            return ins
        sch.op(T, lmm, reads=["rqfm%d" % qpar, "w2c"], writes=["ql"])
        sch.op(V, lambda e, eo=eo: e.tensor_copy(expt[:, eo:eo + 248], ql[:, 0:248]),
               reads=["ql"], writes=["expt%d" % par])
        sch.op(V, lambda e, eo=eo, par=par: e.tensor_reduce(
            bass.AP(negmax, par * 8, [[16, P], [1, 4]]),
            expt[:, eo:eo + 84].rearrange("p (h c) -> p h c", c=C0),
            AX.X, A.max, negate=True),
            reads=["expt%d" % par], writes=["negmax0_%d" % par])
        sch.op(V, lambda e, eo=eo, par=par: e.tensor_reduce(
            bass.AP(negmax, par * 8 + 4, [[16, P], [1, 4]]),
            expt[:, eo + 84:eo + 248].rearrange("p (h c) -> p h c", c=C1),
            AX.X, A.max, negate=True),
            reads=["expt%d" % par], writes=["negmax1_%d" % par])

    def attn1b1(s):
        par = s % 2
        eo = par * 248
        no = par * 8
        groups = [(h, 0, h * C0, C0) for h in range(NH)] + \
                 [(h, 1, NH * C0 + h * C1, C1) for h in range(NH)]
        def expgrp(e, eo=eo, no=no):
            for gi, (h, ci, c0, cw) in enumerate(groups):
                ins = e.activation(
                    pmrg[:, c0:c0 + cw], expt[:, eo + c0:eo + c0 + cw], AF.Exp,
                    bias=negmax[:, no + gi:no + gi + 1], scale=1.0,
                    accum_out=ssum[:, no + gi:no + gi + 1])
            return ins
        sch.op(S, expgrp,
               reads=["expt%d" % par, "negmax0_%d" % par, "negmax1_%d" % par],
               writes=["pmgj"] + ["ssum%d_%d" % (par, g_) for g_ in range(8)])
        sch.op(S, lambda e, no=no: e.activation(lnsm[:, no:no + 8], ssum[:, no:no + 8],
                                                AF.Ln, bias=0.0, scale=1.0),
               reads=["ssum%d_%d" % (par, g_) for g_ in range(8)], writes=["lnsm%d" % par])

    def attn1b2(s):
        par = s % 2
        kp = "probs%d" % par
        kst = "stt%d" % (s % 4)
        eo = par * 248
        no = par * 8
        sch.op(V, lambda e, no=no: e.tensor_sub(bias2[:, 0:8], negmax[:, no:no + 8], lnsm[:, no:no + 8]),
               reads=["negmax0_%d" % par, "negmax1_%d" % par, "lnsm%d" % par], writes=["bias2"])
        sch.op(V, lambda e: e.tensor_scalar(bias2[:, 0:4], bias2[:, 0:4], lnswb[:, 0:1], None, op0=A.add),
               reads=["lnswb"], writes=["bias2"])
        sch.op(V, lambda e: e.tensor_scalar(bias2[:, 4:8], bias2[:, 4:8], lnswb[:, 1:2], None, op0=A.add),
               reads=["lnswb"], writes=["bias2"])
        def exppr(e, par=par, eo=eo):
            for h in range(NH):
                e.activation(
                    probs[:, par * (4 * RR) + h * RR:par * (4 * RR) + h * RR + C1],
                    expt[:, eo + 84 + h * C1:eo + 84 + (h + 1) * C1], AF.Exp,
                    bias=bias2[:, 4 + h:5 + h], scale=1.0)
                ins = e.activation(
                    e0b[:, h * C0:(h + 1) * C0],
                    expt[:, eo + h * C0:eo + (h + 1) * C0], AF.Exp,
                    bias=bias2[:, h:h + 1], scale=1.0)
            return ins
        sch.op(S, exppr, reads=["expt%d" % par, "bias2"], writes=[kp, "e0b"])

        def mrg(e, par=par):
            for h in range(NH):
                ins = e.tensor_add(
                    probs[:, par * (4 * RR) + h * RR + 10:par * (4 * RR) + h * RR + 31],
                    probs[:, par * (4 * RR) + h * RR + 10:par * (4 * RR) + h * RR + 31],
                    e0b[:, h * C0:(h + 1) * C0])
            return ins
        sch.op(V, mrg, reads=["e0b"], writes=[kp])
        sch.op(SY, lambda e, par=par: e.dma_start(
            bass.AP(skew_d, par * SKEWN + SKG, [[ROWP, P], [1, 4 * RR]]),
            probs[:, par * (4 * RR):(par + 1) * (4 * RR)]),
            reads=[kp], writes=["d_skew%d" % par], sem="DSKW", amt=16)
        for g in range(3):
            sch.op(SY, lambda e, g=g, par=par, s=s: e.dma_start(
                stt[:, ((s % 4) * 3 + g) * 512:((s % 4) * 3 + g + 1) * 512],
                bass.AP(skew_d, par * SKEWN + SKG + g * P - 108, [[RR, 512], [1, P]]),
                transpose=True),
                reads=["d_skew%d" % par], writes=["%s_%d" % (kst, g)],
                sem="DSKR%d" % g, amt=16)

    def attn2(s):
        par = s % 2
        kst = ["stt%d_%d" % (s % 4, g) for g in range(3)]
        vkeys = ["vtm%d" % ((s - 1 + g) % 9) for g in range(3)]
        for half in range(2):
            pst, kpst = BD_PS[(s * 2 + half) % 2]

            def bmm(e, s=s, half=half, pst=pst):
                ins = None
                for h in (half * 2, half * 2 + 1):
                    for g in range(3):
                        vslot = (s - 1 + g) % 9
                        ins = e.matmul(
                            pst[:, (h - half * 2) * HD:(h - half * 2 + 1) * HD],
                            bass.AP(stt, ((s % 4) * 3 + g) * 512 + h, [[4 * 3 * 512, P], [4, P]]),
                            vtm[:, vslot * D + h * HD:vslot * D + (h + 1) * HD],
                            start=(g == 0), stop=(g == 2))
                return ins
            sch.op(T, bmm, reads=kst + vkeys, writes=[kpst])
            sch.op(V, lambda e, par=par, half=half, pst=pst: e.tensor_copy(
                amix_tm[:, par * D + half * 512:par * D + (half + 1) * 512], pst[:, 0:512]),
                reads=[kpst], writes=["amix%d" % par])
        sch.op(SY, lambda e, s=s, par=par: e.dma_start(
            bass.AP(amix_d, (s // 4 % 2) * 4 * P * D + (s % 4) * P * D, [[D, P], [1, D]]),
            amix_tm[:, par * D:(par + 1) * D]),
            reads=["amix%d" % par], writes=["d_amix%d" % (s // 4 % 2)], sem="DAW", amt=16)

    def wout_read(quad):
        qpar = quad % 2
        for k in range(KC):
            sch.op(SY, lambda e, k=k, qpar=qpar: e.dma_start(
                amix_fm[:, k * 512:(k + 1) * 512],
                bass.AP(amix_d, qpar * 4 * P * D + k * P, [[D, 512], [1, P]]),
                transpose=True),
                reads=["d_amix%d" % qpar], writes=["amixfm%d" % k],
                sem="DAR%d" % (k % 4), amt=16)

    def wout_mm(quad):
        amixkeys = ["amixfm%d" % k for k in range(KC)]
        for i in range(4):
            s = 4 * quad + i
            par = s % 2
            sch.op(G, lambda e, s=s, par=par: e.dma_start(
                x2[:, par * D:(par + 1) * D], x_in[(s + 1) * P:(s + 2) * P, :]),
                writes=["x2%d" % par], sem="DX2", amt=16)
            for n in range(2):
                pst, kpst = WT_PS[(i * 2 + n) % 2]

                def omm(e, i=i, n=n, pst=pst):
                    ins = None
                    for k in range(KC):
                        ins = e.matmul(
                            pst[:, 0:512],
                            amix_fm[:, k * 512 + i * P:k * 512 + (i + 1) * P],
                            wout_b[:, k * D + n * 512:k * D + (n + 1) * 512],
                            start=(k == 0), stop=(k == KC - 1))
                    return ins
                sch.op(T, omm, reads=amixkeys + ["wout"], writes=[kpst])
                sch.op(V, lambda e, par=par, n=n, pst=pst: e.scalar_tensor_tensor(
                    pout[:, par * D + n * 512:par * D + (n + 1) * 512], pst[:, 0:512], 1.0,
                    x2[:, par * D + n * 512:par * D + (n + 1) * 512], op0=A.mult, op1=A.add),
                    reads=[kpst, "x2%d" % par], writes=["pout%d" % par])
            sch.op(SY, lambda e, s=s, par=par: e.dma_start(
                bass.AP(partial_d, s * P * D, [[D, P], [1, D]]),
                pout[:, par * D:(par + 1) * D]),
                reads=["pout%d" % par], writes=["d_part%d" % s], sem="DP", amt=16)

    for e_ in (-1, 0, 1):
        vload(e_)
    for kind, e_ in (("ln", -1), ("vl", 2), ("ln", 0), ("vl", 3), ("ln", 1),
                     ("vl", 4), ("tv", -1), ("ln", 2), ("vl", 5), ("tv", 0),
                     ("ln", 3), ("vl", 6), ("tv", 1), ("ln", 4), ("vl", 7),
                     ("tv", 2), ("ln", 5), ("tv", 3)):
        {"ln": vpath_ln, "tv": vpath_tv, "vl": vload}[kind](e_)
    sch.full_barrier()
    sch.op(V, lambda e: e.memset(probs[:, :], 0.0), writes=["probs0", "probs1"])

    for s in range(NT + 7):
        if s + 8 <= NB - 2:
            vload(s + 8)
        if 0 <= s - 4 < NT:
            attn2(s - 4)
        if s + 4 <= NB - 2:
            vpath_tv(s + 4)
        if s + 6 <= NB - 2:
            vpath_ln(s + 6)
        if (s - 7) % 4 == 0 and 0 <= s - 7 < NT:
            wout_read((s - 7) // 4)
        if s % 4 == 0 and s < NT:
            qpath(s // 4)
        if 0 <= s - 2 < NT:
            attn1b2(s - 2)
        if s < NT:
            attn1a(s)
        if 0 <= s - 1 < NT:
            attn1b1(s - 1)
        if (s - 10) % 4 == 0 and 0 <= s - 10 < NT:
            wout_mm((s - 10) // 4)

    sch.full_barrier()

    for j in range(4):
        w = KC * HID // 4
        sch.op(SY, lambda e, j=j, w=w: e.dma_start(
            wq1sb[:, j * w:(j + 1) * w],
            bass.AP(wq1t_d, j * w, [[KC * HID, P], [1, w]])),
            reads=["d_wq1"], writes=["wq1sb_%d" % j], sem=p0s[j], amt=16)
        sch.op(SY, lambda e, j=j, w=w: e.dma_start(
            wq2sb[:, j * w:(j + 1) * w],
            bass.AP(wq2t_d, j * w, [[KH * D, P], [1, w]])),
            reads=["d_wq2"], writes=["wq2sb_%d" % j], sem=p0s[4 + j], amt=16)
    bi = 0
    for (row, rkey, dst, dkey, n) in ((gamrow, "gamrow", gamb, "gamb", D),
                                      (betrow, "betrow", betb, "betb", D),
                                      (b1row, "b1row", b1b, "b1b", HID),
                                      (b2row, "b2row", b2b, "b2b", D)):
        for c in range(0, n, 512):
            pst, kpst = WT_PS[bi % 2]
            bi += 1
            sch.op(T, lambda e, row=row, c=c, pst=pst: e.matmul(
                pst[:, 0:512], onesrow[0:1, :], row[0:1, c:c + 512],
                start=True, stop=True),
                reads=[rkey, "onesrow"], writes=[kpst])
            sch.op(S, lambda e, dst=dst, c=c, pst=pst: e.copy(dst[:, c:c + 512], pst[:, 0:512]),
                   reads=[kpst], writes=[dkey])

    def qload(s):
        par = s % 2
        tb = s + 1
        sch.op(G, lambda e, tb=tb, par=par: e.dma_start(
            xld3[:, par * D:(par + 1) * D], x_in[tb * P:(tb + 1) * P, :]),
            writes=["xld3%d" % par], sem="DX3", amt=16)

    def quantize1(s):
        par = s % 2
        tb = s + 1
        kx, kw, kq = "xld3%d" % par, "w3%d" % par, "xq1tm%d" % par
        sch.op(V, lambda e: e.tensor_scalar(
            w3[:, par * D:(par + 1) * D], xld3[:, par * D:(par + 1) * D],
            st_st[:, 2 * tb:2 * tb + 1], st_r1[:, tb:tb + 1], op0=A.subtract, op1=A.mult),
            reads=[kx, "st%d" % tb, "str%d" % tb], writes=[kw])
        sch.op(V, lambda e: e.tensor_scalar(
            w3[:, par * D:(par + 1) * D], w3[:, par * D:(par + 1) * D],
            st2_st[:, 2 * s:2 * s + 1], st_r2[:, s:s + 1], op0=A.subtract, op1=A.mult),
            reads=[kw, "st2_%d" % s, "st2r%d" % s], writes=[kw])
        sch.op(V, lambda e: e.tensor_mul(w3[:, par * D:(par + 1) * D],
                                         w3[:, par * D:(par + 1) * D], gamb[:, :]),
               reads=[kw, "gamb"], writes=[kw])
        sch.op(V, lambda e: e.tensor_add(w3[:, par * D:(par + 1) * D],
                                         w3[:, par * D:(par + 1) * D], betb[:, :]),
               reads=[kw, "betb"], writes=[kw])
        sch.op(V, lambda e: e.tensor_reduce(sc1[:, 0:1], w3[:, par * D:(par + 1) * D],
                                            AX.X, A.max, apply_absolute_value=True),
               reads=[kw], writes=["sc1a"])
        sch.op(V, lambda e: e.tensor_scalar(sc1[:, 1:2], sc1[:, 0:1], 1e-5, None, op0=A.max),
               reads=["sc1a"], writes=["sc1b"])
        sch.op(V, lambda e: e.reciprocal(sc1[:, 2:3], sc1[:, 1:2]),
               reads=["sc1b"], writes=["sc1c"])
        sch.op(V, lambda e: e.tensor_scalar(sc1[:, 3:4], sc1[:, 2:3], 127.0, None, op0=A.mult),
               reads=["sc1c"], writes=["sc1d"])
        sch.op(V, lambda e: e.tensor_scalar(sc1[:, 4:5], sc1[:, 3:4], swb[:, 2:3], None, op0=A.mult),
               reads=["sc1d", "sw1q"], writes=["sc1e"])
        sch.op(V, lambda e: e.reciprocal(inv1t[:, s:s + 1], sc1[:, 4:5]),
               reads=["sc1e"], writes=["inv1_%d" % s])
        sch.op(S, lambda e: e.activation(
            w3[:, par * D:(par + 1) * D], w3[:, par * D:(par + 1) * D], AF.Identity,
            bias=MAG, scale=sc1[:, 3:4]),
            reads=[kw, "sc1d", "magic"], writes=[kw])
        sch.op(S, lambda e: e.activation(
            xq1tm[:, par * D:(par + 1) * D], w3[:, par * D:(par + 1) * D], AF.Identity,
            bias=MAGN, scale=1.0),
            reads=[kw, "magicn"], writes=[kq])
        sch.op(SY, lambda e, s=s: e.dma_start(
            bass.AP(xq1_d, (s // 4 % 4) * 4 * P * D + (s % 4) * P * D, [[D, P], [1, D]]),
            xq1tm[:, par * D:(par + 1) * D]),
            reads=[kq], writes=["d_xq1_%d" % (s % 16)], sem="DQW%d" % (s % 2), amt=16)

    def xq1_read(quad):
        qpar = quad % 2
        for k in range(KC):
            sch.op(SY, lambda e, k=k, quad=quad, qpar=qpar: e.dma_start(
                xq1fm[:, qpar * 4096 + k * 512:qpar * 4096 + (k + 1) * 512],
                bass.AP(xq1_d, (quad % 4) * 4 * P * D + k * P, [[D, 512], [1, P]]),
                transpose=True),
                reads=["d_xq1_%d" % ((4 * quad + i_) % 16) for i_ in range(4)],
                writes=["xq1fm%d_%d" % (qpar, k)],
                sem="DQR%d" % (k % 4), amt=16)

    def ffn_f1(s):
        quad = s // 4
        qpar = quad % 2
        xq1keys = ["xq1fm%d_%d" % (qpar, k) for k in range(KC)]
        for c in range(8):
            pst, kpst = F1_PS[(s * 8 + c) % 5]

            def fmm(e, c=c, s=s, qpar=qpar, pst=pst):
                ins = None
                for k in range(KC):
                    ins = e.matmul(
                        pst[:, 0:512],
                        xq1fm[:, qpar * 4096 + k * 512 + (s % 4) * P:
                              qpar * 4096 + k * 512 + (s % 4 + 1) * P],
                        wq1sb[:, k * HID + c * 512:k * HID + (c + 1) * 512],
                        start=(k == 0), stop=(k == KC - 1))
                return ins
            sch.op(T, fmm, reads=xq1keys + ["wq1sb_%d" % j_ for j_ in range(4)], writes=[kpst])
            sch.op(V, lambda e, c=c, s=s, pst=pst, kpst=kpst: e.scalar_tensor_tensor(
                y1[:, c * 512:(c + 1) * 512], pst[:, 0:512], inv1t[:, s:s + 1],
                b1b[:, c * 512:(c + 1) * 512], op0=A.mult, op1=A.add),
                reads=[kpst, "inv1_%d" % s, "b1b"], writes=["y1h%d" % (c // 4)])

    def ffn_f1gelu(s):
        sch.op(S, lambda e: e.activation(y1[:, 0:2048], y1[:, 0:2048], AF.Gelu,
                                         bias=0.0, scale=1.0),
               reads=["y1h0"], writes=["y1h0"])
        sch.op(S, lambda e: e.activation(y1[:, 2048:HID], y1[:, 2048:HID], AF.Gelu,
                                         bias=0.0, scale=1.0),
               reads=["y1h1"], writes=["y1h1"])

    def ffn_f2head(s):
        par = s % 2
        sch.op(SY, lambda e, s=s, par=par: e.dma_start(
            ppart[:, par * D:(par + 1) * D],
            bass.AP(partial_d, s * P * D, [[D, P], [1, D]])),
            reads=["d_part%d" % s], writes=["ppart%d" % par], sem="DPR", amt=16)
        sch.op(V, lambda e: e.tensor_reduce(sc2[:, 6:7], y1[:, 0:2048], AX.X, A.max,
                                            apply_absolute_value=True),
               reads=["y1h0"], writes=["sc2m0"])
        sch.op(V, lambda e: e.tensor_reduce(sc2[:, 7:8], y1[:, 2048:HID], AX.X, A.max,
                                            apply_absolute_value=True),
               reads=["y1h1"], writes=["sc2m1"])
        sch.op(V, lambda e: e.tensor_scalar(sc2[:, 0:1], sc2[:, 6:7], sc2[:, 7:8],
                                            None, op0=A.max),
               reads=["sc2m0", "sc2m1"], writes=["sc2a"])
        sch.op(V, lambda e: e.tensor_scalar(sc2[:, 1:2], sc2[:, 0:1], 1e-5, None, op0=A.max),
               reads=["sc2a"], writes=["sc2b"])
        sch.op(V, lambda e: e.reciprocal(sc2[:, 2:3], sc2[:, 1:2]),
               reads=["sc2b"], writes=["sc2c"])
        sch.op(V, lambda e: e.tensor_scalar(sc2[:, 3:4], sc2[:, 2:3], 127.0, None, op0=A.mult),
               reads=["sc2c"], writes=["sc2d"])
        sch.op(V, lambda e: e.tensor_scalar(sc2[:, 4:5], sc2[:, 3:4], swb[:, 3:4], None, op0=A.mult),
               reads=["sc2d", "sw2q"], writes=["sc2e"])
        sch.op(V, lambda e: e.reciprocal(sc2[:, 5:6], sc2[:, 4:5]),
               reads=["sc2e"], writes=["sc2f"])
        for h, kh, kx in ((0, "y1h0", "xq2tm0"), (1, "y1h1", "xq2tm1")):
            lo, hi = h * 2048, (h + 1) * 2048
            sch.op(S, lambda e, lo=lo, hi=hi: e.activation(
                y1[:, lo:hi], y1[:, lo:hi], AF.Identity,
                bias=MAG, scale=sc2[:, 3:4]),
                reads=[kh, "sc2d", "magic"], writes=[kh])
            sch.op(S, lambda e, lo=lo, hi=hi: e.activation(
                xq2tm[:, lo:hi], y1[:, lo:hi], AF.Identity,
                bias=MAGN, scale=1.0),
                reads=[kh, "magicn"], writes=[kx])

    def ffn_f2tailA(s):
        for r0 in range(0, KH, 4):
            def trq(e, r0=r0):
                for i in range(4):
                    kb = r0 + i
                    ins = e.transpose(
                        ps_tx[:, i * P:(i + 1) * P],
                        xq2tm[:, kb * P:(kb + 1) * P], ident[:, :])
                return ins
            sch.op(T, trq, reads=["xq2tm%d" % (r0 // 16), "ident"],
                   writes=["pstx"])
            sch.op(S, lambda e, r0=r0: e.copy(
                xq2fm[:, r0 * P:(r0 + 4) * P], ps_tx[:, 0:512]),
                reads=["pstx"], writes=["xq2fm"])

    def ffn_f2tailB(s):
        par = s % 2
        for n in range(2):
            pst, kpst = BD_PS[n % 2]

            def f2mm(e, n=n, pst=pst):
                ins = None
                for k in range(KH):
                    ins = e.matmul(
                        pst[:, 0:512],
                        xq2fm[:, k * P:(k + 1) * P],
                        wq2sb[:, k * D + n * 512:k * D + (n + 1) * 512],
                        start=(k == 0), stop=(k == KH - 1))
                return ins
            sch.op(T, f2mm, reads=["xq2fm"] + ["wq2sb_%d" % j_ for j_ in range(4)],
                   writes=[kpst])
            sch.op(V, lambda e, n=n, pst=pst, kpst=kpst: e.scalar_tensor_tensor(
                t2s[:, n * 512:(n + 1) * 512], pst[:, 0:512], sc2[:, 5:6],
                b2b[:, n * 512:(n + 1) * 512], op0=A.mult, op1=A.add),
                reads=[kpst, "sc2f", "b2b"], writes=["t2s"])
        sch.op(V, lambda e, par=par: e.tensor_add(
            ppart[:, par * D:(par + 1) * D], ppart[:, par * D:(par + 1) * D],
            t2s[:, 0:D]),
            reads=["t2s", "ppart%d" % par], writes=["ppart%d" % par])
        sch.op(SY, lambda e, s=s, par=par: e.dma_start(
            y[s * P:(s + 1) * P, :], ppart[:, par * D:(par + 1) * D]),
            reads=["ppart%d" % par], writes=["ppart%d" % par], sem="DO", amt=16)

    qload(0)
    qload(1)
    for s in range(8):
        quantize1(s)
        qload(s + 2)
    xq1_read(0)
    ffn_f1(0)
    ffn_f1gelu(0)
    for s in range(NT):
        ffn_f2head(s)
        if s % 4 == 0 and (s + 4) // 4 < NT // 4:
            xq1_read((s + 4) // 4)
        if s + 1 < NT:
            ffn_f1(s + 1)
        ffn_f2tailA(s)
        if s + 1 < NT:
            ffn_f1gelu(s + 1)
        if 8 + s < NT:
            quantize1(8 + s)
        if 10 + s < NT:
            qload(10 + s)
        ffn_f2tailB(s)

    sch.wait_only(SY, [("DO", sch.semc["DO"])])

    with nc.Block() as block:
        @block.sync
        def _(e):
            for f in sch.thunks["sync"]:
                f(e)

        @block.scalar
        def _(e):
            for f in sch.thunks["scalar"]:
                f(e)

        @block.vector
        def _(e):
            for f in sch.thunks["vector"]:
                f(e)

        @block.tensor
        def _(e):
            for f in sch.thunks["tensor"]:
                f(e)

        @block.gpsimd
        def _(e):
            for f in sch.thunks["gpsimd"]:
                f(e)

    es.close()
    return nc


_CACHED = [None]


def kernel(**inputs):
    xq = np.ascontiguousarray(inputs["L0_attn_query"], dtype=np.float32)
    B, TT, DD = xq.shape
    assert (B, TT, DD) == (4, 8192, 1024)
    if _CACHED[0] is None:
        _CACHED[0] = build_program()
    nc = _CACHED[0]
    common = {
        "W_qv": np.ascontiguousarray(inputs["L0_attn_W_qv"], np.float32),
        "W2_0": np.ascontiguousarray(inputs["L0_attn_W2_0"], np.float32),
        "W2_1": np.ascontiguousarray(inputs["L0_attn_W2_1"], np.float32),
        "SCW": np.ascontiguousarray(inputs["L0_attn_scale_weights"], np.float32).reshape(1, 2),
        "W_out": np.ascontiguousarray(inputs["L0_attn_W_out"], np.float32),
        "w1": np.ascontiguousarray(inputs["L0_ffn_w1"], np.float32),
        "b1": np.ascontiguousarray(inputs["L0_ffn_b1"], np.float32).reshape(1, HID),
        "w2": np.ascontiguousarray(inputs["L0_ffn_w2"], np.float32),
        "b2": np.ascontiguousarray(inputs["L0_ffn_b2"], np.float32).reshape(1, D),
        "gam": np.ascontiguousarray(inputs["L0_ffn_gamma"], np.float32).reshape(1, D),
        "bet": np.ascontiguousarray(inputs["L0_ffn_bet" + "a"], np.float32).reshape(1, D),
    }
    in_maps = []
    for c in range(8):
        b, h = c // 2, c % 2
        t0 = h * 4096
        xe = np.zeros((TIN, D), np.float32)
        lo, hi = t0 - P, t0 + 4096 + P
        s_, e_ = max(lo, 0), min(hi, TT)
        xe[s_ - lo:s_ - lo + (e_ - s_)] = xq[b, s_:e_]
        m = dict(common)
        m["x_in"] = xe
        in_maps.append(m)
    res = run_bass_kernel_spmd(nc, in_maps, core_ids=list(range(8)))
    out = np.empty((B, TT, DD), np.float32)
    for c in range(8):
        b, h = c // 2, c % 2
        out[b, h * 4096:(h + 1) * 4096] = res.results[c]["y"]
    return out
